# version 02
# baseline (speedup 1.0000x reference)
import numpy as np
import ml_dtypes
import concourse.bass as bass
import concourse.mybir as mybir
from concourse.bass_utils import run_bass_kernel_spmd
from contextlib import ExitStack

F32 = mybir.dt.float32
BF16 = mybir.dt.bfloat16
I32 = mybir.dt.int32
AF = mybir.ActivationFunctionType
ALU = mybir.AluOpType
AX = mybir.AxisListType

T = 2048
E = 2048
NJC = 64
NT = 16
OWN = 8
NH = 32
JB = 4
R2048 = float(2 ** -11)
RMEAN = float(2 ** -24)

JSEQ = [
    [3, 2, 1, 0, 15, 14, 13, 12, 11, 10, 9, 8],
    [7, 6, 5, 4, 3, 2, 1, 0, 15, 14, 13, 12, 11, 10, 9, 8],
]
NU = len(JSEQ[0]) + len(JSEQ[1])


class K:
    def __init__(self):
        self.q = {"sync": [], "tensor": [], "vector": [], "scalar": [], "gpsimd": []}
        self.cnt = {}
        self.sems = {}

    def inc(self, s, k):
        self.cnt[s] = self.cnt.get(s, 0) + k
        return self.cnt[s]

    def c(self, s):
        return self.cnt.get(s, 0)

    def emit(self, eng, fn):
        self.q[eng].append(fn)

    def wait(self, eng, s, val=None):
        v = self.c(s) if val is None else val
        if v <= 0:
            return
        self.emit(eng, lambda e, h=s, v=v: e.wait_ge(self.sems[h], v))

    def dma(self, eng, out, in_, s):
        self.inc(s, 16)
        self.emit(eng, lambda e, o=out, i=in_, h=s: e.dma_start(out=o, in_=i).then_inc(self.sems[h], 16))

    def op(self, eng, meth, s=None, **kw):
        if s is not None:
            self.inc(s, 1)

        def f(e, m=meth, kw=kw, h=s):
            r = getattr(e, m)(**kw)
            if h is not None:
                r.then_inc(self.sems[h], 1)
        self.emit(eng, f)

    def mm(self, out, lhsT, rhs, start, stop, s=None):
        if s is not None:
            self.inc(s, 1)

        def f(e, o=out, l=lhsT, r=rhs, st=start, sp=stop, h=s):
            x = e.matmul(o, l, r, start=st, stop=sp)
            if h is not None:
                x.then_inc(self.sems[h], 1)
        self.emit("tensor", f)

    def tr(self, out, in_, idn, s=None):
        if s is not None:
            self.inc(s, 1)

        def f(e, o=out, i=in_, d=idn, h=s):
            x = e.transpose(o, i, d)
            if h is not None:
                x.then_inc(self.sems[h], 1)
        self.emit("tensor", f)

    def vnop(self):
        self.inc("sV", 1)
        self.emit("vector", lambda e: e.nop().then_inc(self.sems["sV"], 1))

    def barrier(self):
        sems = [s for s in self.cnt if self.cnt[s] > 0]
        for eng in self.q:
            for s in sems:
                self.wait(eng, s)


def build():
    nc = bass.Bass("TRN2", target_bir_lowering=False, debug=False)
    ap = lambda t: t.ap()
    x_in = ap(nc.dram_tensor("x_in", [T, E], F32, kind="ExternalInput"))
    wu = ap(nc.dram_tensor("wu", [NJC, 128, E], F32, kind="ExternalInput"))
    wd = ap(nc.dram_tensor("wd", [NJC, 128, E], F32, kind="ExternalInput"))
    wq_ = ap(nc.dram_tensor("wq_", [16, 128, E], F32, kind="ExternalInput"))
    wk_ = ap(nc.dram_tensor("wk_", [16, 128, E], F32, kind="ExternalInput"))
    wv_ = ap(nc.dram_tensor("wv_", [16, 128, E], F32, kind="ExternalInput"))
    wo_ = ap(nc.dram_tensor("wo_", [16, 128, E], F32, kind="ExternalInput"))
    nwr = ap(nc.dram_tensor("nwr", [128, E], F32, kind="ExternalInput"))
    gamv = ap(nc.dram_tensor("gamv", [1, E], F32, kind="ExternalInput"))
    con = ap(nc.dram_tensor("con", [128, 6], F32, kind="ExternalInput"))
    ones128 = ap(nc.dram_tensor("ones128", [128, 128], F32, kind="ExternalInput"))
    idn_f = ap(nc.dram_tensor("idn_f", [128, 128], F32, kind="ExternalInput"))
    idn_b = ap(nc.dram_tensor("idn_b", [128, 128], BF16, kind="ExternalInput"))
    tmat = ap(nc.dram_tensor("tmat", [128, 128], BF16, kind="ExternalInput"))
    masks = ap(nc.dram_tensor("masks", [NU, 128, 512], BF16, kind="ExternalInput"))
    y = ap(nc.dram_tensor("y", [OWN * 128, E], F32, kind="ExternalOutput"))
    sc = lambda n, sh, dt: ap(nc.dram_tensor(n, sh, dt, kind="Internal"))
    x1s = sc("x1s", [T, E], F32)
    xns = sc("xns", [T, E], F32)
    x2s = sc("x2s", [OWN * 128, E], F32)
    xn2s = sc("xn2s", [OWN * 128, E], F32)
    xq1 = sc("xq1", [T, E], BF16)
    xq2 = sc("xq2", [OWN * 128, E], BF16)
    wqsu = sc("wqsu", [NJC, 128, E], BF16)
    wqsd = sc("wqsd", [NJC, 128, E], BF16)
    xt1h = sc("xt1h", [16, 128, T], BF16)
    xt1l = sc("xt1l", [16, 128, T], BF16)
    qth = sc("qth", [E, OWN * 128], BF16)
    qtl = sc("qtl", [E, OWN * 128], BF16)
    kth = sc("kth", [E, T], BF16)
    ktl = sc("ktl", [E, T], BF16)
    vth = sc("vth", [E, T], BF16)
    vtl = sc("vtl", [E, T], BF16)
    oth = sc("oth", [E, OWN * 128], BF16)
    otl = sc("otl", [E, OWN * 128], BF16)

    k = K()
    with ExitStack() as ctx:
        sb = lambda n, sh, dt: ctx.enter_context(nc.sbuf_tensor(n, sh, dt))
        psb = lambda n, sh: ctx.enter_context(nc.psum_tensor(n, sh, F32))
        for s in ["sW", "sG", "sM", "sT", "sV", "sS", "sY"]:
            k.sems[s] = ctx.enter_context(nc.semaphore(s))

        A_bf = sb("A_bf", [128, 32768], BF16)
        A_f32 = sb("A_f32", [128, 16384], F32)
        A_wq = sb("A_wq", [128, 12288], BF16)
        A_x = sb("A_x", [128, 4096], F32)
        A_w = sb("A_w", [128, 2048], F32)
        nw = sb("nw", [128, E], F32)
        gprep = sb("gprep", [128, E], F32)
        xqi = sb("xqi", [128, 512], I32)
        conb = sb("conb", [128, 6], F32)
        o128 = sb("o128", [128, 128], F32)
        idf = sb("idf", [128, 128], F32)
        idb = sb("idb", [128, 128], BF16)
        tmb = sb("tmb", [128, 128], BF16)
        stat = sb("stat", [128, 24], F32)
        gcol = sb("gcol", [128, 16], F32)
        gcol2 = sb("gcol2", [128, 16], F32)
        scal = sb("scal", [1, 16], F32)
        srow = sb("srow", [1, 3072], F32)
        carry = srow[0:1, 0:512]
        mbuf = A_bf[:, 12288:13312]
        wsum = stat[:, 20:24]
        r128 = stat[:, 18:20]
        wsumc = lambda i: stat[:, 20 + i:21 + i]
        wsum2 = stat[:, 22:23]

        pA = psb("pA", [128, 512])
        pB = psb("pB", [128, 512])
        pC = psb("pC", [128, 512])
        pD = psb("pD", [128, 512])
        pE = psb("pE", [128, 512])
        pF = psb("pF", [128, 512])
        pG = psb("pG", [128, 512])
        pT = ctx.enter_context(nc.psum_tensor("pT", [128, 1024], BF16))

        ONES = conb[:, 0:1]
        EPS5 = conb[:, 1:2]
        EPS6 = conb[:, 2:3]

        k.dma("sync", conb[:], con, "sW")
        k.dma("sync", o128[:], ones128, "sW")
        k.dma("sync", idf[:], idn_f, "sW")
        k.dma("sync", idb[:], idn_b, "sW")
        k.dma("sync", tmb[:], tmat, "sW")
        k.dma("sync", nw[:], nwr, "sW")
        k.dma("sync", srow[0:1, 1024:1024 + E], gamv, "sW")
        k.emit("gpsimd", lambda e: e.memset(wsum, 0.0).then_inc(k.sems["sG"], 1))
        k.inc("sG", 1)
        wbuf = [A_w[:], A_f32[:, 0:2048]]
        red_cnt = []
        si = 0
        for wi, wsrc in ((0, wu), (1, wd)):
            for jc in range(NJC):
                if si >= 2:
                    k.wait("sync", "sV", red_cnt[si - 2])
                k.dma("sync", wbuf[si % 2], wsrc[jc], "sW")
                k.wait("vector", "sW")
                if si == 0:
                    k.wait("vector", "sG", 1)
                k.op("vector", "tensor_reduce", s="sV", out=wsum2, in_=wbuf[si % 2],
                     axis=AX.X, op=ALU.add, apply_absolute_value=True)
                k.wait("vector", "sV")
                k.op("vector", "tensor_tensor", s="sV", out=wsumc(wi),
                     in0=wsum2, in1=wsumc(wi), op=ALU.add)
                red_cnt.append(k.c("sV"))
                si += 1

        def emit_prep(src_x, dst_xn, dst_xq, gc, gc_i, stat0):
            xsl = A_x[:, 0:2048]
            nsl = A_x[:, 2048:4096]
            sq = A_f32[:, 2048:4096]
            k.wait("gpsimd", "sV")
            k.dma("gpsimd", xsl, src_x, "sG")
            k.wait("vector", "sG")
            k.wait("vector", "sS")
            k.op("vector", "tensor_reduce", s="sV", out=stat[:, stat0:stat0 + 1],
                 in_=xsl, axis=AX.X, op=ALU.add)
            k.wait("scalar", "sV")
            k.op("scalar", "activation", s="sS", out=stat[:, stat0 + 1:stat0 + 2],
                 in_=stat[:, stat0:stat0 + 1], func=AF.Identity, scale=R2048)
            k.wait("vector", "sS")
            k.op("vector", "tensor_scalar", out=xsl, in0=xsl,
                 scalar1=stat[:, stat0 + 1:stat0 + 2], scalar2=None, op0=ALU.subtract)
            k.op("vector", "tensor_tensor", out=sq, in0=xsl, in1=xsl, op=ALU.mult)
            k.op("vector", "tensor_reduce", s="sV", out=stat[:, stat0 + 2:stat0 + 3],
                 in_=sq, axis=AX.X, op=ALU.add)
            k.wait("scalar", "sV")
            k.op("scalar", "activation", s="sS", out=stat[:, stat0 + 3:stat0 + 4],
                 in_=stat[:, stat0 + 2:stat0 + 3], func=AF.Sqrt, scale=R2048, bias=EPS5)
            k.wait("vector", "sS")
            k.op("vector", "reciprocal", s="sV", out=stat[:, stat0 + 4:stat0 + 5],
                 in_=stat[:, stat0 + 3:stat0 + 4])
            k.wait("vector", "sV")
            k.op("vector", "tensor_scalar", s="sV", out=nsl, in0=xsl,
                 scalar1=stat[:, stat0 + 4:stat0 + 5], scalar2=None, op0=ALU.mult)
            k.wait("gpsimd", "sV")
            k.dma("gpsimd", dst_xn, nsl, "sG")
            k.op("vector", "tensor_tensor", out=sq, in0=nsl, in1=nsl, op=ALU.mult)
            k.op("vector", "tensor_reduce", s="sV", out=stat[:, stat0 + 5:stat0 + 6],
                 in_=sq, axis=AX.X, op=ALU.add)
            k.wait("scalar", "sV")
            k.op("scalar", "activation", s="sS", out=stat[:, stat0 + 6:stat0 + 7],
                 in_=stat[:, stat0 + 5:stat0 + 6], func=AF.Sqrt, scale=R2048, bias=EPS6)
            k.wait("vector", "sS")
            k.op("vector", "reciprocal", s="sV", out=stat[:, stat0 + 7:stat0 + 8],
                 in_=stat[:, stat0 + 6:stat0 + 7])
            k.wait("vector", "sV")
            k.op("vector", "tensor_scalar", out=xsl, in0=nsl,
                 scalar1=stat[:, stat0 + 7:stat0 + 8], scalar2=None, op0=ALU.mult)
            k.op("vector", "tensor_tensor", out=xsl, in0=xsl, in1=nw[:], op=ALU.mult)
            k.op("vector", "tensor_reduce", s="sV", out=stat[:, stat0 + 11:stat0 + 12],
                 in_=xsl, axis=AX.X, op=ALU.max, apply_absolute_value=True)
            k.wait("vector", "sV")
            k.op("vector", "tensor_tensor", s="sV", out=gc[:, gc_i:gc_i + 1],
                 in0=stat[:, stat0 + 11:stat0 + 12], in1=conb[:, 4:5], op=ALU.max)
            k.wait("vector", "sV")
            k.op("vector", "reciprocal", s="sV", out=stat[:, stat0 + 9:stat0 + 10],
                 in_=gc[:, gc_i:gc_i + 1])
            k.wait("scalar", "sV")
            k.op("scalar", "activation", s="sS", out=stat[:, stat0 + 10:stat0 + 11],
                 in_=stat[:, stat0 + 9:stat0 + 10], func=AF.Identity, scale=127.0)
            k.wait("vector", "sS")
            k.op("vector", "tensor_scalar", out=xsl, in0=xsl,
                 scalar1=stat[:, stat0 + 10:stat0 + 11], scalar2=None, op0=ALU.mult)
            stg = A_wq[:, 4096:6144]
            for hcol in range(4):
                cs = slice(hcol * 512, hcol * 512 + 512)
                k.op("vector", "tensor_copy", s="sV", out=xqi[:], in_=xsl[:, cs])
                k.wait("vector", "sV")
                k.op("vector", "tensor_copy", s="sV", out=stg[:, cs], in_=xqi[:])
            k.wait("gpsimd", "sV")
            k.dma("gpsimd", dst_xq, stg, "sG")

        k.wait("scalar", "sW", 7 * 16)
        for tile in range(NT):
            r = slice(tile * 128, tile * 128 + 128)
            emit_prep(x_in[r, :], xns[r, :], xq1[r, :], gcol, tile, 0)

        k.barrier()
        for wi in range(2):
            k.mm(pG[0:1, 0:1], wsumc(wi), ONES, True, True, s="sT")
            k.wait("scalar", "sT")
            k.op("scalar", "activation", s="sS", out=scal[0:1, wi:wi + 1],
                 in_=pG[0:1, 0:1], func=AF.Identity, scale=RMEAN)
            k.wait("vector", "sS")
            k.op("vector", "tensor_tensor", s="sV", out=scal[0:1, 8 + wi:9 + wi],
                 in0=scal[0:1, wi:wi + 1], in1=conb[0:1, 4:5], op=ALU.max)
            k.wait("vector", "sV")
            k.op("vector", "reciprocal", s="sV", out=scal[0:1, 2 + wi:3 + wi],
                 in_=scal[0:1, 8 + wi:9 + wi])
            k.wait("tensor", "sV")
            k.wait("vector", "sV")
            k.mm(pG[:, 0:1], o128[0:1, :], scal[0:1, 2 + wi:3 + wi], True, True, s="sT")
            k.wait("vector", "sT")
            k.op("vector", "tensor_copy", s="sV", out=stat[:, 18 + wi:19 + wi], in_=pG[:, 0:1])
            k.wait("tensor", "sV")
        k.op("vector", "reciprocal", s="sV", out=scal[0:1, 4:5], in_=conb[0:1, 3:4])
        k.wait("vector", "sV")
        k.op("vector", "tensor_tensor", s="sV", out=scal[0:1, 5:6],
             in0=scal[0:1, 8:9], in1=scal[0:1, 4:5], op=ALU.mult)
        k.wait("scalar", "sV")
        k.op("scalar", "activation", s="sS", out=scal[0:1, 6:7], in_=scal[0:1, 9:10],
             func=AF.Identity, scale=0.5)
        k.wait("vector", "sS")
        k.op("vector", "tensor_scalar", s="sV", out=srow[0:1, 1024:1024 + E],
             in0=srow[0:1, 1024:1024 + E], scalar1=scal[0:1, 6:7], scalar2=None, op0=ALU.mult)
        k.wait("tensor", "sV")
        for g4 in range(4):
            k.mm(pG[:, 0:512], o128[0:1, :], srow[0:1, 1024 + g4 * 512:1536 + g4 * 512],
                 True, True, s="sT")
            k.wait("vector", "sT")
            k.op("vector", "tensor_copy", s="sV", out=gprep[:, g4 * 512:g4 * 512 + 512],
                 in_=pG[:, 0:512])
            k.wait("tensor", "sV")
        k.barrier()

        def emit_ffn(ts0, xq_src, x_src, xn_src, gc, first_w, out_kind):
            n_tiles = 8
            ntok = 1024
            ngrp = 2
            xqT = lambda ec: A_bf[:, ec * 1024: ec * 1024 + ntok]
            hhi = lambda j4: A_bf[:, 16384 + j4 * 1024: 16384 + j4 * 1024 + ntok]
            hlo = lambda j4: A_bf[:, 20480 + j4 * 1024: 20480 + j4 * 1024 + ntok]
            acc = lambda sub: A_f32[:, sub * 2048: sub * 2048 + 2048]
            srep = A_x[:, 0:1024]
            hs = A_x[:, 1024:1536]
            hsl = A_x[:, 1536:2048]
            wqu = lambda par: A_wq[:, 8192 + par * 2048: 8192 + par * 2048 + 2048]
            wqd = lambda j4: A_wq[:, j4 * 2048: j4 * 2048 + 2048]
            stg = A_bf[:, 24576:24576 + 2048]

            for i in range(n_tiles):
                r = slice((ts0 + i) * 128, (ts0 + i) * 128 + 128)
                k.wait("gpsimd", "sT")
                k.dma("gpsimd", stg, xq_src[r, :], "sG")
                k.wait("tensor", "sG")
                for ecb in range(0, 16, 4):
                    for ec in range(ecb, ecb + 4):
                        k.tr(pT[:, (ec % 4) * 128:(ec % 4) * 128 + 128],
                             stg[:, ec * 128:ec * 128 + 128], idb[:],
                             s=("sT" if ec == ecb + 3 else None))
                    k.wait("vector", "sT")
                    for ec in range(ecb, ecb + 4):
                        k.op("vector", "tensor_copy", s=("sV" if ec == ecb + 3 else None),
                             out=xqT(ec)[:, i * 128:i * 128 + 128],
                             in_=pT[:, (ec % 4) * 128:(ec % 4) * 128 + 128])
                    k.wait("tensor", "sV")
            for i in range(n_tiles):
                k.mm(pG[0:1, (i % 4) * 128:(i % 4) * 128 + 128],
                     gc[:, ts0 + i:ts0 + i + 1], idf[:], True, True,
                     s=("sT" if i % 4 == 3 else None))
                if i % 4 == 3:
                    k.wait("vector", "sT")
                    base = (i // 4) * 512
                    k.op("vector", "tensor_scalar", s="sV", out=srow[0:1, base:base + 512],
                         in0=pG[0:1, 0:512], scalar1=scal[0:1, 5:6], scalar2=None,
                         op0=ALU.mult)
                    k.wait("tensor", "sV")
            for g in range(ngrp):
                k.mm(pG[:, 0:512], o128[0:1, :], srow[0:1, g * 512:g * 512 + 512],
                     True, True, s="sT")
                k.wait("vector", "sT")
                k.op("vector", "tensor_copy", s="sV", out=srep[:, g * 512:g * 512 + 512],
                     in_=pG[:, 0:512])
                k.wait("tensor", "sV")

            split_done = {-1: k.c("sV")}
            dn_evac = {0: 0, 1: 0}
            for j4base in range(0, NJC, JB):
                for jj in range(JB):
                    jc = j4base + jj
                    par = jc % 2
                    if first_w:
                        k.wait("sync", "sV", split_done[jc - 1])
                        k.dma("sync", A_w[:], wu[jc], "sW")
                        k.wait("vector", "sW")
                        k.wait("vector", "sG")
                        k.op("vector", "tensor_scalar", out=A_w[:], in0=A_w[:],
                             scalar1=stat[:, 18:19], scalar2=None, op0=ALU.mult)
                        k.op("vector", "tensor_scalar", out=A_w[:], in0=A_w[:],
                             scalar1=1.0, scalar2=-1.0, op0=ALU.min, op1=ALU.max)
                        for hcol in range(4):
                            cs = slice(hcol * 512, hcol * 512 + 512)
                            k.op("vector", "tensor_copy", s="sV", out=xqi[:], in_=A_w[:, cs])
                            k.wait("vector", "sV")
                            k.op("vector", "tensor_copy", s="sV", out=wqu(par)[:, cs], in_=xqi[:])
                        k.wait("gpsimd", "sV")
                        k.dma("gpsimd", wqsu[jc], wqu(par), "sG")
                        k.wait("sync", "sV")
                        k.dma("sync", A_w[:], wd[jc], "sW")
                        k.wait("vector", "sW")
                        k.wait("vector", "sG")
                        k.op("vector", "tensor_scalar", out=A_w[:], in0=A_w[:],
                             scalar1=stat[:, 19:20], scalar2=None, op0=ALU.mult)
                        k.op("vector", "tensor_scalar", out=A_w[:], in0=A_w[:],
                             scalar1=1.0, scalar2=-1.0, op0=ALU.min, op1=ALU.max)
                        for hcol in range(4):
                            cs = slice(hcol * 512, hcol * 512 + 512)
                            k.op("vector", "tensor_copy", s="sV", out=xqi[:], in_=A_w[:, cs])
                            k.wait("vector", "sV")
                            k.op("vector", "tensor_copy", s="sV", out=wqd(jj)[:, cs], in_=xqi[:])
                        k.wait("gpsimd", "sV")
                        k.dma("gpsimd", wqsd[jc], wqd(jj), "sG")
                        wq_w = None
                    else:
                        k.dma("sync", wqu(par)[:], wqsu[jc], "sW")
                        k.dma("sync", wqd(jj)[:], wqsd[jc], "sW")
                        wq_w = k.c("sW")
                    for g in range(ngrp):
                        pU = pA if (jc * ngrp + g) % 2 == 0 else pB
                        if wq_w is not None:
                            k.wait("tensor", "sW", wq_w)
                        k.wait("tensor", "sV")
                        for ec in range(16):
                            k.mm(pU[:, 0:512], wqu(par)[:, ec * 128:ec * 128 + 128],
                                 xqT(ec)[:, g * 512:g * 512 + 512],
                                 ec == 0, ec == 15, s=("sT" if ec == 15 else None))
                        k.wait("vector", "sT")
                        k.op("vector", "tensor_tensor", s="sV", out=hs, in0=pU[:, 0:512],
                             in1=srep[:, g * 512:g * 512 + 512], op=ALU.mult)
                        k.wait("scalar", "sV")
                        k.op("scalar", "activation", s="sS", out=hsl, in_=hs, func=AF.Silu)
                        k.wait("vector", "sS")
                        k.op("vector", "tensor_copy",
                             out=hhi(jj)[:, g * 512:g * 512 + 512], in_=hsl)
                        k.op("vector", "scalar_tensor_tensor", s="sV",
                             out=hlo(jj)[:, g * 512:g * 512 + 512], in0=hsl, scalar=1.0,
                             in1=hhi(jj)[:, g * 512:g * 512 + 512],
                             op0=ALU.mult, op1=ALU.subtract)
                    split_done[jc] = k.c("sV")
                for sub in range(8):
                    for dg in range(4):
                        par = (sub * 4 + dg) % 2
                        pDn = pC if par == 0 else pD
                        k.wait("tensor", "sV", max(split_done[j4base + JB - 1], dn_evac[par]))
                        for jj in range(JB):
                            for hl in range(2):
                                hbuf = hhi if hl == 0 else hlo
                                k.mm(pDn[:, 0:512],
                                     hbuf(jj)[:, sub * 128:sub * 128 + 128],
                                     wqd(jj)[:, dg * 512:dg * 512 + 512],
                                     jj == 0 and hl == 0, jj == JB - 1 and hl == 1,
                                     s=("sT" if (jj == JB - 1 and hl == 1) else None))
                        k.wait("vector", "sT")
                        if j4base == 0:
                            k.op("vector", "tensor_copy", s="sV",
                                 out=acc(sub)[:, dg * 512:dg * 512 + 512], in_=pDn[:, 0:512])
                        else:
                            k.op("vector", "tensor_tensor", s="sV",
                                 out=acc(sub)[:, dg * 512:dg * 512 + 512],
                                 in0=pDn[:, 0:512],
                                 in1=acc(sub)[:, dg * 512:dg * 512 + 512], op=ALU.add)
                        dn_evac[par] = k.c("sV")

            k.barrier()
            for i in range(n_tiles):
                tile = ts0 + i
                r = slice(tile * 128, tile * 128 + 128)
                xsl = A_x[:, 0:2048]
                nsl = A_x[:, 2048:4096]
                if out_kind == "y":
                    k.wait("gpsimd", "sY")
                k.wait("gpsimd", "sV")
                k.wait("gpsimd", "sT")
                k.dma("gpsimd", xsl, x_src[r, :], "sG")
                k.dma("gpsimd", nsl, xn_src[r, :], "sG")
                k.wait("vector", "sG")
                k.op("vector", "tensor_tensor", out=acc(i), in0=acc(i), in1=gprep[:],
                     op=ALU.mult)
                k.op("vector", "scalar_tensor_tensor", out=nsl, in0=nsl, scalar=0.5,
                     in1=xsl, op0=ALU.mult, op1=ALU.add)
                k.op("vector", "tensor_tensor", s="sV", out=xsl, in0=acc(i), in1=nsl,
                     op=ALU.add)
                if out_kind == "y":
                    k.wait("sync", "sV")
                    k.dma("sync", y[i * 128:i * 128 + 128, :], xsl, "sY")
                    continue
                k.wait("gpsimd", "sV")
                k.dma("gpsimd", x1s[r, :], xsl, "sG")
                k.wait("vector", "sG")
                sq = acc(i)
                k.op("vector", "tensor_reduce", s="sV", out=stat[:, 12:13], in_=xsl,
                     axis=AX.X, op=ALU.add)
                k.wait("scalar", "sV")
                k.op("scalar", "activation", s="sS", out=stat[:, 13:14],
                     in_=stat[:, 12:13], func=AF.Identity, scale=R2048)
                k.wait("vector", "sS")
                k.op("vector", "tensor_scalar", out=xsl, in0=xsl,
                     scalar1=stat[:, 13:14], scalar2=None, op0=ALU.subtract)
                k.op("vector", "tensor_tensor", out=sq, in0=xsl, in1=xsl, op=ALU.mult)
                k.op("vector", "tensor_reduce", s="sV", out=stat[:, 14:15], in_=sq,
                     axis=AX.X, op=ALU.add)
                k.wait("scalar", "sV")
                k.op("scalar", "activation", s="sS", out=stat[:, 15:16],
                     in_=stat[:, 14:15], func=AF.Sqrt, scale=R2048, bias=EPS5)
                k.wait("vector", "sS")
                k.op("vector", "reciprocal", s="sV", out=stat[:, 16:17],
                     in_=stat[:, 15:16])
                k.wait("vector", "sV")
                k.op("vector", "tensor_scalar", s="sV", out=xsl, in0=xsl,
                     scalar1=stat[:, 16:17], scalar2=None, op0=ALU.mult)
                k.wait("tensor", "sV")
                sth = A_bf[:, 24576:24576 + 2048]
                stl = A_bf[:, 26624:26624 + 2048]
                for ecb in range(0, 16, 4):
                    for ec in range(ecb, ecb + 4):
                        k.tr(pA[:, (ec % 4) * 128:(ec % 4) * 128 + 128],
                             xsl[:, ec * 128:ec * 128 + 128], idf[:],
                             s=("sT" if ec == ecb + 3 else None))
                    k.wait("vector", "sT")
                    for ec in range(ecb, ecb + 4):
                        psl = pA[:, (ec % 4) * 128:(ec % 4) * 128 + 128]
                        k.op("vector", "tensor_copy",
                             out=sth[:, ec * 128:ec * 128 + 128], in_=psl)
                        k.op("vector", "scalar_tensor_tensor",
                             s=("sV" if ec == ecb + 3 else None),
                             out=stl[:, ec * 128:ec * 128 + 128], in0=psl, scalar=1.0,
                             in1=sth[:, ec * 128:ec * 128 + 128],
                             op0=ALU.mult, op1=ALU.subtract)
                    k.wait("tensor", "sV")
                k.wait("gpsimd", "sV")
                for ec in range(16):
                    k.dma("gpsimd", xt1h[ec, :, tile * 128:tile * 128 + 128],
                          sth[:, ec * 128:ec * 128 + 128], "sG")
                    k.dma("gpsimd", xt1l[ec, :, tile * 128:tile * 128 + 128],
                          stl[:, ec * 128:ec * 128 + 128], "sG")
            k.barrier()

        emit_ffn(0, xq1, x_in, xns, gcol, True, "x1")
        emit_ffn(8, xq1, x_in, xns, gcol, False, "x1")

        def emit_proj(mats, half):
            hof = half * 1024
            xh = lambda ec: A_bf[:, ec * 1024: ec * 1024 + 1024]
            xl = lambda ec: A_bf[:, 16384 + ec * 1024: 16384 + ec * 1024 + 1024]
            for ec in range(16):
                k.dma("sync", xh(ec), xt1h[ec, :, hof:hof + 1024], "sW")
                k.dma("sync", xl(ec), xt1l[ec, :, hof:hof + 1024], "sW")
            sth = A_wq[:, 8192:8192 + 512]
            stl = A_wq[:, 8704:8704 + 512]
            for (wsrc, dsth, dstl, ownrel) in mats:
                for dc in range(16):
                    par = dc % 2
                    whi = A_wq[:, par * 4096:par * 4096 + 2048]
                    wlo = A_wq[:, par * 4096 + 2048:par * 4096 + 4096]
                    k.wait("sync", "sT")
                    k.dma("sync", A_w[:], wsrc[dc], "sW")
                    k.wait("vector", "sW")
                    k.op("vector", "tensor_copy", out=whi[:], in_=A_w[:])
                    k.op("vector", "scalar_tensor_tensor", s="sV", out=wlo[:], in0=A_w[:],
                         scalar=1.0, in1=whi[:], op0=ALU.mult, op1=ALU.subtract)
                    for g in range(2):
                        pP = pA if g == 0 else pB
                        k.wait("tensor", "sW")
                        k.wait("tensor", "sV")
                        for ec in range(16):
                            xhs = xh(ec)[:, g * 512:g * 512 + 512]
                            xls = xl(ec)[:, g * 512:g * 512 + 512]
                            wh = whi[:, ec * 128:ec * 128 + 128]
                            wl = wlo[:, ec * 128:ec * 128 + 128]
                            k.mm(pP[:, 0:512], wh, xhs, ec == 0, False)
                            k.mm(pP[:, 0:512], wl, xhs, False, False)
                            k.mm(pP[:, 0:512], wh, xls, False, ec == 15,
                                 s=("sT" if ec == 15 else None))
                        k.wait("vector", "sT")
                        k.wait("vector", "sG")
                        qscale = 0.125 if ownrel else 1.0
                        k.op("vector", "tensor_scalar", out=sth[:], in0=pP[:, 0:512],
                             scalar1=qscale, scalar2=None, op0=ALU.mult)
                        k.op("vector", "scalar_tensor_tensor", s="sV", out=stl[:],
                             in0=pP[:, 0:512], scalar=qscale, in1=sth[:],
                             op0=ALU.mult, op1=ALU.subtract)
                        k.wait("gpsimd", "sV")
                        rr = slice(dc * 128, dc * 128 + 128)
                        if ownrel:
                            cc = slice(g * 512, g * 512 + 512)
                        else:
                            cc = slice(hof + g * 512, hof + g * 512 + 512)
                        k.dma("gpsimd", dsth[rr, cc], sth[:], "sG")
                        k.dma("gpsimd", dstl[rr, cc], stl[:], "sG")
            k.barrier()

        emit_proj([(wk_, kth, ktl, False), (wv_, vth, vtl, False),
                   (wq_, qth, qtl, True)], 0)
        emit_proj([(wk_, kth, ktl, False), (wv_, vth, vtl, False)], 1)

        kb_h = A_bf[0:64, 0:2048]
        kb_l = A_bf[0:64, 2048:4096]
        vtb = A_bf[:, 4096:6144]
        vb_h = A_bf[:, 6144:7168]
        vb_l = A_bf[:, 7168:8192]
        qb_h = A_bf[0:64, 8192:8704]
        qb_l = A_bf[0:64, 8704:9216]
        p_hi = A_bf[:, 9216:9728]
        p_lo = A_bf[:, 9728:10240]
        e_hi = A_bf[:, 10240:10752]
        e_lo = A_bf[:, 10752:11264]
        osth = A_bf[0:64, 11264:11776]
        ostl = A_bf[0:64, 11776:12288]
        t_f = A_f32[:, 0:512]
        p_f = A_f32[:, 512:1024]
        eS_f = A_f32[:, 1024:1536]
        e_f = A_f32[:, 1536:2048]

        for h in range(NH):
            hr = slice(h * 64, h * 64 + 64)
            k.wait("sync", "sT")
            k.wait("sync", "sV")
            k.dma("sync", kb_h, kth[hr, :], "sW")
            k.dma("sync", kb_l, ktl[hr, :], "sW")
            k.dma("sync", vtb[0:64, :], vth[hr, :], "sW")
            k.dma("sync", vtb[64:128, :], vtl[hr, :], "sW")
            k.wait("tensor", "sW")
            for jtb in range(0, 16, 4):
                for jt in range(jtb, jtb + 4):
                    k.tr(pT[:, (jt % 4) * 128:(jt % 4) * 128 + 128],
                         vtb[:, jt * 128:jt * 128 + 128], idb[:],
                         s=("sT" if jt == jtb + 3 else None))
                k.wait("vector", "sT")
                for jt in range(jtb, jtb + 4):
                    psl = pT[:, (jt % 4) * 128:(jt % 4) * 128 + 128]
                    k.op("vector", "tensor_copy", out=vb_h[:, jt * 64:jt * 64 + 64],
                         in_=psl[:, 0:64])
                    k.op("vector", "tensor_copy", s=("sV" if jt == jtb + 3 else None),
                         out=vb_l[:, jt * 64:jt * 64 + 64], in_=psl[:, 64:128])
                k.wait("tensor", "sV")
            for s2 in range(2):
                seq = JSEQ[s2]
                nu = len(seq)
                pO = pE if s2 == 0 else pF
                k.wait("sync", "sT")
                k.dma("sync", qb_h, qth[hr, s2 * 512:s2 * 512 + 512], "sW")
                k.dma("sync", qb_l, qtl[hr, s2 * 512:s2 * 512 + 512], "sW")
                k.inc("sS", 1)
                k.emit("scalar", lambda e: e.memzero(carry).then_inc(k.sems["sS"], 1))
                qld = k.c("sW")
                base_ts = k.c("sT")
                base_ss = k.c("sS")
                base_vs = k.c("sV")
                base_sm = k.c("sM")
                mbase = 0 if s2 == 0 else len(JSEQ[0])
                for pf in range(2):
                    k.dma("scalar", mbuf[:, (pf % 2) * 512:(pf % 2) * 512 + 512],
                          masks[mbase + pf], "sM")
                for ui, sl in enumerate(seq):
                    mpar = ui % 2
                    ks = slice(sl * 128, sl * 128 + 128)
                    vs_ = slice(sl * 64, sl * 64 + 64)
                    if ui == 0:
                        k.wait("tensor", "sW", qld)
                        k.wait("tensor", "sS", base_ss)
                    else:
                        k.wait("tensor", "sS", base_ss + 4 * (ui - 1) + 1)
                    k.mm(pA[:, 0:512], kb_h[:, ks], qb_h[:], True, False)
                    k.mm(pA[:, 0:512], kb_l[:, ks], qb_h[:], False, False)
                    k.mm(pA[:, 0:512], kb_h[:, ks], qb_l[:], False, True, s="sT")
                    k.wait("tensor", "sV", base_vs + 3 * ui + 2)
                    k.wait("tensor", "sS", base_ss + 4 * ui if ui > 0 else base_ss)
                    k.mm(pB[:, 0:512], tmb[:], p_hi[:], True, False)
                    k.mm(pB[:, 0:512], tmb[:], p_lo[:], False, False)
                    k.mm(pB[:, 0:512], o128[0:1, :], carry, False, True, s="sT")
                    k.wait("tensor", "sV", base_vs + 3 * ui + 3)
                    k.mm(pO[0:64, 0:512], vb_h[:, vs_], e_hi[:], ui == 0, False)
                    k.mm(pO[0:64, 0:512], vb_h[:, vs_], e_lo[:], False, False)
                    k.mm(pO[0:64, 0:512], vb_l[:, vs_], e_hi[:], False, ui == nu - 1, s="sT")
                    k.wait("scalar", "sT", base_ts + 3 * ui + 1)
                    k.wait("scalar", "sV", base_vs + 3 * ui if ui > 0 else base_vs)
                    k.op("scalar", "activation", s="sS", out=t_f, in_=pA[:, 0:512],
                         func=AF.Exp)
                    k.wait("scalar", "sV", base_vs + 3 * ui + 1)
                    k.op("scalar", "activation", s="sS", out=p_f, in_=t_f, func=AF.Ln,
                         bias=ONES)
                    if ui + 2 < nu:
                        k.dma("scalar", mbuf[:, ((ui + 2) % 2) * 512:((ui + 2) % 2) * 512 + 512],
                              masks[mbase + ui + 2], "sM")
                    k.wait("scalar", "sT", base_ts + 3 * ui + 2)
                    k.op("scalar", "copy", s="sS", out=carry, in_=pB[0:1, 0:512])
                    k.op("scalar", "activation", s="sS", out=eS_f, in_=pB[:, 0:512],
                         func=AF.Exp)
                    k.wait("vector", "sS", base_ss + 4 * ui + 1)
                    k.wait("vector", "sM", base_sm + 16 * (ui + 1))
                    k.op("vector", "tensor_tensor", s="sV", out=t_f, in0=t_f,
                         in1=mbuf[:, mpar * 512:mpar * 512 + 512], op=ALU.mult)
                    k.wait("vector", "sS", base_ss + 4 * ui + 2)
                    k.op("vector", "tensor_copy", out=p_hi[:], in_=p_f)
                    k.op("vector", "scalar_tensor_tensor", s="sV", out=p_lo[:], in0=p_f,
                         scalar=1.0, in1=p_hi[:], op0=ALU.mult, op1=ALU.subtract)
                    k.wait("vector", "sS", base_ss + 4 * ui + 4)
                    if ui > 0:
                        k.wait("vector", "sT", base_ts + 3 * ui)
                    k.op("vector", "tensor_tensor", out=e_f, in0=t_f, in1=eS_f, op=ALU.mult)
                    k.op("vector", "tensor_copy", out=e_hi[:], in_=e_f)
                    k.op("vector", "scalar_tensor_tensor", s="sV", out=e_lo[:], in0=e_f,
                         scalar=1.0, in1=e_hi[:], op0=ALU.mult, op1=ALU.subtract)
                k.wait("vector", "sT")
                k.wait("vector", "sG")
                k.op("vector", "tensor_copy", out=osth[:], in_=pO[0:64, 0:512])
                k.op("vector", "scalar_tensor_tensor", s="sV", out=ostl[:],
                     in0=pO[0:64, 0:512], scalar=1.0, in1=osth[:],
                     op0=ALU.mult, op1=ALU.subtract)
                k.wait("gpsimd", "sV")
                k.dma("gpsimd", oth[hr, s2 * 512:s2 * 512 + 512], osth[:], "sG")
                k.dma("gpsimd", otl[hr, s2 * 512:s2 * 512 + 512], ostl[:], "sG")
        k.barrier()

        oh = lambda hc: A_bf[:, hc * 1024: hc * 1024 + 1024]
        ol = lambda hc: A_bf[:, 16384 + hc * 1024: 16384 + hc * 1024 + 1024]
        for hc in range(16):
            rr = slice(hc * 128, hc * 128 + 128)
            k.dma("sync", oh(hc), oth[rr, :], "sW")
            k.dma("sync", ol(hc), otl[rr, :], "sW")
        whh = A_wq[:, 0:2048]
        wll = A_wq[:, 2048:4096]
        evb = A_x[:, 0:512]
        for g in range(2):
            for i_l in range(4):
                tile = g * 4 + i_l
                k.wait("gpsimd", "sV")
                k.wait("gpsimd", "sS")
                k.dma("gpsimd", A_f32[:, i_l * 2048:i_l * 2048 + 2048],
                      x1s[tile * 128:tile * 128 + 128, :], "sG")
            for ec in range(16):
                k.wait("sync", "sT")
                k.dma("sync", A_w[:], wo_[ec], "sW")
                k.wait("vector", "sW")
                k.wait("vector", "sT")
                k.op("vector", "tensor_copy", out=whh[:], in_=A_w[:])
                k.op("vector", "scalar_tensor_tensor", s="sV", out=wll[:], in0=A_w[:],
                     scalar=1.0, in1=whh[:], op0=ALU.mult, op1=ALU.subtract)
                k.wait("tensor", "sV")
                k.wait("tensor", "sW")
                for hc in range(16):
                    wh = whh[:, hc * 128:hc * 128 + 128]
                    wl = wll[:, hc * 128:hc * 128 + 128]
                    ohs = oh(hc)[:, g * 512:g * 512 + 512]
                    ols = ol(hc)[:, g * 512:g * 512 + 512]
                    k.mm(pG[:, 0:512], wh, ohs, hc == 0, False)
                    k.mm(pG[:, 0:512], wl, ohs, False, False)
                    k.mm(pG[:, 0:512], wh, ols, False, hc == 15,
                         s=("sT" if hc == 15 else None))
                k.wait("vector", "sT")
                k.op("vector", "tensor_copy", s="sV", out=evb[:], in_=pG[:, 0:512])
                k.wait("tensor", "sV")
                k.wait("tensor", "sG")
                for i_l in range(4):
                    k.tr(pA[:, i_l * 128:i_l * 128 + 128],
                         evb[:, i_l * 128:i_l * 128 + 128], idf[:],
                         s=("sT" if i_l == 3 else None))
                k.wait("vector", "sT")
                for i_l in range(4):
                    dst = A_f32[:, i_l * 2048 + ec * 128:i_l * 2048 + ec * 128 + 128]
                    k.op("vector", "tensor_tensor", s=("sV" if i_l == 3 else None),
                         out=dst, in0=pA[:, i_l * 128:i_l * 128 + 128], in1=dst,
                         op=ALU.add)
                k.wait("tensor", "sV")
            for i_l in range(4):
                tile = g * 4 + i_l
                r = slice(tile * 128, tile * 128 + 128)
                xsl = A_f32[:, i_l * 2048:i_l * 2048 + 2048]
                nsl = A_x[:, 2048:4096]
                sq = A_x[:, 0:2048]
                k.wait("gpsimd", "sV")
                k.dma("gpsimd", x2s[r, :], xsl, "sG")
                k.wait("vector", "sS")
                k.op("vector", "tensor_reduce", s="sV", out=stat[:, 0:1], in_=xsl,
                     axis=AX.X, op=ALU.add)
                k.wait("scalar", "sV")
                k.op("scalar", "activation", s="sS", out=stat[:, 1:2],
                     in_=stat[:, 0:1], func=AF.Identity, scale=R2048)
                k.wait("vector", "sS")
                k.wait("vector", "sG")
                k.op("vector", "tensor_scalar", out=xsl, in0=xsl,
                     scalar1=stat[:, 1:2], scalar2=None, op0=ALU.subtract)
                k.op("vector", "tensor_tensor", out=sq, in0=xsl, in1=xsl, op=ALU.mult)
                k.op("vector", "tensor_reduce", s="sV", out=stat[:, 2:3], in_=sq,
                     axis=AX.X, op=ALU.add)
                k.wait("scalar", "sV")
                k.op("scalar", "activation", s="sS", out=stat[:, 3:4], in_=stat[:, 2:3],
                     func=AF.Sqrt, scale=R2048, bias=EPS5)
                k.wait("vector", "sS")
                k.op("vector", "reciprocal", s="sV", out=stat[:, 4:5], in_=stat[:, 3:4])
                k.wait("vector", "sV")
                k.op("vector", "tensor_scalar", s="sV", out=nsl, in0=xsl,
                     scalar1=stat[:, 4:5], scalar2=None, op0=ALU.mult)
                k.wait("gpsimd", "sV")
                k.dma("gpsimd", xn2s[r, :], nsl, "sG")
                k.op("vector", "tensor_tensor", out=sq, in0=nsl, in1=nsl, op=ALU.mult)
                k.op("vector", "tensor_reduce", s="sV", out=stat[:, 5:6], in_=sq,
                     axis=AX.X, op=ALU.add)
                k.wait("scalar", "sV")
                k.op("scalar", "activation", s="sS", out=stat[:, 6:7], in_=stat[:, 5:6],
                     func=AF.Sqrt, scale=R2048, bias=EPS6)
                k.wait("vector", "sS")
                k.op("vector", "reciprocal", s="sV", out=stat[:, 7:8], in_=stat[:, 6:7])
                k.wait("vector", "sV")
                k.op("vector", "tensor_scalar", out=xsl, in0=nsl,
                     scalar1=stat[:, 7:8], scalar2=None, op0=ALU.mult)
                k.op("vector", "tensor_tensor", out=xsl, in0=xsl, in1=nw[:], op=ALU.mult)
                k.op("vector", "tensor_reduce", s="sV", out=stat[:, 11:12], in_=xsl,
                     axis=AX.X, op=ALU.max, apply_absolute_value=True)
                k.wait("vector", "sV")
                k.op("vector", "tensor_tensor", s="sV", out=gcol2[:, tile:tile + 1],
                     in0=stat[:, 11:12], in1=conb[:, 4:5], op=ALU.max)
                k.wait("vector", "sV")
                k.op("vector", "reciprocal", s="sV", out=stat[:, 9:10],
                     in_=gcol2[:, tile:tile + 1])
                k.wait("scalar", "sV")
                k.op("scalar", "activation", s="sS", out=stat[:, 10:11],
                     in_=stat[:, 9:10], func=AF.Identity, scale=127.0)
                k.wait("vector", "sS")
                k.op("vector", "tensor_scalar", out=xsl, in0=xsl,
                     scalar1=stat[:, 10:11], scalar2=None, op0=ALU.mult)
                stg2 = A_wq[:, 4096:6144]
                for hcol in range(4):
                    cs = slice(hcol * 512, hcol * 512 + 512)
                    k.op("vector", "tensor_copy", s="sV", out=xqi[:], in_=xsl[:, cs])
                    k.wait("vector", "sV")
                    k.op("vector", "tensor_copy", s="sV", out=stg2[:, cs], in_=xqi[:])
                k.wait("gpsimd", "sV")
                k.dma("gpsimd", xq2[r, :], stg2, "sG")
        k.barrier()

        emit_ffn(0, xq2, x2s, xn2s, gcol2, False, "y")

        k.wait("gpsimd", "sY")
        k.wait("gpsimd", "sG")
        k.wait("gpsimd", "sW")
        k.wait("gpsimd", "sM")

        block = ctx.enter_context(nc.Block())

        @block.sync
        def _(e):
            for f in k.q["sync"]:
                f(e)

        @block.tensor
        def _(e):
            for f in k.q["tensor"]:
                f(e)

        @block.vector
        def _(e):
            for f in k.q["vector"]:
                f(e)

        @block.scalar
        def _(e):
            for f in k.q["scalar"]:
                f(e)

        @block.gpsimd
        def _(e):
            for f in k.q["gpsimd"]:
                f(e)

    return nc


def _chunkT(w):
    return np.ascontiguousarray(
        w.reshape(16, 128, 16, 128).transpose(2, 1, 0, 3).reshape(16, 128, 2048))


def _masks_for(half):
    ms = np.zeros((NU, 128, 512), dtype=ml_dtypes.bfloat16)
    mi = 0
    for s2 in range(2):
        for sl in JSEQ[s2]:
            gi = half * 1024 + s2 * 512 + np.arange(512)[None, :]
            if sl < 8:
                gj = half * 1024 + sl * 128 + np.arange(128)[:, None]
            else:
                gj = (1 - half) * 1024 + (sl - 8) * 128 + np.arange(128)[:, None]
            ms[mi] = (gj < gi).astype(np.float32)
            mi += 1
    return ms


def kernel(L0_attn_hidden_states, L0_attn_Wq, L0_attn_Wk, L0_attn_Wv, L0_attn_Wo,
           L0_ffn_weight_up, L0_ffn_weight_down, L0_ffn_norm_weight, L0_ffn_gamma):
    x = np.ascontiguousarray(L0_attn_hidden_states, dtype=np.float32)
    wu = np.ascontiguousarray(
        np.asarray(L0_ffn_weight_up, dtype=np.float32)
        .reshape(64, 128, 16, 128).transpose(0, 3, 2, 1).reshape(64, 128, 2048))
    wdn = np.ascontiguousarray(
        np.asarray(L0_ffn_weight_down, dtype=np.float32).T.reshape(64, 128, 2048))
    wq_ = _chunkT(np.asarray(L0_attn_Wq, dtype=np.float32))
    wk_ = _chunkT(np.asarray(L0_attn_Wk, dtype=np.float32))
    wv_ = _chunkT(np.asarray(L0_attn_Wv, dtype=np.float32))
    wo_ = _chunkT(np.asarray(L0_attn_Wo, dtype=np.float32))
    nwr = np.ascontiguousarray(
        np.broadcast_to(np.asarray(L0_ffn_norm_weight, dtype=np.float32), (128, E)))
    gamv = np.asarray(L0_ffn_gamma, dtype=np.float32).reshape(1, E)
    con = np.zeros((128, 6), dtype=np.float32)
    con[:, 0] = 1.0
    con[:, 1] = 1e-5
    con[:, 2] = 1e-6
    con[:, 3] = 127.0
    con[:, 4] = 1e-8
    ones128 = np.ones((128, 128), dtype=np.float32)
    idn_f = np.eye(128, dtype=np.float32)
    idn_b = np.eye(128, dtype=np.float32).astype(ml_dtypes.bfloat16)
    jj = np.arange(128)
    tmat = (-(jj[:, None] >= jj[None, :]).astype(np.float32)).astype(ml_dtypes.bfloat16)

    nc = build()
    in_maps = []
    for c in range(8):
        b, half = c // 2, c % 2
        own = x[b, half * 1024:(half + 1) * 1024]
        oth2 = x[b, (1 - half) * 1024:(2 - half) * 1024]
        x_c = np.ascontiguousarray(np.concatenate([own, oth2], axis=0))
        in_maps.append({
            "x_in": x_c, "wu": wu, "wd": wdn, "wq_": wq_, "wk_": wk_,
            "wv_": wv_, "wo_": wo_, "nwr": nwr, "gamv": gamv, "con": con,
            "ones128": ones128, "idn_f": idn_f, "idn_b": idn_b, "tmat": tmat,
            "masks": _masks_for(half),
        })
    res = run_bass_kernel_spmd(nc, in_maps, core_ids=list(range(8)))
    out = np.empty((4, T, E), dtype=np.float32)
    for c in range(8):
        b, half = c // 2, c % 2
        out[b, half * 1024:(half + 1) * 1024] = res.results[c]["y"]
    return out


if __name__ == "__main__":
    import reference as _r
    ins = _r.setup_inputs()
    o = kernel(**{kk: np.asarray(v) for kk, v in ins.items()})
    e = np.asarray(_r.reference(**ins))
    err = np.linalg.norm(o - e) / (np.linalg.norm(e) + 1e-12)
    print("rel l2:", err)


# version 04
# speedup vs baseline: 1.0483x; 1.0483x over previous
import numpy as np
import ml_dtypes
import concourse.bass as bass
import concourse.mybir as mybir
from concourse.bass_utils import run_bass_kernel_spmd
from contextlib import ExitStack

F32 = mybir.dt.float32
BF16 = mybir.dt.bfloat16
I32 = mybir.dt.int32
AF = mybir.ActivationFunctionType
ALU = mybir.AluOpType
AX = mybir.AxisListType

T = 2048
E = 2048
NJC = 64
NT = 16
OWN = 8
NH = 32
JB = 4
R2048 = float(2 ** -11)
RMEAN = float(2 ** -24)

JSEQ = [
    [3, 2, 1, 0, 15, 14, 13, 12, 11, 10, 9, 8],
    [7, 6, 5, 4, 3, 2, 1, 0, 15, 14, 13, 12, 11, 10, 9, 8],
]
NU = len(JSEQ[0]) + len(JSEQ[1])


class K:
    def __init__(self):
        self.q = {"sync": [], "tensor": [], "vector": [], "scalar": [], "gpsimd": []}
        self.cnt = {}
        self.sems = {}

    def inc(self, s, k):
        self.cnt[s] = self.cnt.get(s, 0) + k
        return self.cnt[s]

    def c(self, s):
        return self.cnt.get(s, 0)

    def emit(self, eng, fn):
        self.q[eng].append(fn)

    def wait(self, eng, s, val=None):
        v = self.c(s) if val is None else val
        if v <= 0:
            return
        self.emit(eng, lambda e, h=s, v=v: e.wait_ge(self.sems[h], v))

    def dma(self, eng, out, in_, s):
        self.inc(s, 16)
        self.emit(eng, lambda e, o=out, i=in_, h=s: e.dma_start(out=o, in_=i).then_inc(self.sems[h], 16))

    def op(self, eng, meth, s=None, **kw):
        if s is not None:
            self.inc(s, 1)

        def f(e, m=meth, kw=kw, h=s):
            r = getattr(e, m)(**kw)
            if h is not None:
                r.then_inc(self.sems[h], 1)
        self.emit(eng, f)

    def mm(self, out, lhsT, rhs, start, stop, s=None):
        if s is not None:
            self.inc(s, 1)

        def f(e, o=out, l=lhsT, r=rhs, st=start, sp=stop, h=s):
            x = e.matmul(o, l, r, start=st, stop=sp)
            if h is not None:
                x.then_inc(self.sems[h], 1)
        self.emit("tensor", f)

    def tr(self, out, in_, idn, s=None):
        if s is not None:
            self.inc(s, 1)

        def f(e, o=out, i=in_, d=idn, h=s):
            x = e.transpose(o, i, d)
            if h is not None:
                x.then_inc(self.sems[h], 1)
        self.emit("tensor", f)

    def vnop(self):
        self.inc("sV", 1)
        self.emit("vector", lambda e: e.nop().then_inc(self.sems["sV"], 1))

    def barrier(self):
        sems = [s for s in self.cnt if self.cnt[s] > 0]
        for eng in self.q:
            for s in sems:
                self.wait(eng, s)


def build():
    nc = bass.Bass("TRN2", target_bir_lowering=False, debug=False)
    ap = lambda t: t.ap()
    x_in = ap(nc.dram_tensor("x_in", [T, E], F32, kind="ExternalInput"))
    wu = ap(nc.dram_tensor("wu", [NJC, 128, E], F32, kind="ExternalInput"))
    wd = ap(nc.dram_tensor("wd", [NJC, 128, E], F32, kind="ExternalInput"))
    wq_ = ap(nc.dram_tensor("wq_", [16, 128, E], F32, kind="ExternalInput"))
    wk_ = ap(nc.dram_tensor("wk_", [16, 128, E], F32, kind="ExternalInput"))
    wv_ = ap(nc.dram_tensor("wv_", [16, 128, E], F32, kind="ExternalInput"))
    wo_ = ap(nc.dram_tensor("wo_", [16, 128, E], F32, kind="ExternalInput"))
    nwr = ap(nc.dram_tensor("nwr", [128, E], F32, kind="ExternalInput"))
    gamv = ap(nc.dram_tensor("gamv", [1, E], F32, kind="ExternalInput"))
    con = ap(nc.dram_tensor("con", [128, 6], F32, kind="ExternalInput"))
    ones128 = ap(nc.dram_tensor("ones128", [128, 128], F32, kind="ExternalInput"))
    idn_f = ap(nc.dram_tensor("idn_f", [128, 128], F32, kind="ExternalInput"))
    idn_b = ap(nc.dram_tensor("idn_b", [128, 128], BF16, kind="ExternalInput"))
    tmat = ap(nc.dram_tensor("tmat", [128, 128], BF16, kind="ExternalInput"))
    masks = ap(nc.dram_tensor("masks", [NU, 128, 512], BF16, kind="ExternalInput"))
    y = ap(nc.dram_tensor("y", [OWN * 128, E], F32, kind="ExternalOutput"))
    sc = lambda n, sh, dt: ap(nc.dram_tensor(n, sh, dt, kind="Internal"))
    x1s = sc("x1s", [T, E], F32)
    xns = sc("xns", [T, E], F32)
    x2s = sc("x2s", [OWN * 128, E], F32)
    xn2s = sc("xn2s", [OWN * 128, E], F32)
    xq1 = sc("xq1", [T, E], BF16)
    xq2 = sc("xq2", [OWN * 128, E], BF16)
    wqsu = sc("wqsu", [NJC, 128, E], BF16)
    wqsd = sc("wqsd", [NJC, 128, E], BF16)
    xt1h = sc("xt1h", [16, 128, T], BF16)
    xt1l = sc("xt1l", [16, 128, T], BF16)
    qth = sc("qth", [E, OWN * 128], BF16)
    qtl = sc("qtl", [E, OWN * 128], BF16)
    kth = sc("kth", [E, T], BF16)
    ktl = sc("ktl", [E, T], BF16)
    vth = sc("vth", [E, T], BF16)
    vtl = sc("vtl", [E, T], BF16)
    oth = sc("oth", [E, OWN * 128], BF16)
    otl = sc("otl", [E, OWN * 128], BF16)

    k = K()
    with ExitStack() as ctx:
        sb = lambda n, sh, dt: ctx.enter_context(nc.sbuf_tensor(n, sh, dt))
        psb = lambda n, sh: ctx.enter_context(nc.psum_tensor(n, sh, F32))
        for s in ["sW", "sG", "sM", "sT", "sV", "sS", "sY"]:
            k.sems[s] = ctx.enter_context(nc.semaphore(s))

        A_bf = sb("A_bf", [128, 32768], BF16)
        A_f32 = sb("A_f32", [128, 16384], F32)
        A_wq = sb("A_wq", [128, 12288], BF16)
        A_x = sb("A_x", [128, 4096], F32)
        A_w = sb("A_w", [128, 2048], F32)
        nw = sb("nw", [128, E], F32)
        gprep = sb("gprep", [128, E], F32)
        xqi = sb("xqi", [128, 512], I32)
        conb = sb("conb", [128, 6], F32)
        o128 = sb("o128", [128, 128], F32)
        idf = sb("idf", [128, 128], F32)
        idb = sb("idb", [128, 128], BF16)
        tmb = sb("tmb", [128, 128], BF16)
        stat = sb("stat", [128, 24], F32)
        gcol = sb("gcol", [128, 16], F32)
        gcol2 = sb("gcol2", [128, 16], F32)
        scal = sb("scal", [1, 16], F32)
        srow = sb("srow", [1, 3072], F32)
        carry = srow[0:1, 0:512]
        mbuf = A_bf[:, 12288:13312]
        wsum = stat[:, 20:24]
        r128 = stat[:, 18:20]
        wsumc = lambda i: stat[:, 20 + i:21 + i]
        wsum2 = stat[:, 22:23]

        pA = psb("pA", [128, 512])
        pB = psb("pB", [128, 512])
        pC = psb("pC", [128, 512])
        pD = psb("pD", [128, 512])
        pE = psb("pE", [128, 512])
        pF = psb("pF", [128, 512])
        pG = psb("pG", [128, 512])
        pT = ctx.enter_context(nc.psum_tensor("pT", [128, 1024], BF16))

        ONES = conb[:, 0:1]
        EPS5 = conb[:, 1:2]
        EPS6 = conb[:, 2:3]

        k.dma("sync", conb[:], con, "sW")
        k.dma("sync", o128[:], ones128, "sW")
        k.dma("sync", idf[:], idn_f, "sW")
        k.dma("sync", idb[:], idn_b, "sW")
        k.dma("sync", tmb[:], tmat, "sW")
        k.dma("sync", nw[:], nwr, "sW")
        k.dma("sync", srow[0:1, 1024:1024 + E], gamv, "sW")
        k.emit("gpsimd", lambda e: e.memset(wsum, 0.0).then_inc(k.sems["sG"], 1))
        k.inc("sG", 1)
        wbuf = [A_w[:], A_f32[:, 0:2048], A_f32[:, 2048:4096], A_f32[:, 4096:6144]]
        red_cnt = []
        si = 0
        for wi, wsrc in ((0, wu), (1, wd)):
            for jc in range(NJC):
                if si >= 4:
                    k.wait("sync", "sV", red_cnt[si - 4])
                k.dma("sync", wbuf[si % 4], wsrc[jc], "sW")
                k.wait("vector", "sW")
                if si == 0:
                    k.wait("vector", "sG", 1)
                k.op("vector", "tensor_reduce", s="sV", out=wsum2, in_=wbuf[si % 4],
                     axis=AX.X, op=ALU.add, apply_absolute_value=True)
                k.wait("vector", "sV")
                k.op("vector", "tensor_tensor", s="sV", out=wsumc(wi),
                     in0=wsum2, in1=wsumc(wi), op=ALU.add)
                red_cnt.append(k.c("sV"))
                si += 1

        def emit_prep(src_x, dst_xn, dst_xq, gc, gc_i, stat0):
            xsl = A_x[:, 0:2048]
            nsl = A_x[:, 2048:4096]
            sq = A_f32[:, 2048:4096]
            k.wait("gpsimd", "sV")
            k.dma("gpsimd", xsl, src_x, "sG")
            k.wait("vector", "sG")
            k.wait("vector", "sS")
            k.op("vector", "tensor_reduce", s="sV", out=stat[:, stat0:stat0 + 1],
                 in_=xsl, axis=AX.X, op=ALU.add)
            k.wait("scalar", "sV")
            k.op("scalar", "activation", s="sS", out=stat[:, stat0 + 1:stat0 + 2],
                 in_=stat[:, stat0:stat0 + 1], func=AF.Identity, scale=R2048)
            k.wait("vector", "sS")
            k.op("vector", "tensor_scalar", out=xsl, in0=xsl,
                 scalar1=stat[:, stat0 + 1:stat0 + 2], scalar2=None, op0=ALU.subtract)
            k.op("vector", "tensor_tensor", out=sq, in0=xsl, in1=xsl, op=ALU.mult)
            k.op("vector", "tensor_reduce", s="sV", out=stat[:, stat0 + 2:stat0 + 3],
                 in_=sq, axis=AX.X, op=ALU.add)
            k.wait("scalar", "sV")
            k.op("scalar", "activation", s="sS", out=stat[:, stat0 + 3:stat0 + 4],
                 in_=stat[:, stat0 + 2:stat0 + 3], func=AF.Sqrt, scale=R2048, bias=EPS5)
            k.wait("vector", "sS")
            k.op("vector", "reciprocal", s="sV", out=stat[:, stat0 + 4:stat0 + 5],
                 in_=stat[:, stat0 + 3:stat0 + 4])
            k.wait("vector", "sV")
            k.op("vector", "tensor_scalar", s="sV", out=nsl, in0=xsl,
                 scalar1=stat[:, stat0 + 4:stat0 + 5], scalar2=None, op0=ALU.mult)
            k.wait("gpsimd", "sV")
            k.dma("gpsimd", dst_xn, nsl, "sG")
            k.op("vector", "tensor_tensor", out=sq, in0=nsl, in1=nsl, op=ALU.mult)
            k.op("vector", "tensor_reduce", s="sV", out=stat[:, stat0 + 5:stat0 + 6],
                 in_=sq, axis=AX.X, op=ALU.add)
            k.wait("scalar", "sV")
            k.op("scalar", "activation", s="sS", out=stat[:, stat0 + 6:stat0 + 7],
                 in_=stat[:, stat0 + 5:stat0 + 6], func=AF.Sqrt, scale=R2048, bias=EPS6)
            k.wait("vector", "sS")
            k.op("vector", "reciprocal", s="sV", out=stat[:, stat0 + 7:stat0 + 8],
                 in_=stat[:, stat0 + 6:stat0 + 7])
            k.wait("vector", "sV")
            k.op("vector", "tensor_scalar", out=xsl, in0=nsl,
                 scalar1=stat[:, stat0 + 7:stat0 + 8], scalar2=None, op0=ALU.mult)
            k.op("vector", "tensor_tensor", out=xsl, in0=xsl, in1=nw[:], op=ALU.mult)
            k.op("vector", "tensor_reduce", s="sV", out=stat[:, stat0 + 11:stat0 + 12],
                 in_=xsl, axis=AX.X, op=ALU.max, apply_absolute_value=True)
            k.wait("vector", "sV")
            k.op("vector", "tensor_tensor", s="sV", out=gc[:, gc_i:gc_i + 1],
                 in0=stat[:, stat0 + 11:stat0 + 12], in1=conb[:, 4:5], op=ALU.max)
            k.wait("vector", "sV")
            k.op("vector", "reciprocal", s="sV", out=stat[:, stat0 + 9:stat0 + 10],
                 in_=gc[:, gc_i:gc_i + 1])
            k.wait("scalar", "sV")
            k.op("scalar", "activation", s="sS", out=stat[:, stat0 + 10:stat0 + 11],
                 in_=stat[:, stat0 + 9:stat0 + 10], func=AF.Identity, scale=127.0)
            k.wait("vector", "sS")
            k.op("vector", "tensor_scalar", out=xsl, in0=xsl,
                 scalar1=stat[:, stat0 + 10:stat0 + 11], scalar2=None, op0=ALU.mult)
            stg = A_wq[:, 4096:6144]
            for hcol in range(4):
                cs = slice(hcol * 512, hcol * 512 + 512)
                k.op("vector", "tensor_copy", s="sV", out=xqi[:], in_=xsl[:, cs])
                k.wait("vector", "sV")
                k.op("vector", "tensor_copy", s="sV", out=stg[:, cs], in_=xqi[:])
            k.wait("gpsimd", "sV")
            k.dma("gpsimd", dst_xq, stg, "sG")

        k.wait("scalar", "sW", 7 * 16)
        for tile in range(NT):
            r = slice(tile * 128, tile * 128 + 128)
            emit_prep(x_in[r, :], xns[r, :], xq1[r, :], gcol, tile, 0)

        k.barrier()
        for wi in range(2):
            k.mm(pG[0:1, 0:1], wsumc(wi), ONES, True, True, s="sT")
            k.wait("scalar", "sT")
            k.op("scalar", "activation", s="sS", out=scal[0:1, wi:wi + 1],
                 in_=pG[0:1, 0:1], func=AF.Identity, scale=RMEAN)
            k.wait("vector", "sS")
            k.op("vector", "tensor_tensor", s="sV", out=scal[0:1, 8 + wi:9 + wi],
                 in0=scal[0:1, wi:wi + 1], in1=conb[0:1, 4:5], op=ALU.max)
            k.wait("vector", "sV")
            k.op("vector", "reciprocal", s="sV", out=scal[0:1, 2 + wi:3 + wi],
                 in_=scal[0:1, 8 + wi:9 + wi])
            k.wait("tensor", "sV")
            k.wait("vector", "sV")
            k.mm(pG[:, 0:1], o128[0:1, :], scal[0:1, 2 + wi:3 + wi], True, True, s="sT")
            k.wait("vector", "sT")
            k.op("vector", "tensor_copy", s="sV", out=stat[:, 18 + wi:19 + wi], in_=pG[:, 0:1])
            k.wait("tensor", "sV")
        k.op("vector", "reciprocal", s="sV", out=scal[0:1, 4:5], in_=conb[0:1, 3:4])
        k.wait("vector", "sV")
        k.op("vector", "tensor_tensor", s="sV", out=scal[0:1, 5:6],
             in0=scal[0:1, 8:9], in1=scal[0:1, 4:5], op=ALU.mult)
        k.wait("scalar", "sV")
        k.op("scalar", "activation", s="sS", out=scal[0:1, 6:7], in_=scal[0:1, 9:10],
             func=AF.Identity, scale=0.5)
        k.wait("vector", "sS")
        k.op("vector", "tensor_scalar", s="sV", out=srow[0:1, 1024:1024 + E],
             in0=srow[0:1, 1024:1024 + E], scalar1=scal[0:1, 6:7], scalar2=None, op0=ALU.mult)
        k.wait("tensor", "sV")
        for g4 in range(4):
            k.mm(pG[:, 0:512], o128[0:1, :], srow[0:1, 1024 + g4 * 512:1536 + g4 * 512],
                 True, True, s="sT")
            k.wait("vector", "sT")
            k.op("vector", "tensor_copy", s="sV", out=gprep[:, g4 * 512:g4 * 512 + 512],
                 in_=pG[:, 0:512])
            k.wait("tensor", "sV")
        k.barrier()

        def emit_ffn(ts0, xq_src, x_src, xn_src, gc, first_w, out_kind):
            n_tiles = 8
            ntok = 1024
            ngrp = 2
            xqT = lambda ec: A_bf[:, ec * 1024: ec * 1024 + ntok]
            hhi = lambda j4: A_bf[:, 16384 + j4 * 1024: 16384 + j4 * 1024 + ntok]
            hlo = lambda j4: A_bf[:, 20480 + j4 * 1024: 20480 + j4 * 1024 + ntok]
            acc = lambda sub: A_f32[:, sub * 2048: sub * 2048 + 2048]
            srep = A_x[:, 0:1024]
            hs = A_x[:, 1024:1536]
            hsl = A_x[:, 1536:2048]
            wqu = lambda par: A_wq[:, 8192 + par * 2048: 8192 + par * 2048 + 2048]
            wqd = lambda j4: A_wq[:, j4 * 2048: j4 * 2048 + 2048]
            stg = A_bf[:, 24576:24576 + 2048]

            for i in range(n_tiles):
                r = slice((ts0 + i) * 128, (ts0 + i) * 128 + 128)
                k.wait("gpsimd", "sT")
                k.dma("gpsimd", stg, xq_src[r, :], "sG")
                k.wait("tensor", "sG")
                for ecb in range(0, 16, 4):
                    for ec in range(ecb, ecb + 4):
                        k.tr(pT[:, (ec % 4) * 128:(ec % 4) * 128 + 128],
                             stg[:, ec * 128:ec * 128 + 128], idb[:],
                             s=("sT" if ec == ecb + 3 else None))
                    k.wait("vector", "sT")
                    for ec in range(ecb, ecb + 4):
                        k.op("vector", "tensor_copy", s=("sV" if ec == ecb + 3 else None),
                             out=xqT(ec)[:, i * 128:i * 128 + 128],
                             in_=pT[:, (ec % 4) * 128:(ec % 4) * 128 + 128])
                    k.wait("tensor", "sV")
            for i in range(n_tiles):
                k.mm(pG[0:1, (i % 4) * 128:(i % 4) * 128 + 128],
                     gc[:, ts0 + i:ts0 + i + 1], idf[:], True, True,
                     s=("sT" if i % 4 == 3 else None))
                if i % 4 == 3:
                    k.wait("vector", "sT")
                    base = (i // 4) * 512
                    k.op("vector", "tensor_scalar", s="sV", out=srow[0:1, base:base + 512],
                         in0=pG[0:1, 0:512], scalar1=scal[0:1, 5:6], scalar2=None,
                         op0=ALU.mult)
                    k.wait("tensor", "sV")
            for g in range(ngrp):
                k.mm(pG[:, 0:512], o128[0:1, :], srow[0:1, g * 512:g * 512 + 512],
                     True, True, s="sT")
                k.wait("vector", "sT")
                k.op("vector", "tensor_copy", s="sV", out=srep[:, g * 512:g * 512 + 512],
                     in_=pG[:, 0:512])
                k.wait("tensor", "sV")

            split_done = {-1: k.c("sV")}
            dn_evac = {0: 0, 1: 0}
            pu_free = {0: 0, 1: 0}
            wqv = {}
            upT = {}
            dnT_prev = 0
            for j4base in range(0, NJC, JB):
                if not first_w:
                    k.wait("sync", "sT", dnT_prev)
                    for jj2 in range(JB):
                        k.dma("sync", wqd(jj2)[:], wqsd[j4base + jj2], "sW")
                for jj in range(JB):
                    jc = j4base + jj
                    par = jc % 2
                    if first_w:
                        k.wait("sync", "sV", wqv.get(jc - 1, 0))
                        k.dma("sync", A_w[:], wu[jc], "sW")
                        k.wait("vector", "sW")
                        k.wait("vector", "sG")
                        k.op("vector", "tensor_scalar", out=A_w[:], in0=A_w[:],
                             scalar1=stat[:, 18:19], scalar2=None, op0=ALU.mult)
                        k.op("vector", "tensor_scalar", out=A_w[:], in0=A_w[:],
                             scalar1=1.0, scalar2=-1.0, op0=ALU.min, op1=ALU.max)
                        for hcol in range(4):
                            cs = slice(hcol * 512, hcol * 512 + 512)
                            k.op("vector", "tensor_copy", s="sV", out=xqi[:], in_=A_w[:, cs])
                            k.wait("vector", "sV")
                            k.op("vector", "tensor_copy", s="sV", out=wqu(par)[:, cs], in_=xqi[:])
                        k.wait("gpsimd", "sV")
                        k.dma("gpsimd", wqsu[jc], wqu(par), "sG")
                        k.wait("sync", "sV")
                        k.dma("sync", A_w[:], wd[jc], "sW")
                        k.wait("vector", "sW")
                        k.wait("vector", "sG")
                        k.op("vector", "tensor_scalar", out=A_w[:], in0=A_w[:],
                             scalar1=stat[:, 19:20], scalar2=None, op0=ALU.mult)
                        k.op("vector", "tensor_scalar", out=A_w[:], in0=A_w[:],
                             scalar1=1.0, scalar2=-1.0, op0=ALU.min, op1=ALU.max)
                        for hcol in range(4):
                            cs = slice(hcol * 512, hcol * 512 + 512)
                            k.op("vector", "tensor_copy", s="sV", out=xqi[:], in_=A_w[:, cs])
                            k.wait("vector", "sV")
                            k.op("vector", "tensor_copy", s="sV", out=wqd(jj)[:, cs], in_=xqi[:])
                        k.wait("gpsimd", "sV")
                        k.dma("gpsimd", wqsd[jc], wqd(jj), "sG")
                        wqv[jc] = k.c("sV")
                        wq_w = None
                    else:
                        k.wait("sync", "sT", upT.get(jc - 2, 0))
                        k.dma("sync", wqu(par)[:], wqsu[jc], "sW")
                        wqv[jc] = 0
                        wq_w = k.c("sW")
                    for g in range(ngrp):
                        gpar = (jc * ngrp + g) % 2
                        pU = pA if gpar == 0 else pB
                        if wq_w is not None:
                            k.wait("tensor", "sW", wq_w)
                        k.wait("tensor", "sV", max(wqv[jc], pu_free[gpar]))
                        for ec in range(16):
                            k.mm(pU[:, 0:512], wqu(par)[:, ec * 128:ec * 128 + 128],
                                 xqT(ec)[:, g * 512:g * 512 + 512],
                                 ec == 0, ec == 15, s=("sT" if ec == 15 else None))
                        if g == ngrp - 1:
                            upT[jc] = k.c("sT")
                        k.wait("vector", "sT")
                        k.op("vector", "tensor_tensor", s="sV", out=hs, in0=pU[:, 0:512],
                             in1=srep[:, g * 512:g * 512 + 512], op=ALU.mult)
                        pu_free[gpar] = k.c("sV")
                        k.wait("scalar", "sV")
                        k.op("scalar", "activation", s="sS", out=hsl, in_=hs, func=AF.Silu)
                        k.wait("vector", "sS")
                        k.op("vector", "tensor_copy",
                             out=hhi(jj)[:, g * 512:g * 512 + 512], in_=hsl)
                        k.op("vector", "scalar_tensor_tensor", s="sV",
                             out=hlo(jj)[:, g * 512:g * 512 + 512], in0=hsl, scalar=1.0,
                             in1=hhi(jj)[:, g * 512:g * 512 + 512],
                             op0=ALU.mult, op1=ALU.subtract)
                    split_done[jc] = k.c("sV")
                for sub in range(8):
                    for dg in range(4):
                        par = (sub * 4 + dg) % 2
                        pDn = pC if par == 0 else pD
                        k.wait("tensor", "sV", max(split_done[j4base + JB - 1], dn_evac[par]))
                        for jj in range(JB):
                            for hl in range(2):
                                hbuf = hhi if hl == 0 else hlo
                                k.mm(pDn[:, 0:512],
                                     hbuf(jj)[:, sub * 128:sub * 128 + 128],
                                     wqd(jj)[:, dg * 512:dg * 512 + 512],
                                     jj == 0 and hl == 0, jj == JB - 1 and hl == 1,
                                     s=("sT" if (jj == JB - 1 and hl == 1) else None))
                        k.wait("vector", "sT")
                        if j4base == 0:
                            k.op("vector", "tensor_copy", s="sV",
                                 out=acc(sub)[:, dg * 512:dg * 512 + 512], in_=pDn[:, 0:512])
                        else:
                            k.op("vector", "tensor_tensor", s="sV",
                                 out=acc(sub)[:, dg * 512:dg * 512 + 512],
                                 in0=pDn[:, 0:512],
                                 in1=acc(sub)[:, dg * 512:dg * 512 + 512], op=ALU.add)
                        dn_evac[par] = k.c("sV")
                dnT_prev = k.c("sT")

            k.barrier()
            for i in range(n_tiles):
                tile = ts0 + i
                r = slice(tile * 128, tile * 128 + 128)
                xsl = A_x[:, 0:2048]
                nsl = A_x[:, 2048:4096]
                if out_kind == "y":
                    k.wait("gpsimd", "sY")
                k.wait("gpsimd", "sV")
                k.wait("gpsimd", "sT")
                k.dma("gpsimd", xsl, x_src[r, :], "sG")
                k.dma("gpsimd", nsl, xn_src[r, :], "sG")
                k.wait("vector", "sG")
                k.op("vector", "tensor_tensor", out=acc(i), in0=acc(i), in1=gprep[:],
                     op=ALU.mult)
                k.op("vector", "scalar_tensor_tensor", out=nsl, in0=nsl, scalar=0.5,
                     in1=xsl, op0=ALU.mult, op1=ALU.add)
                k.op("vector", "tensor_tensor", s="sV", out=xsl, in0=acc(i), in1=nsl,
                     op=ALU.add)
                if out_kind == "y":
                    k.wait("sync", "sV")
                    k.dma("sync", y[i * 128:i * 128 + 128, :], xsl, "sY")
                    continue
                k.wait("gpsimd", "sV")
                k.dma("gpsimd", x1s[r, :], xsl, "sG")
                k.wait("vector", "sG")
                sq = acc(i)
                k.op("vector", "tensor_reduce", s="sV", out=stat[:, 12:13], in_=xsl,
                     axis=AX.X, op=ALU.add)
                k.wait("scalar", "sV")
                k.op("scalar", "activation", s="sS", out=stat[:, 13:14],
                     in_=stat[:, 12:13], func=AF.Identity, scale=R2048)
                k.wait("vector", "sS")
                k.op("vector", "tensor_scalar", out=xsl, in0=xsl,
                     scalar1=stat[:, 13:14], scalar2=None, op0=ALU.subtract)
                k.op("vector", "tensor_tensor", out=sq, in0=xsl, in1=xsl, op=ALU.mult)
                k.op("vector", "tensor_reduce", s="sV", out=stat[:, 14:15], in_=sq,
                     axis=AX.X, op=ALU.add)
                k.wait("scalar", "sV")
                k.op("scalar", "activation", s="sS", out=stat[:, 15:16],
                     in_=stat[:, 14:15], func=AF.Sqrt, scale=R2048, bias=EPS5)
                k.wait("vector", "sS")
                k.op("vector", "reciprocal", s="sV", out=stat[:, 16:17],
                     in_=stat[:, 15:16])
                k.wait("vector", "sV")
                k.op("vector", "tensor_scalar", s="sV", out=xsl, in0=xsl,
                     scalar1=stat[:, 16:17], scalar2=None, op0=ALU.mult)
                k.wait("tensor", "sV")
                sth = A_bf[:, 24576:24576 + 2048]
                stl = A_bf[:, 26624:26624 + 2048]
                for ecb in range(0, 16, 4):
                    for ec in range(ecb, ecb + 4):
                        k.tr(pA[:, (ec % 4) * 128:(ec % 4) * 128 + 128],
                             xsl[:, ec * 128:ec * 128 + 128], idf[:],
                             s=("sT" if ec == ecb + 3 else None))
                    k.wait("vector", "sT")
                    for ec in range(ecb, ecb + 4):
                        psl = pA[:, (ec % 4) * 128:(ec % 4) * 128 + 128]
                        k.op("vector", "tensor_copy",
                             out=sth[:, ec * 128:ec * 128 + 128], in_=psl)
                        k.op("vector", "scalar_tensor_tensor",
                             s=("sV" if ec == ecb + 3 else None),
                             out=stl[:, ec * 128:ec * 128 + 128], in0=psl, scalar=1.0,
                             in1=sth[:, ec * 128:ec * 128 + 128],
                             op0=ALU.mult, op1=ALU.subtract)
                    k.wait("tensor", "sV")
                k.wait("gpsimd", "sV")
                for ec in range(16):
                    k.dma("gpsimd", xt1h[ec, :, tile * 128:tile * 128 + 128],
                          sth[:, ec * 128:ec * 128 + 128], "sG")
                    k.dma("gpsimd", xt1l[ec, :, tile * 128:tile * 128 + 128],
                          stl[:, ec * 128:ec * 128 + 128], "sG")
            k.barrier()

        emit_ffn(0, xq1, x_in, xns, gcol, True, "x1")
        emit_ffn(8, xq1, x_in, xns, gcol, False, "x1")

        def emit_proj(mats, half):
            hof = half * 1024
            xh = lambda ec: A_bf[:, ec * 1024: ec * 1024 + 1024]
            xl = lambda ec: A_bf[:, 16384 + ec * 1024: 16384 + ec * 1024 + 1024]
            for ec in range(16):
                k.dma("sync", xh(ec), xt1h[ec, :, hof:hof + 1024], "sW")
                k.dma("sync", xl(ec), xt1l[ec, :, hof:hof + 1024], "sW")
            sth = A_wq[:, 8192:8192 + 512]
            stl = A_wq[:, 8704:8704 + 512]
            for (wsrc, dsth, dstl, ownrel) in mats:
                for dc in range(16):
                    par = dc % 2
                    whi = A_wq[:, par * 4096:par * 4096 + 2048]
                    wlo = A_wq[:, par * 4096 + 2048:par * 4096 + 4096]
                    k.wait("sync", "sT")
                    k.dma("sync", A_w[:], wsrc[dc], "sW")
                    k.wait("vector", "sW")
                    k.op("vector", "tensor_copy", out=whi[:], in_=A_w[:])
                    k.op("vector", "scalar_tensor_tensor", s="sV", out=wlo[:], in0=A_w[:],
                         scalar=1.0, in1=whi[:], op0=ALU.mult, op1=ALU.subtract)
                    for g in range(2):
                        pP = pA if g == 0 else pB
                        k.wait("tensor", "sW")
                        k.wait("tensor", "sV")
                        for ec in range(16):
                            xhs = xh(ec)[:, g * 512:g * 512 + 512]
                            xls = xl(ec)[:, g * 512:g * 512 + 512]
                            wh = whi[:, ec * 128:ec * 128 + 128]
                            wl = wlo[:, ec * 128:ec * 128 + 128]
                            k.mm(pP[:, 0:512], wh, xhs, ec == 0, False)
                            k.mm(pP[:, 0:512], wl, xhs, False, False)
                            k.mm(pP[:, 0:512], wh, xls, False, ec == 15,
                                 s=("sT" if ec == 15 else None))
                        k.wait("vector", "sT")
                        k.wait("vector", "sG")
                        qscale = 0.125 if ownrel else 1.0
                        k.op("vector", "tensor_scalar", out=sth[:], in0=pP[:, 0:512],
                             scalar1=qscale, scalar2=None, op0=ALU.mult)
                        k.op("vector", "scalar_tensor_tensor", s="sV", out=stl[:],
                             in0=pP[:, 0:512], scalar=qscale, in1=sth[:],
                             op0=ALU.mult, op1=ALU.subtract)
                        k.wait("gpsimd", "sV")
                        rr = slice(dc * 128, dc * 128 + 128)
                        if ownrel:
                            cc = slice(g * 512, g * 512 + 512)
                        else:
                            cc = slice(hof + g * 512, hof + g * 512 + 512)
                        k.dma("gpsimd", dsth[rr, cc], sth[:], "sG")
                        k.dma("gpsimd", dstl[rr, cc], stl[:], "sG")
            k.barrier()

        emit_proj([(wk_, kth, ktl, False), (wv_, vth, vtl, False),
                   (wq_, qth, qtl, True)], 0)
        emit_proj([(wk_, kth, ktl, False), (wv_, vth, vtl, False)], 1)

        kb_h = A_bf[0:64, 0:2048]
        kb_l = A_bf[0:64, 2048:4096]
        vtb = A_bf[:, 4096:6144]
        vb_h = A_bf[:, 6144:7168]
        vb_l = A_bf[:, 7168:8192]
        qb_h = A_bf[0:64, 8192:8704]
        qb_l = A_bf[0:64, 8704:9216]
        p_hi = lambda u: A_bf[:, 13312 + (u % 2) * 512:13824 + (u % 2) * 512]
        p_lo = lambda u: A_bf[:, 14336 + (u % 2) * 512:14848 + (u % 2) * 512]
        e_hi = lambda u: A_bf[:, 15360 + (u % 2) * 512:15872 + (u % 2) * 512]
        e_lo = lambda u: A_bf[:, 16384 + (u % 2) * 512:16896 + (u % 2) * 512]
        osth = A_bf[0:64, 11264:11776]
        ostl = A_bf[0:64, 11776:12288]
        t_f = lambda u: A_f32[:, (u % 2) * 2048:(u % 2) * 2048 + 512]
        p_f = lambda u: A_f32[:, (u % 2) * 2048 + 512:(u % 2) * 2048 + 1024]
        eS_f = lambda u: A_f32[:, (u % 2) * 2048 + 1024:(u % 2) * 2048 + 1536]
        e_f = lambda u: A_f32[:, (u % 2) * 2048 + 1536:(u % 2) * 2048 + 2048]

        for h in range(NH):
            hr = slice(h * 64, h * 64 + 64)
            k.wait("sync", "sT")
            k.wait("sync", "sV")
            k.dma("sync", kb_h, kth[hr, :], "sW")
            k.dma("sync", kb_l, ktl[hr, :], "sW")
            k.dma("sync", vtb[0:64, :], vth[hr, :], "sW")
            k.dma("sync", vtb[64:128, :], vtl[hr, :], "sW")
            k.wait("tensor", "sW")
            for jtb in range(0, 16, 4):
                for jt in range(jtb, jtb + 4):
                    k.tr(pT[:, (jt % 4) * 128:(jt % 4) * 128 + 128],
                         vtb[:, jt * 128:jt * 128 + 128], idb[:],
                         s=("sT" if jt == jtb + 3 else None))
                k.wait("vector", "sT")
                for jt in range(jtb, jtb + 4):
                    psl = pT[:, (jt % 4) * 128:(jt % 4) * 128 + 128]
                    k.op("vector", "tensor_copy", out=vb_h[:, jt * 64:jt * 64 + 64],
                         in_=psl[:, 0:64])
                    k.op("vector", "tensor_copy", s=("sV" if jt == jtb + 3 else None),
                         out=vb_l[:, jt * 64:jt * 64 + 64], in_=psl[:, 64:128])
                k.wait("tensor", "sV")
            for s2 in range(2):
                seq = JSEQ[s2]
                nu = len(seq)
                pO = pE if s2 == 0 else pF
                k.wait("sync", "sT")
                k.dma("sync", qb_h, qth[hr, s2 * 512:s2 * 512 + 512], "sW")
                k.dma("sync", qb_l, qtl[hr, s2 * 512:s2 * 512 + 512], "sW")
                k.inc("sS", 1)
                k.emit("scalar", lambda e: e.memzero(carry).then_inc(k.sems["sS"], 1))
                qld = k.c("sW")
                base_ts = k.c("sT")
                base_ss = k.c("sS")
                base_vs = k.c("sV")
                base_sm = k.c("sM")
                mbase = 0 if s2 == 0 else len(JSEQ[0])
                for pf in range(2):
                    k.dma("scalar", mbuf[:, (pf % 2) * 512:(pf % 2) * 512 + 512],
                          masks[mbase + pf], "sM")
                for ui, sl in enumerate(seq):
                    mpar = ui % 2
                    ks = slice(sl * 128, sl * 128 + 128)
                    vs_ = slice(sl * 64, sl * 64 + 64)
                    pZ = pA if ui % 2 == 0 else pC
                    pS = pB if ui % 2 == 0 else pD
                    if ui == 0:
                        k.wait("tensor", "sW", qld)
                    if ui < 2:
                        k.wait("tensor", "sS", base_ss)
                    else:
                        k.wait("tensor", "sS", base_ss + 4 * (ui - 2) + 1)
                    k.mm(pZ[:, 0:512], kb_h[:, ks], qb_h[:], True, False)
                    k.mm(pZ[:, 0:512], kb_l[:, ks], qb_h[:], False, False)
                    k.mm(pZ[:, 0:512], kb_h[:, ks], qb_l[:], False, True, s="sT")
                    k.wait("tensor", "sV", base_vs + 3 * ui + 2)
                    k.wait("tensor", "sS", base_ss + 4 * ui - 1 if ui > 0 else base_ss)
                    k.mm(pS[:, 0:512], tmb[:], p_hi(ui), True, False)
                    k.mm(pS[:, 0:512], tmb[:], p_lo(ui), False, False)
                    k.mm(pS[:, 0:512], o128[0:1, :], carry, False, True, s="sT")
                    k.wait("tensor", "sV", base_vs + 3 * ui + 3)
                    k.mm(pO[0:64, 0:512], vb_h[:, vs_], e_hi(ui), ui == 0, False)
                    k.mm(pO[0:64, 0:512], vb_h[:, vs_], e_lo(ui), False, False)
                    k.mm(pO[0:64, 0:512], vb_l[:, vs_], e_hi(ui), False, ui == nu - 1, s="sT")
                    k.wait("scalar", "sT", base_ts + 3 * ui + 1)
                    if ui >= 2:
                        k.wait("scalar", "sV", base_vs + 3 * (ui - 2) + 3)
                    k.op("scalar", "activation", s="sS", out=t_f(ui), in_=pZ[:, 0:512],
                         func=AF.Exp)
                    k.wait("scalar", "sV", base_vs + 3 * ui + 1)
                    k.op("scalar", "activation", s="sS", out=p_f(ui), in_=t_f(ui),
                         func=AF.Ln, bias=ONES)
                    if ui + 2 < nu:
                        k.dma("scalar", mbuf[:, ((ui + 2) % 2) * 512:((ui + 2) % 2) * 512 + 512],
                              masks[mbase + ui + 2], "sM")
                    k.wait("scalar", "sT", base_ts + 3 * ui + 2)
                    k.op("scalar", "copy", s="sS", out=carry, in_=pS[0:1, 0:512])
                    k.op("scalar", "activation", s="sS", out=eS_f(ui), in_=pS[:, 0:512],
                         func=AF.Exp)
                    k.wait("vector", "sS", base_ss + 4 * ui + 1)
                    k.wait("vector", "sM", base_sm + 16 * (ui + 1))
                    k.op("vector", "tensor_tensor", s="sV", out=t_f(ui), in0=t_f(ui),
                         in1=mbuf[:, mpar * 512:mpar * 512 + 512], op=ALU.mult)
                    k.wait("vector", "sS", base_ss + 4 * ui + 2)
                    if ui >= 2:
                        k.wait("vector", "sT", base_ts + 3 * (ui - 2) + 2)
                    k.op("vector", "tensor_copy", out=p_hi(ui), in_=p_f(ui))
                    k.op("vector", "scalar_tensor_tensor", s="sV", out=p_lo(ui), in0=p_f(ui),
                         scalar=1.0, in1=p_hi(ui), op0=ALU.mult, op1=ALU.subtract)
                    k.wait("vector", "sS", base_ss + 4 * ui + 4)
                    if ui >= 2:
                        k.wait("vector", "sT", base_ts + 3 * (ui - 2) + 3)
                    k.op("vector", "tensor_tensor", out=e_f(ui), in0=t_f(ui), in1=eS_f(ui),
                         op=ALU.mult)
                    k.op("vector", "tensor_copy", out=e_hi(ui), in_=e_f(ui))
                    k.op("vector", "scalar_tensor_tensor", s="sV", out=e_lo(ui), in0=e_f(ui),
                         scalar=1.0, in1=e_hi(ui), op0=ALU.mult, op1=ALU.subtract)
                k.wait("vector", "sT")
                k.wait("vector", "sG")
                k.op("vector", "tensor_copy", out=osth[:], in_=pO[0:64, 0:512])
                k.op("vector", "scalar_tensor_tensor", s="sV", out=ostl[:],
                     in0=pO[0:64, 0:512], scalar=1.0, in1=osth[:],
                     op0=ALU.mult, op1=ALU.subtract)
                k.wait("gpsimd", "sV")
                k.dma("gpsimd", oth[hr, s2 * 512:s2 * 512 + 512], osth[:], "sG")
                k.dma("gpsimd", otl[hr, s2 * 512:s2 * 512 + 512], ostl[:], "sG")
        k.barrier()

        oh = lambda hc: A_bf[:, hc * 1024: hc * 1024 + 1024]
        ol = lambda hc: A_bf[:, 16384 + hc * 1024: 16384 + hc * 1024 + 1024]
        for hc in range(16):
            rr = slice(hc * 128, hc * 128 + 128)
            k.dma("sync", oh(hc), oth[rr, :], "sW")
            k.dma("sync", ol(hc), otl[rr, :], "sW")
        whh = A_wq[:, 0:2048]
        wll = A_wq[:, 2048:4096]
        evb = A_x[:, 0:512]
        for g in range(2):
            for i_l in range(4):
                tile = g * 4 + i_l
                k.wait("gpsimd", "sV")
                k.wait("gpsimd", "sS")
                k.dma("gpsimd", A_f32[:, i_l * 2048:i_l * 2048 + 2048],
                      x1s[tile * 128:tile * 128 + 128, :], "sG")
            for ec in range(16):
                k.wait("sync", "sT")
                k.dma("sync", A_w[:], wo_[ec], "sW")
                k.wait("vector", "sW")
                k.wait("vector", "sT")
                k.op("vector", "tensor_copy", out=whh[:], in_=A_w[:])
                k.op("vector", "scalar_tensor_tensor", s="sV", out=wll[:], in0=A_w[:],
                     scalar=1.0, in1=whh[:], op0=ALU.mult, op1=ALU.subtract)
                k.wait("tensor", "sV")
                k.wait("tensor", "sW")
                for hc in range(16):
                    wh = whh[:, hc * 128:hc * 128 + 128]
                    wl = wll[:, hc * 128:hc * 128 + 128]
                    ohs = oh(hc)[:, g * 512:g * 512 + 512]
                    ols = ol(hc)[:, g * 512:g * 512 + 512]
                    k.mm(pG[:, 0:512], wh, ohs, hc == 0, False)
                    k.mm(pG[:, 0:512], wl, ohs, False, False)
                    k.mm(pG[:, 0:512], wh, ols, False, hc == 15,
                         s=("sT" if hc == 15 else None))
                k.wait("vector", "sT")
                k.op("vector", "tensor_copy", s="sV", out=evb[:], in_=pG[:, 0:512])
                k.wait("tensor", "sV")
                k.wait("tensor", "sG")
                for i_l in range(4):
                    k.tr(pA[:, i_l * 128:i_l * 128 + 128],
                         evb[:, i_l * 128:i_l * 128 + 128], idf[:],
                         s=("sT" if i_l == 3 else None))
                k.wait("vector", "sT")
                for i_l in range(4):
                    dst = A_f32[:, i_l * 2048 + ec * 128:i_l * 2048 + ec * 128 + 128]
                    k.op("vector", "tensor_tensor", s=("sV" if i_l == 3 else None),
                         out=dst, in0=pA[:, i_l * 128:i_l * 128 + 128], in1=dst,
                         op=ALU.add)
                k.wait("tensor", "sV")
            for i_l in range(4):
                tile = g * 4 + i_l
                r = slice(tile * 128, tile * 128 + 128)
                xsl = A_f32[:, i_l * 2048:i_l * 2048 + 2048]
                nsl = A_x[:, 2048:4096]
                sq = A_x[:, 0:2048]
                k.wait("gpsimd", "sV")
                k.dma("gpsimd", x2s[r, :], xsl, "sG")
                k.wait("vector", "sS")
                k.op("vector", "tensor_reduce", s="sV", out=stat[:, 0:1], in_=xsl,
                     axis=AX.X, op=ALU.add)
                k.wait("scalar", "sV")
                k.op("scalar", "activation", s="sS", out=stat[:, 1:2],
                     in_=stat[:, 0:1], func=AF.Identity, scale=R2048)
                k.wait("vector", "sS")
                k.wait("vector", "sG")
                k.op("vector", "tensor_scalar", out=xsl, in0=xsl,
                     scalar1=stat[:, 1:2], scalar2=None, op0=ALU.subtract)
                k.op("vector", "tensor_tensor", out=sq, in0=xsl, in1=xsl, op=ALU.mult)
                k.op("vector", "tensor_reduce", s="sV", out=stat[:, 2:3], in_=sq,
                     axis=AX.X, op=ALU.add)
                k.wait("scalar", "sV")
                k.op("scalar", "activation", s="sS", out=stat[:, 3:4], in_=stat[:, 2:3],
                     func=AF.Sqrt, scale=R2048, bias=EPS5)
                k.wait("vector", "sS")
                k.op("vector", "reciprocal", s="sV", out=stat[:, 4:5], in_=stat[:, 3:4])
                k.wait("vector", "sV")
                k.op("vector", "tensor_scalar", s="sV", out=nsl, in0=xsl,
                     scalar1=stat[:, 4:5], scalar2=None, op0=ALU.mult)
                k.wait("gpsimd", "sV")
                k.dma("gpsimd", xn2s[r, :], nsl, "sG")
                k.op("vector", "tensor_tensor", out=sq, in0=nsl, in1=nsl, op=ALU.mult)
                k.op("vector", "tensor_reduce", s="sV", out=stat[:, 5:6], in_=sq,
                     axis=AX.X, op=ALU.add)
                k.wait("scalar", "sV")
                k.op("scalar", "activation", s="sS", out=stat[:, 6:7], in_=stat[:, 5:6],
                     func=AF.Sqrt, scale=R2048, bias=EPS6)
                k.wait("vector", "sS")
                k.op("vector", "reciprocal", s="sV", out=stat[:, 7:8], in_=stat[:, 6:7])
                k.wait("vector", "sV")
                k.op("vector", "tensor_scalar", out=xsl, in0=nsl,
                     scalar1=stat[:, 7:8], scalar2=None, op0=ALU.mult)
                k.op("vector", "tensor_tensor", out=xsl, in0=xsl, in1=nw[:], op=ALU.mult)
                k.op("vector", "tensor_reduce", s="sV", out=stat[:, 11:12], in_=xsl,
                     axis=AX.X, op=ALU.max, apply_absolute_value=True)
                k.wait("vector", "sV")
                k.op("vector", "tensor_tensor", s="sV", out=gcol2[:, tile:tile + 1],
                     in0=stat[:, 11:12], in1=conb[:, 4:5], op=ALU.max)
                k.wait("vector", "sV")
                k.op("vector", "reciprocal", s="sV", out=stat[:, 9:10],
                     in_=gcol2[:, tile:tile + 1])
                k.wait("scalar", "sV")
                k.op("scalar", "activation", s="sS", out=stat[:, 10:11],
                     in_=stat[:, 9:10], func=AF.Identity, scale=127.0)
                k.wait("vector", "sS")
                k.op("vector", "tensor_scalar", out=xsl, in0=xsl,
                     scalar1=stat[:, 10:11], scalar2=None, op0=ALU.mult)
                stg2 = A_wq[:, 4096:6144]
                for hcol in range(4):
                    cs = slice(hcol * 512, hcol * 512 + 512)
                    k.op("vector", "tensor_copy", s="sV", out=xqi[:], in_=xsl[:, cs])
                    k.wait("vector", "sV")
                    k.op("vector", "tensor_copy", s="sV", out=stg2[:, cs], in_=xqi[:])
                k.wait("gpsimd", "sV")
                k.dma("gpsimd", xq2[r, :], stg2, "sG")
        k.barrier()

        emit_ffn(0, xq2, x2s, xn2s, gcol2, False, "y")

        k.wait("gpsimd", "sY")
        k.wait("gpsimd", "sG")
        k.wait("gpsimd", "sW")
        k.wait("gpsimd", "sM")

        block = ctx.enter_context(nc.Block())

        @block.sync
        def _(e):
            for f in k.q["sync"]:
                f(e)

        @block.tensor
        def _(e):
            for f in k.q["tensor"]:
                f(e)

        @block.vector
        def _(e):
            for f in k.q["vector"]:
                f(e)

        @block.scalar
        def _(e):
            for f in k.q["scalar"]:
                f(e)

        @block.gpsimd
        def _(e):
            for f in k.q["gpsimd"]:
                f(e)

    return nc


def _chunkT(w):
    return np.ascontiguousarray(
        w.reshape(16, 128, 16, 128).transpose(2, 1, 0, 3).reshape(16, 128, 2048))


def _masks_for(half):
    ms = np.zeros((NU, 128, 512), dtype=ml_dtypes.bfloat16)
    mi = 0
    for s2 in range(2):
        for sl in JSEQ[s2]:
            gi = half * 1024 + s2 * 512 + np.arange(512)[None, :]
            if sl < 8:
                gj = half * 1024 + sl * 128 + np.arange(128)[:, None]
            else:
                gj = (1 - half) * 1024 + (sl - 8) * 128 + np.arange(128)[:, None]
            ms[mi] = (gj < gi).astype(np.float32)
            mi += 1
    return ms


def kernel(L0_attn_hidden_states, L0_attn_Wq, L0_attn_Wk, L0_attn_Wv, L0_attn_Wo,
           L0_ffn_weight_up, L0_ffn_weight_down, L0_ffn_norm_weight, L0_ffn_gamma):
    x = np.ascontiguousarray(L0_attn_hidden_states, dtype=np.float32)
    wu = np.ascontiguousarray(
        np.asarray(L0_ffn_weight_up, dtype=np.float32)
        .reshape(64, 128, 16, 128).transpose(0, 3, 2, 1).reshape(64, 128, 2048))
    wdn = np.ascontiguousarray(
        np.asarray(L0_ffn_weight_down, dtype=np.float32).T.reshape(64, 128, 2048))
    wq_ = _chunkT(np.asarray(L0_attn_Wq, dtype=np.float32))
    wk_ = _chunkT(np.asarray(L0_attn_Wk, dtype=np.float32))
    wv_ = _chunkT(np.asarray(L0_attn_Wv, dtype=np.float32))
    wo_ = _chunkT(np.asarray(L0_attn_Wo, dtype=np.float32))
    nwr = np.ascontiguousarray(
        np.broadcast_to(np.asarray(L0_ffn_norm_weight, dtype=np.float32), (128, E)))
    gamv = np.asarray(L0_ffn_gamma, dtype=np.float32).reshape(1, E)
    con = np.zeros((128, 6), dtype=np.float32)
    con[:, 0] = 1.0
    con[:, 1] = 1e-5
    con[:, 2] = 1e-6
    con[:, 3] = 127.0
    con[:, 4] = 1e-8
    ones128 = np.ones((128, 128), dtype=np.float32)
    idn_f = np.eye(128, dtype=np.float32)
    idn_b = np.eye(128, dtype=np.float32).astype(ml_dtypes.bfloat16)
    jj = np.arange(128)
    tmat = (-(jj[:, None] >= jj[None, :]).astype(np.float32)).astype(ml_dtypes.bfloat16)

    nc = build()
    in_maps = []
    for c in range(8):
        b, half = c // 2, c % 2
        own = x[b, half * 1024:(half + 1) * 1024]
        oth2 = x[b, (1 - half) * 1024:(2 - half) * 1024]
        x_c = np.ascontiguousarray(np.concatenate([own, oth2], axis=0))
        in_maps.append({
            "x_in": x_c, "wu": wu, "wd": wdn, "wq_": wq_, "wk_": wk_,
            "wv_": wv_, "wo_": wo_, "nwr": nwr, "gamv": gamv, "con": con,
            "ones128": ones128, "idn_f": idn_f, "idn_b": idn_b, "tmat": tmat,
            "masks": _masks_for(half),
        })
    res = run_bass_kernel_spmd(nc, in_maps, core_ids=list(range(8)))
    out = np.empty((4, T, E), dtype=np.float32)
    for c in range(8):
        b, half = c // 2, c % 2
        out[b, half * 1024:(half + 1) * 1024] = res.results[c]["y"]
    return out


if __name__ == "__main__":
    import reference as _r
    ins = _r.setup_inputs()
    o = kernel(**{kk: np.asarray(v) for kk, v in ins.items()})
    e = np.asarray(_r.reference(**ins))
    err = np.linalg.norm(o - e) / (np.linalg.norm(e) + 1e-12)
    print("rel l2:", err)


# version 06
# speedup vs baseline: 1.2066x; 1.1510x over previous
import numpy as np
import ml_dtypes
import concourse.bass as bass
import concourse.mybir as mybir
from concourse.bass_utils import run_bass_kernel_spmd
from contextlib import ExitStack

F32 = mybir.dt.float32
BF16 = mybir.dt.bfloat16
I32 = mybir.dt.int32
AF = mybir.ActivationFunctionType
ALU = mybir.AluOpType
AX = mybir.AxisListType

T = 2048
E = 2048
NJC = 64
NT = 16
OWN = 8
NH = 32
JB = 4
R2048 = float(2 ** -11)
RMEAN = float(2 ** -24)

JSEQ = [
    [3, 2, 1, 0, 15, 14, 13, 12, 11, 10, 9, 8],
    [7, 6, 5, 4, 3, 2, 1, 0, 15, 14, 13, 12, 11, 10, 9, 8],
]
NU = len(JSEQ[0]) + len(JSEQ[1])


class K:
    def __init__(self):
        self.q = {"sync": [], "tensor": [], "vector": [], "scalar": [], "gpsimd": []}
        self.cnt = {}
        self.sems = {}

    def inc(self, s, k):
        self.cnt[s] = self.cnt.get(s, 0) + k
        return self.cnt[s]

    def c(self, s):
        return self.cnt.get(s, 0)

    def emit(self, eng, fn):
        self.q[eng].append(fn)

    def wait(self, eng, s, val=None):
        v = self.c(s) if val is None else val
        if v <= 0:
            return
        self.emit(eng, lambda e, h=s, v=v: e.wait_ge(self.sems[h], v))

    def dma(self, eng, out, in_, s):
        self.inc(s, 16)
        self.emit(eng, lambda e, o=out, i=in_, h=s: e.dma_start(out=o, in_=i).then_inc(self.sems[h], 16))

    def op(self, eng, meth, s=None, **kw):
        if s is not None:
            self.inc(s, 1)

        def f(e, m=meth, kw=kw, h=s):
            r = getattr(e, m)(**kw)
            if h is not None:
                r.then_inc(self.sems[h], 1)
        self.emit(eng, f)

    def mm(self, out, lhsT, rhs, start, stop, s=None):
        if s is not None:
            self.inc(s, 1)

        def f(e, o=out, l=lhsT, r=rhs, st=start, sp=stop, h=s):
            x = e.matmul(o, l, r, start=st, stop=sp)
            if h is not None:
                x.then_inc(self.sems[h], 1)
        self.emit("tensor", f)

    def tr(self, out, in_, idn, s=None):
        if s is not None:
            self.inc(s, 1)

        def f(e, o=out, i=in_, d=idn, h=s):
            x = e.transpose(o, i, d)
            if h is not None:
                x.then_inc(self.sems[h], 1)
        self.emit("tensor", f)

    def vnop(self):
        self.inc("sV", 1)
        self.emit("vector", lambda e: e.nop().then_inc(self.sems["sV"], 1))

    def barrier(self):
        sems = [s for s in self.cnt if self.cnt[s] > 0]
        for eng in self.q:
            for s in sems:
                self.wait(eng, s)


def build():
    nc = bass.Bass("TRN2", target_bir_lowering=False, debug=False)
    ap = lambda t: t.ap()
    x_in = ap(nc.dram_tensor("x_in", [T, E], F32, kind="ExternalInput"))
    wu = ap(nc.dram_tensor("wu", [NJC, 128, E], F32, kind="ExternalInput"))
    wd = ap(nc.dram_tensor("wd", [NJC, 128, E], F32, kind="ExternalInput"))
    wq_ = ap(nc.dram_tensor("wq_", [16, 128, E], F32, kind="ExternalInput"))
    wk_ = ap(nc.dram_tensor("wk_", [16, 128, E], F32, kind="ExternalInput"))
    wv_ = ap(nc.dram_tensor("wv_", [16, 128, E], F32, kind="ExternalInput"))
    wo_ = ap(nc.dram_tensor("wo_", [16, 128, E], F32, kind="ExternalInput"))
    nwr = ap(nc.dram_tensor("nwr", [128, E], F32, kind="ExternalInput"))
    gamv = ap(nc.dram_tensor("gamv", [1, E], F32, kind="ExternalInput"))
    con = ap(nc.dram_tensor("con", [128, 6], F32, kind="ExternalInput"))
    ones128 = ap(nc.dram_tensor("ones128", [128, 128], F32, kind="ExternalInput"))
    idn_f = ap(nc.dram_tensor("idn_f", [128, 128], F32, kind="ExternalInput"))
    idn_b = ap(nc.dram_tensor("idn_b", [128, 128], BF16, kind="ExternalInput"))
    tmat = ap(nc.dram_tensor("tmat", [128, 128], BF16, kind="ExternalInput"))
    masks = ap(nc.dram_tensor("masks", [NU, 128, 512], BF16, kind="ExternalInput"))
    y = ap(nc.dram_tensor("y", [OWN * 128, E], F32, kind="ExternalOutput"))
    sc = lambda n, sh, dt: ap(nc.dram_tensor(n, sh, dt, kind="Internal"))
    x1s = sc("x1s", [T, E], F32)
    xns = sc("xns", [T, E], F32)
    x2s = sc("x2s", [OWN * 128, E], F32)
    xn2s = sc("xn2s", [OWN * 128, E], F32)
    xq1 = sc("xq1", [T, E], BF16)
    xq2 = sc("xq2", [OWN * 128, E], BF16)
    wqsu = sc("wqsu", [NJC, 128, E], BF16)
    wqsd = sc("wqsd", [NJC, 128, E], BF16)
    xt1h = sc("xt1h", [16, 128, T], BF16)
    xt1l = sc("xt1l", [16, 128, T], BF16)
    qth = sc("qth", [E, OWN * 128], BF16)
    qtl = sc("qtl", [E, OWN * 128], BF16)
    kth = sc("kth", [E, T], BF16)
    ktl = sc("ktl", [E, T], BF16)
    vth = sc("vth", [E, T], BF16)
    vtl = sc("vtl", [E, T], BF16)
    oth = sc("oth", [E, OWN * 128], BF16)
    otl = sc("otl", [E, OWN * 128], BF16)

    k = K()
    with ExitStack() as ctx:
        sb = lambda n, sh, dt: ctx.enter_context(nc.sbuf_tensor(n, sh, dt))
        psb = lambda n, sh: ctx.enter_context(nc.psum_tensor(n, sh, F32))
        for s in ["sW", "sG", "sM", "sT", "sV", "sS", "sY"]:
            k.sems[s] = ctx.enter_context(nc.semaphore(s))

        A_bf = sb("A_bf", [128, 32768], BF16)
        A_f32 = sb("A_f32", [128, 16384], F32)
        A_wq = sb("A_wq", [128, 12288], BF16)
        A_x = sb("A_x", [128, 4096], F32)
        A_w = sb("A_w", [128, 2048], F32)
        nw = sb("nw", [128, E], F32)
        gprep = sb("gprep", [128, E], F32)
        xqi = sb("xqi", [128, 512], I32)
        conb = sb("conb", [128, 6], F32)
        o128 = sb("o128", [128, 128], F32)
        idf = sb("idf", [128, 128], F32)
        idb = sb("idb", [128, 128], BF16)
        tmb = sb("tmb", [128, 128], BF16)
        stat = sb("stat", [128, 24], F32)
        gcol = sb("gcol", [128, 16], F32)
        gcol2 = sb("gcol2", [128, 16], F32)
        scal = sb("scal", [1, 16], F32)
        srow = sb("srow", [1, 3072], F32)
        carry = srow[0:1, 0:512]
        mbuf = A_bf[:, 12288:13312]
        wsum = stat[:, 20:24]
        r128 = stat[:, 18:20]
        wsumc = lambda i: stat[:, 20 + i:21 + i]
        wsum2 = stat[:, 22:23]

        pA = psb("pA", [128, 512])
        pB = psb("pB", [128, 512])
        pC = psb("pC", [128, 512])
        pD = psb("pD", [128, 512])
        pE = psb("pE", [128, 512])
        pF = psb("pF", [128, 512])
        pG = psb("pG", [128, 512])
        pT = ctx.enter_context(nc.psum_tensor("pT", [128, 1024], BF16))

        ONES = conb[:, 0:1]
        EPS5 = conb[:, 1:2]
        EPS6 = conb[:, 2:3]

        k.dma("sync", conb[:], con, "sW")
        k.dma("sync", o128[:], ones128, "sW")
        k.dma("sync", idf[:], idn_f, "sW")
        k.dma("sync", idb[:], idn_b, "sW")
        k.dma("sync", tmb[:], tmat, "sW")
        k.dma("sync", nw[:], nwr, "sW")
        k.dma("sync", srow[0:1, 1024:1024 + E], gamv, "sW")
        k.emit("gpsimd", lambda e: e.memset(wsum, 0.0).then_inc(k.sems["sG"], 1))
        k.inc("sG", 1)
        wbuf = [A_w[:], A_f32[:, 0:2048], A_f32[:, 2048:4096], A_f32[:, 4096:6144]]
        red_cnt = []
        si = 0
        for wi, wsrc in ((0, wu), (1, wd)):
            for jc in range(NJC):
                if si >= 4:
                    k.wait("sync", "sV", red_cnt[si - 4])
                k.dma("sync", wbuf[si % 4], wsrc[jc], "sW")
                k.wait("vector", "sW")
                if si == 0:
                    k.wait("vector", "sG", 1)
                k.op("vector", "tensor_reduce", s="sV", out=wsum2, in_=wbuf[si % 4],
                     axis=AX.X, op=ALU.add, apply_absolute_value=True)
                k.wait("vector", "sV")
                k.op("vector", "tensor_tensor", s="sV", out=wsumc(wi),
                     in0=wsum2, in1=wsumc(wi), op=ALU.add)
                red_cnt.append(k.c("sV"))
                si += 1

        def emit_prep(src_x, dst_xn, dst_xq, gc, gc_i, stat0):
            xsl = A_x[:, 0:2048]
            nsl = A_x[:, 2048:4096]
            sq = A_f32[:, 2048:4096]
            k.wait("gpsimd", "sV")
            k.dma("gpsimd", xsl, src_x, "sG")
            k.wait("vector", "sG")
            k.wait("vector", "sS")
            k.op("vector", "tensor_reduce", s="sV", out=stat[:, stat0:stat0 + 1],
                 in_=xsl, axis=AX.X, op=ALU.add)
            k.wait("scalar", "sV")
            k.op("scalar", "activation", s="sS", out=stat[:, stat0 + 1:stat0 + 2],
                 in_=stat[:, stat0:stat0 + 1], func=AF.Identity, scale=R2048)
            k.wait("vector", "sS")
            k.op("vector", "tensor_scalar", out=xsl, in0=xsl,
                 scalar1=stat[:, stat0 + 1:stat0 + 2], scalar2=None, op0=ALU.subtract)
            k.op("vector", "tensor_tensor", out=sq, in0=xsl, in1=xsl, op=ALU.mult)
            k.op("vector", "tensor_reduce", s="sV", out=stat[:, stat0 + 2:stat0 + 3],
                 in_=sq, axis=AX.X, op=ALU.add)
            k.wait("scalar", "sV")
            k.op("scalar", "activation", s="sS", out=stat[:, stat0 + 3:stat0 + 4],
                 in_=stat[:, stat0 + 2:stat0 + 3], func=AF.Sqrt, scale=R2048, bias=EPS5)
            k.wait("vector", "sS")
            k.op("vector", "reciprocal", s="sV", out=stat[:, stat0 + 4:stat0 + 5],
                 in_=stat[:, stat0 + 3:stat0 + 4])
            k.wait("vector", "sV")
            k.op("vector", "tensor_scalar", s="sV", out=nsl, in0=xsl,
                 scalar1=stat[:, stat0 + 4:stat0 + 5], scalar2=None, op0=ALU.mult)
            k.wait("gpsimd", "sV")
            k.dma("gpsimd", dst_xn, nsl, "sG")
            k.op("vector", "tensor_tensor", out=sq, in0=nsl, in1=nsl, op=ALU.mult)
            k.op("vector", "tensor_reduce", s="sV", out=stat[:, stat0 + 5:stat0 + 6],
                 in_=sq, axis=AX.X, op=ALU.add)
            k.wait("scalar", "sV")
            k.op("scalar", "activation", s="sS", out=stat[:, stat0 + 6:stat0 + 7],
                 in_=stat[:, stat0 + 5:stat0 + 6], func=AF.Sqrt, scale=R2048, bias=EPS6)
            k.wait("vector", "sS")
            k.op("vector", "reciprocal", s="sV", out=stat[:, stat0 + 7:stat0 + 8],
                 in_=stat[:, stat0 + 6:stat0 + 7])
            k.wait("vector", "sV")
            k.op("vector", "tensor_scalar", out=xsl, in0=nsl,
                 scalar1=stat[:, stat0 + 7:stat0 + 8], scalar2=None, op0=ALU.mult)
            k.op("vector", "tensor_tensor", out=xsl, in0=xsl, in1=nw[:], op=ALU.mult)
            k.op("vector", "tensor_reduce", s="sV", out=stat[:, stat0 + 11:stat0 + 12],
                 in_=xsl, axis=AX.X, op=ALU.max, apply_absolute_value=True)
            k.wait("vector", "sV")
            k.op("vector", "tensor_tensor", s="sV", out=gc[:, gc_i:gc_i + 1],
                 in0=stat[:, stat0 + 11:stat0 + 12], in1=conb[:, 4:5], op=ALU.max)
            k.wait("vector", "sV")
            k.op("vector", "reciprocal", s="sV", out=stat[:, stat0 + 9:stat0 + 10],
                 in_=gc[:, gc_i:gc_i + 1])
            k.wait("scalar", "sV")
            k.op("scalar", "activation", s="sS", out=stat[:, stat0 + 10:stat0 + 11],
                 in_=stat[:, stat0 + 9:stat0 + 10], func=AF.Identity, scale=127.0)
            k.wait("vector", "sS")
            k.op("vector", "tensor_scalar", out=xsl, in0=xsl,
                 scalar1=stat[:, stat0 + 10:stat0 + 11], scalar2=None, op0=ALU.mult)
            stg = A_wq[:, 4096:6144]
            for hcol in range(4):
                cs = slice(hcol * 512, hcol * 512 + 512)
                k.op("vector", "tensor_copy", s="sV", out=xqi[:], in_=xsl[:, cs])
                k.wait("vector", "sV")
                k.op("vector", "tensor_copy", s="sV", out=stg[:, cs], in_=xqi[:])
            k.wait("gpsimd", "sV")
            k.dma("gpsimd", dst_xq, stg, "sG")

        k.wait("scalar", "sW", 7 * 16)
        for tile in range(NT):
            r = slice(tile * 128, tile * 128 + 128)
            emit_prep(x_in[r, :], xns[r, :], xq1[r, :], gcol, tile, 0)

        k.barrier()
        for wi in range(2):
            k.mm(pG[0:1, 0:1], wsumc(wi), ONES, True, True, s="sT")
            k.wait("scalar", "sT")
            k.op("scalar", "activation", s="sS", out=scal[0:1, wi:wi + 1],
                 in_=pG[0:1, 0:1], func=AF.Identity, scale=RMEAN)
            k.wait("vector", "sS")
            k.op("vector", "tensor_tensor", s="sV", out=scal[0:1, 8 + wi:9 + wi],
                 in0=scal[0:1, wi:wi + 1], in1=conb[0:1, 4:5], op=ALU.max)
            k.wait("vector", "sV")
            k.op("vector", "reciprocal", s="sV", out=scal[0:1, 2 + wi:3 + wi],
                 in_=scal[0:1, 8 + wi:9 + wi])
            k.wait("tensor", "sV")
            k.wait("vector", "sV")
            k.mm(pG[:, 0:1], o128[0:1, :], scal[0:1, 2 + wi:3 + wi], True, True, s="sT")
            k.wait("vector", "sT")
            k.op("vector", "tensor_copy", s="sV", out=stat[:, 18 + wi:19 + wi], in_=pG[:, 0:1])
            k.wait("tensor", "sV")
        k.op("vector", "reciprocal", s="sV", out=scal[0:1, 4:5], in_=conb[0:1, 3:4])
        k.wait("vector", "sV")
        k.op("vector", "tensor_tensor", s="sV", out=scal[0:1, 5:6],
             in0=scal[0:1, 8:9], in1=scal[0:1, 4:5], op=ALU.mult)
        k.wait("scalar", "sV")
        k.op("scalar", "activation", s="sS", out=scal[0:1, 6:7], in_=scal[0:1, 9:10],
             func=AF.Identity, scale=0.5)
        k.wait("vector", "sS")
        k.op("vector", "tensor_scalar", s="sV", out=srow[0:1, 1024:1024 + E],
             in0=srow[0:1, 1024:1024 + E], scalar1=scal[0:1, 6:7], scalar2=None, op0=ALU.mult)
        k.wait("tensor", "sV")
        for g4 in range(4):
            k.mm(pG[:, 0:512], o128[0:1, :], srow[0:1, 1024 + g4 * 512:1536 + g4 * 512],
                 True, True, s="sT")
            k.wait("vector", "sT")
            k.op("vector", "tensor_copy", s="sV", out=gprep[:, g4 * 512:g4 * 512 + 512],
                 in_=pG[:, 0:512])
            k.wait("tensor", "sV")
        k.barrier()

        def emit_ffn(ts0, xq_src, x_src, xn_src, gc, first_w, out_kind):
            n_tiles = 8
            ntok = 1024
            ngrp = 2
            xqT = lambda ec: A_bf[:, ec * 1024: ec * 1024 + ntok]
            hhi = lambda j4: A_bf[:, 16384 + j4 * 1024: 16384 + j4 * 1024 + ntok]
            hlo = lambda j4: A_bf[:, 20480 + j4 * 1024: 20480 + j4 * 1024 + ntok]
            acc = lambda sub: A_f32[:, sub * 2048: sub * 2048 + 2048]
            srep = A_x[:, 0:1024]
            hs = A_x[:, 1024:1536]
            hsl = A_x[:, 1536:2048]
            wqu = lambda par: A_wq[:, 8192 + par * 2048: 8192 + par * 2048 + 2048]
            wqd = lambda j4: A_wq[:, j4 * 2048: j4 * 2048 + 2048]
            stg = A_bf[:, 24576:24576 + 2048]

            for i in range(n_tiles):
                r = slice((ts0 + i) * 128, (ts0 + i) * 128 + 128)
                k.wait("gpsimd", "sT")
                k.dma("gpsimd", stg, xq_src[r, :], "sG")
                k.wait("tensor", "sG")
                for ecb in range(0, 16, 4):
                    for ec in range(ecb, ecb + 4):
                        k.tr(pT[:, (ec % 4) * 128:(ec % 4) * 128 + 128],
                             stg[:, ec * 128:ec * 128 + 128], idb[:],
                             s=("sT" if ec == ecb + 3 else None))
                    k.wait("vector", "sT")
                    for ec in range(ecb, ecb + 4):
                        k.op("vector", "tensor_copy", s=("sV" if ec == ecb + 3 else None),
                             out=xqT(ec)[:, i * 128:i * 128 + 128],
                             in_=pT[:, (ec % 4) * 128:(ec % 4) * 128 + 128])
                    k.wait("tensor", "sV")
            for i in range(n_tiles):
                k.mm(pG[0:1, (i % 4) * 128:(i % 4) * 128 + 128],
                     gc[:, ts0 + i:ts0 + i + 1], idf[:], True, True,
                     s=("sT" if i % 4 == 3 else None))
                if i % 4 == 3:
                    k.wait("vector", "sT")
                    base = (i // 4) * 512
                    k.op("vector", "tensor_scalar", s="sV", out=srow[0:1, base:base + 512],
                         in0=pG[0:1, 0:512], scalar1=scal[0:1, 5:6], scalar2=None,
                         op0=ALU.mult)
                    k.wait("tensor", "sV")
            for g in range(ngrp):
                k.mm(pG[:, 0:512], o128[0:1, :], srow[0:1, g * 512:g * 512 + 512],
                     True, True, s="sT")
                k.wait("vector", "sT")
                k.op("vector", "tensor_copy", s="sV", out=srep[:, g * 512:g * 512 + 512],
                     in_=pG[:, 0:512])
                k.wait("tensor", "sV")

            split_done = {-1: k.c("sV")}
            dn_evac = {0: 0, 1: 0}
            pu_free = {0: 0, 1: 0}
            wqv = {}
            upT = {}
            dnT_prev = 0
            for j4base in range(0, NJC, JB):
                if not first_w:
                    k.wait("sync", "sT", dnT_prev)
                    for jj2 in range(JB):
                        k.dma("sync", wqd(jj2)[:], wqsd[j4base + jj2], "sW")
                for jj in range(JB):
                    jc = j4base + jj
                    par = jc % 2
                    if first_w:
                        k.wait("sync", "sV", wqv.get(jc - 1, 0))
                        k.dma("sync", A_w[:], wu[jc], "sW")
                        k.wait("vector", "sW")
                        k.wait("vector", "sG")
                        k.op("vector", "tensor_scalar", out=A_w[:], in0=A_w[:],
                             scalar1=stat[:, 18:19], scalar2=None, op0=ALU.mult)
                        k.op("vector", "tensor_scalar", out=A_w[:], in0=A_w[:],
                             scalar1=1.0, scalar2=-1.0, op0=ALU.min, op1=ALU.max)
                        for hcol in range(4):
                            cs = slice(hcol * 512, hcol * 512 + 512)
                            k.op("vector", "tensor_copy", s="sV", out=xqi[:], in_=A_w[:, cs])
                            k.wait("vector", "sV")
                            k.op("vector", "tensor_copy", s="sV", out=wqu(par)[:, cs], in_=xqi[:])
                        k.wait("gpsimd", "sV")
                        k.dma("gpsimd", wqsu[jc], wqu(par), "sG")
                        k.wait("sync", "sV")
                        k.dma("sync", A_w[:], wd[jc], "sW")
                        k.wait("vector", "sW")
                        k.wait("vector", "sG")
                        k.op("vector", "tensor_scalar", out=A_w[:], in0=A_w[:],
                             scalar1=stat[:, 19:20], scalar2=None, op0=ALU.mult)
                        k.op("vector", "tensor_scalar", out=A_w[:], in0=A_w[:],
                             scalar1=1.0, scalar2=-1.0, op0=ALU.min, op1=ALU.max)
                        for hcol in range(4):
                            cs = slice(hcol * 512, hcol * 512 + 512)
                            k.op("vector", "tensor_copy", s="sV", out=xqi[:], in_=A_w[:, cs])
                            k.wait("vector", "sV")
                            k.op("vector", "tensor_copy", s="sV", out=wqd(jj)[:, cs], in_=xqi[:])
                        k.wait("gpsimd", "sV")
                        k.dma("gpsimd", wqsd[jc], wqd(jj), "sG")
                        wqv[jc] = k.c("sV")
                        wq_w = None
                    else:
                        k.wait("sync", "sT", upT.get(jc - 2, 0))
                        k.dma("sync", wqu(par)[:], wqsu[jc], "sW")
                        wqv[jc] = 0
                        wq_w = k.c("sW")
                    for g in range(ngrp):
                        gpar = (jc * ngrp + g) % 2
                        pU = pA if gpar == 0 else pB
                        if wq_w is not None:
                            k.wait("tensor", "sW", wq_w)
                        k.wait("tensor", "sV", max(wqv[jc], pu_free[gpar]))
                        for ec in range(16):
                            k.mm(pU[:, 0:512], wqu(par)[:, ec * 128:ec * 128 + 128],
                                 xqT(ec)[:, g * 512:g * 512 + 512],
                                 ec == 0, ec == 15, s=("sT" if ec == 15 else None))
                        if g == ngrp - 1:
                            upT[jc] = k.c("sT")
                        k.wait("vector", "sT")
                        k.op("vector", "tensor_tensor", s="sV", out=hs, in0=pU[:, 0:512],
                             in1=srep[:, g * 512:g * 512 + 512], op=ALU.mult)
                        pu_free[gpar] = k.c("sV")
                        k.wait("scalar", "sV")
                        k.op("scalar", "activation", s="sS", out=hsl, in_=hs, func=AF.Silu)
                        k.wait("vector", "sS")
                        k.op("vector", "tensor_copy",
                             out=hhi(jj)[:, g * 512:g * 512 + 512], in_=hsl)
                        k.op("vector", "scalar_tensor_tensor", s="sV",
                             out=hlo(jj)[:, g * 512:g * 512 + 512], in0=hsl, scalar=1.0,
                             in1=hhi(jj)[:, g * 512:g * 512 + 512],
                             op0=ALU.mult, op1=ALU.subtract)
                    split_done[jc] = k.c("sV")
                for sub in range(8):
                    for dg in range(4):
                        par = (sub * 4 + dg) % 2
                        pDn = pC if par == 0 else pD
                        k.wait("tensor", "sV", max(split_done[j4base + JB - 1], dn_evac[par]))
                        for jj in range(JB):
                            for hl in range(2):
                                hbuf = hhi if hl == 0 else hlo
                                k.mm(pDn[:, 0:512],
                                     hbuf(jj)[:, sub * 128:sub * 128 + 128],
                                     wqd(jj)[:, dg * 512:dg * 512 + 512],
                                     jj == 0 and hl == 0, jj == JB - 1 and hl == 1,
                                     s=("sT" if (jj == JB - 1 and hl == 1) else None))
                        k.wait("vector", "sT")
                        if j4base == 0:
                            k.op("vector", "tensor_copy", s="sV",
                                 out=acc(sub)[:, dg * 512:dg * 512 + 512], in_=pDn[:, 0:512])
                        else:
                            k.op("vector", "tensor_tensor", s="sV",
                                 out=acc(sub)[:, dg * 512:dg * 512 + 512],
                                 in0=pDn[:, 0:512],
                                 in1=acc(sub)[:, dg * 512:dg * 512 + 512], op=ALU.add)
                        dn_evac[par] = k.c("sV")
                dnT_prev = k.c("sT")

            k.barrier()
            for i in range(n_tiles):
                tile = ts0 + i
                r = slice(tile * 128, tile * 128 + 128)
                xsl = A_x[:, 0:2048]
                nsl = A_x[:, 2048:4096]
                if out_kind == "y":
                    k.wait("gpsimd", "sY")
                k.wait("gpsimd", "sV")
                k.wait("gpsimd", "sT")
                k.dma("gpsimd", xsl, x_src[r, :], "sG")
                k.dma("gpsimd", nsl, xn_src[r, :], "sG")
                k.wait("vector", "sG")
                k.op("vector", "tensor_tensor", out=acc(i), in0=acc(i), in1=gprep[:],
                     op=ALU.mult)
                k.op("vector", "scalar_tensor_tensor", out=nsl, in0=nsl, scalar=0.5,
                     in1=xsl, op0=ALU.mult, op1=ALU.add)
                k.op("vector", "tensor_tensor", s="sV", out=xsl, in0=acc(i), in1=nsl,
                     op=ALU.add)
                if out_kind == "y":
                    k.wait("sync", "sV")
                    k.dma("sync", y[i * 128:i * 128 + 128, :], xsl, "sY")
                    continue
                k.wait("gpsimd", "sV")
                k.dma("gpsimd", x1s[r, :], xsl, "sG")
                k.wait("vector", "sG")
                sq = acc(i)
                k.op("vector", "tensor_reduce", s="sV", out=stat[:, 12:13], in_=xsl,
                     axis=AX.X, op=ALU.add)
                k.wait("scalar", "sV")
                k.op("scalar", "activation", s="sS", out=stat[:, 13:14],
                     in_=stat[:, 12:13], func=AF.Identity, scale=R2048)
                k.wait("vector", "sS")
                k.op("vector", "tensor_scalar", out=xsl, in0=xsl,
                     scalar1=stat[:, 13:14], scalar2=None, op0=ALU.subtract)
                k.op("vector", "tensor_tensor", out=sq, in0=xsl, in1=xsl, op=ALU.mult)
                k.op("vector", "tensor_reduce", s="sV", out=stat[:, 14:15], in_=sq,
                     axis=AX.X, op=ALU.add)
                k.wait("scalar", "sV")
                k.op("scalar", "activation", s="sS", out=stat[:, 15:16],
                     in_=stat[:, 14:15], func=AF.Sqrt, scale=R2048, bias=EPS5)
                k.wait("vector", "sS")
                k.op("vector", "reciprocal", s="sV", out=stat[:, 16:17],
                     in_=stat[:, 15:16])
                k.wait("vector", "sV")
                k.op("vector", "tensor_scalar", s="sV", out=xsl, in0=xsl,
                     scalar1=stat[:, 16:17], scalar2=None, op0=ALU.mult)
                k.wait("tensor", "sV")
                sth = A_bf[:, 24576:24576 + 2048]
                stl = A_bf[:, 26624:26624 + 2048]
                for ecb in range(0, 16, 4):
                    for ec in range(ecb, ecb + 4):
                        k.tr(pA[:, (ec % 4) * 128:(ec % 4) * 128 + 128],
                             xsl[:, ec * 128:ec * 128 + 128], idf[:],
                             s=("sT" if ec == ecb + 3 else None))
                    k.wait("vector", "sT")
                    for ec in range(ecb, ecb + 4):
                        psl = pA[:, (ec % 4) * 128:(ec % 4) * 128 + 128]
                        k.op("vector", "tensor_copy",
                             out=sth[:, ec * 128:ec * 128 + 128], in_=psl)
                        k.op("vector", "scalar_tensor_tensor",
                             s=("sV" if ec == ecb + 3 else None),
                             out=stl[:, ec * 128:ec * 128 + 128], in0=psl, scalar=1.0,
                             in1=sth[:, ec * 128:ec * 128 + 128],
                             op0=ALU.mult, op1=ALU.subtract)
                    k.wait("tensor", "sV")
                k.wait("gpsimd", "sV")
                for ec in range(16):
                    k.dma("gpsimd", xt1h[ec, :, tile * 128:tile * 128 + 128],
                          sth[:, ec * 128:ec * 128 + 128], "sG")
                    k.dma("gpsimd", xt1l[ec, :, tile * 128:tile * 128 + 128],
                          stl[:, ec * 128:ec * 128 + 128], "sG")
            k.barrier()

        emit_ffn(0, xq1, x_in, xns, gcol, True, "x1")
        emit_ffn(8, xq1, x_in, xns, gcol, False, "x1")

        def emit_proj(mats, half):
            hof = half * 1024
            xh = lambda ec: A_bf[:, ec * 1024: ec * 1024 + 1024]
            xl = lambda ec: A_bf[:, 16384 + ec * 1024: 16384 + ec * 1024 + 1024]
            for ec in range(16):
                k.dma("sync", xh(ec), xt1h[ec, :, hof:hof + 1024], "sW")
                k.dma("sync", xl(ec), xt1l[ec, :, hof:hof + 1024], "sW")
            sth = A_wq[:, 8192:8192 + 512]
            stl = A_wq[:, 8704:8704 + 512]
            for (wsrc, dsth, dstl, ownrel) in mats:
                for dc in range(16):
                    par = dc % 2
                    whi = A_wq[:, par * 4096:par * 4096 + 2048]
                    wlo = A_wq[:, par * 4096 + 2048:par * 4096 + 4096]
                    k.wait("sync", "sT")
                    k.dma("sync", A_w[:], wsrc[dc], "sW")
                    k.wait("vector", "sW")
                    k.op("vector", "tensor_copy", out=whi[:], in_=A_w[:])
                    k.op("vector", "scalar_tensor_tensor", s="sV", out=wlo[:], in0=A_w[:],
                         scalar=1.0, in1=whi[:], op0=ALU.mult, op1=ALU.subtract)
                    for g in range(2):
                        pP = pA if g == 0 else pB
                        k.wait("tensor", "sW")
                        k.wait("tensor", "sV")
                        for ec in range(16):
                            xhs = xh(ec)[:, g * 512:g * 512 + 512]
                            xls = xl(ec)[:, g * 512:g * 512 + 512]
                            wh = whi[:, ec * 128:ec * 128 + 128]
                            wl = wlo[:, ec * 128:ec * 128 + 128]
                            k.mm(pP[:, 0:512], wh, xhs, ec == 0, False)
                            k.mm(pP[:, 0:512], wl, xhs, False, False)
                            k.mm(pP[:, 0:512], wh, xls, False, ec == 15,
                                 s=("sT" if ec == 15 else None))
                        k.wait("vector", "sT")
                        k.wait("vector", "sG")
                        qscale = 0.125 if ownrel else 1.0
                        k.op("vector", "tensor_scalar", out=sth[:], in0=pP[:, 0:512],
                             scalar1=qscale, scalar2=None, op0=ALU.mult)
                        k.op("vector", "scalar_tensor_tensor", s="sV", out=stl[:],
                             in0=pP[:, 0:512], scalar=qscale, in1=sth[:],
                             op0=ALU.mult, op1=ALU.subtract)
                        k.wait("gpsimd", "sV")
                        rr = slice(dc * 128, dc * 128 + 128)
                        if ownrel:
                            cc = slice(g * 512, g * 512 + 512)
                        else:
                            cc = slice(hof + g * 512, hof + g * 512 + 512)
                        k.dma("gpsimd", dsth[rr, cc], sth[:], "sG")
                        k.dma("gpsimd", dstl[rr, cc], stl[:], "sG")
            k.barrier()

        emit_proj([(wk_, kth, ktl, False), (wv_, vth, vtl, False),
                   (wq_, qth, qtl, True)], 0)
        emit_proj([(wk_, kth, ktl, False), (wv_, vth, vtl, False)], 1)

        kb_h = A_bf[0:64, 0:2048]
        kb_l = A_bf[0:64, 2048:4096]
        vtb = A_bf[:, 4096:6144]
        vb_h = A_bf[:, 6144:7168]
        vb_l = A_bf[:, 7168:8192]
        qb_h = A_bf[0:64, 8192:8704]
        qb_l = A_bf[0:64, 8704:9216]
        p_hi = lambda u: A_bf[:, 13312 + (u % 2) * 512:13824 + (u % 2) * 512]
        p_lo = lambda u: A_bf[:, 14336 + (u % 2) * 512:14848 + (u % 2) * 512]
        e_hi = lambda u: A_bf[:, 15360 + (u % 2) * 512:15872 + (u % 2) * 512]
        e_lo = lambda u: A_bf[:, 16384 + (u % 2) * 512:16896 + (u % 2) * 512]
        osth = A_bf[0:64, 11264:11776]
        ostl = A_bf[0:64, 11776:12288]
        t_f = lambda u: A_f32[:, (u % 2) * 2048:(u % 2) * 2048 + 512]
        p_f = lambda u: A_f32[:, (u % 2) * 2048 + 512:(u % 2) * 2048 + 1024]
        eS_f = lambda u: A_f32[:, (u % 2) * 2048 + 1024:(u % 2) * 2048 + 1536]
        e_f = lambda u: A_f32[:, (u % 2) * 2048 + 1536:(u % 2) * 2048 + 2048]

        for h in range(NH):
            hr = slice(h * 64, h * 64 + 64)
            k.wait("sync", "sT")
            k.wait("sync", "sV")
            k.dma("sync", kb_h, kth[hr, :], "sW")
            k.dma("sync", kb_l, ktl[hr, :], "sW")
            k.dma("sync", vtb[0:64, :], vth[hr, :], "sW")
            k.dma("sync", vtb[64:128, :], vtl[hr, :], "sW")
            k.wait("tensor", "sW")
            for jtb in range(0, 16, 4):
                for jt in range(jtb, jtb + 4):
                    k.tr(pT[:, (jt % 4) * 128:(jt % 4) * 128 + 128],
                         vtb[:, jt * 128:jt * 128 + 128], idb[:],
                         s=("sT" if jt == jtb + 3 else None))
                k.wait("vector", "sT")
                for jt in range(jtb, jtb + 4):
                    psl = pT[:, (jt % 4) * 128:(jt % 4) * 128 + 128]
                    k.op("vector", "tensor_copy", out=vb_h[:, jt * 64:jt * 64 + 64],
                         in_=psl[:, 0:64])
                    k.op("vector", "tensor_copy", s=("sV" if jt == jtb + 3 else None),
                         out=vb_l[:, jt * 64:jt * 64 + 64], in_=psl[:, 64:128])
                k.wait("tensor", "sV")
            for s2 in range(2):
                seq = JSEQ[s2]
                nu = len(seq)
                pO = pE if s2 == 0 else pF
                k.wait("sync", "sT")
                k.dma("sync", qb_h, qth[hr, s2 * 512:s2 * 512 + 512], "sW")
                k.dma("sync", qb_l, qtl[hr, s2 * 512:s2 * 512 + 512], "sW")
                k.inc("sS", 1)
                k.emit("scalar", lambda e: e.memzero(carry).then_inc(k.sems["sS"], 1))
                qld = k.c("sW")
                base_ts = k.c("sT")
                base_ss = k.c("sS")
                base_vs = k.c("sV")
                base_sm = k.c("sM")
                mbase = 0 if s2 == 0 else len(JSEQ[0])
                for pf in range(2):
                    k.dma("scalar", mbuf[:, (pf % 2) * 512:(pf % 2) * 512 + 512],
                          masks[mbase + pf], "sM")
                pZb = [pA, pC, pG]
                tz, tsx, tav = {}, {}, {}
                ss_exp = {}

                def em_z(v):
                    sl2 = seq[v]
                    ks2 = slice(sl2 * 128, sl2 * 128 + 128)
                    if v == 0:
                        k.wait("tensor", "sW", qld)
                    k.wait("tensor", "sS", ss_exp.get(v - 3, base_ss))
                    pZv = pZb[v % 3]
                    k.mm(pZv[:, 0:512], kb_h[:, ks2], qb_h[:], True, False)
                    k.mm(pZv[:, 0:512], kb_l[:, ks2], qb_h[:], False, False)
                    k.mm(pZv[:, 0:512], kb_h[:, ks2], qb_l[:], False, True, s="sT")
                    tz[v] = k.c("sT")

                def em_av(v):
                    sl2 = seq[v]
                    vsl = slice(sl2 * 64, sl2 * 64 + 64)
                    k.wait("tensor", "sV", base_vs + 3 * v + 3)
                    k.mm(pO[0:64, 0:512], vb_h[:, vsl], e_hi(v), v == 0, False)
                    k.mm(pO[0:64, 0:512], vb_h[:, vsl], e_lo(v), False, False)
                    k.mm(pO[0:64, 0:512], vb_l[:, vsl], e_hi(v), False, v == nu - 1,
                         s="sT")
                    tav[v] = k.c("sT")

                em_z(0)
                if nu > 1:
                    em_z(1)
                for ui, sl in enumerate(seq):
                    mpar = ui % 2
                    pS = pB if ui % 2 == 0 else pD
                    k.wait("tensor", "sV", base_vs + 3 * ui + 2)
                    k.wait("tensor", "sS", base_ss + 4 * ui - 1 if ui > 0 else base_ss)
                    k.mm(pS[:, 0:512], tmb[:], p_hi(ui), True, False)
                    k.mm(pS[:, 0:512], tmb[:], p_lo(ui), False, False)
                    k.mm(pS[:, 0:512], o128[0:1, :], carry, False, True, s="sT")
                    tsx[ui] = k.c("sT")
                    if ui >= 1:
                        em_av(ui - 1)
                    if ui + 2 < nu:
                        em_z(ui + 2)
                    k.wait("scalar", "sT", tz[ui])
                    if ui >= 2:
                        k.wait("scalar", "sV", base_vs + 3 * (ui - 2) + 3)
                    k.op("scalar", "activation", s="sS", out=t_f(ui),
                         in_=pZb[ui % 3][:, 0:512], func=AF.Exp)
                    ss_exp[ui] = k.c("sS")
                    k.wait("scalar", "sV", base_vs + 3 * ui + 1)
                    k.op("scalar", "activation", s="sS", out=p_f(ui), in_=t_f(ui),
                         func=AF.Ln, bias=ONES)
                    if ui + 2 < nu:
                        k.dma("scalar", mbuf[:, ((ui + 2) % 2) * 512:((ui + 2) % 2) * 512 + 512],
                              masks[mbase + ui + 2], "sM")
                    k.wait("scalar", "sT", tsx[ui])
                    k.op("scalar", "copy", s="sS", out=carry, in_=pS[0:1, 0:512])
                    k.op("scalar", "activation", s="sS", out=eS_f(ui), in_=pS[:, 0:512],
                         func=AF.Exp)
                    k.wait("vector", "sS", base_ss + 4 * ui + 1)
                    k.wait("vector", "sM", base_sm + 16 * (ui + 1))
                    k.op("vector", "tensor_tensor", s="sV", out=t_f(ui), in0=t_f(ui),
                         in1=mbuf[:, mpar * 512:mpar * 512 + 512], op=ALU.mult)
                    k.wait("vector", "sS", base_ss + 4 * ui + 2)
                    if ui >= 2:
                        k.wait("vector", "sT", tsx[ui - 2])
                    k.op("vector", "tensor_copy", out=p_hi(ui), in_=p_f(ui))
                    k.op("vector", "scalar_tensor_tensor", s="sV", out=p_lo(ui), in0=p_f(ui),
                         scalar=1.0, in1=p_hi(ui), op0=ALU.mult, op1=ALU.subtract)
                    k.wait("vector", "sS", base_ss + 4 * ui + 4)
                    if ui >= 2:
                        k.wait("vector", "sT", tav[ui - 2])
                    k.op("vector", "tensor_tensor", out=e_f(ui), in0=t_f(ui), in1=eS_f(ui),
                         op=ALU.mult)
                    k.op("vector", "tensor_copy", out=e_hi(ui), in_=e_f(ui))
                    k.op("vector", "scalar_tensor_tensor", s="sV", out=e_lo(ui), in0=e_f(ui),
                         scalar=1.0, in1=e_hi(ui), op0=ALU.mult, op1=ALU.subtract)
                em_av(nu - 1)
                k.wait("vector", "sT")
                k.wait("vector", "sG")
                k.op("vector", "tensor_copy", out=osth[:], in_=pO[0:64, 0:512])
                k.op("vector", "scalar_tensor_tensor", s="sV", out=ostl[:],
                     in0=pO[0:64, 0:512], scalar=1.0, in1=osth[:],
                     op0=ALU.mult, op1=ALU.subtract)
                k.wait("gpsimd", "sV")
                k.dma("gpsimd", oth[hr, s2 * 512:s2 * 512 + 512], osth[:], "sG")
                k.dma("gpsimd", otl[hr, s2 * 512:s2 * 512 + 512], ostl[:], "sG")
        k.barrier()

        oh = lambda hc: A_bf[:, hc * 1024: hc * 1024 + 1024]
        ol = lambda hc: A_bf[:, 16384 + hc * 1024: 16384 + hc * 1024 + 1024]
        for hc in range(16):
            rr = slice(hc * 128, hc * 128 + 128)
            k.dma("sync", oh(hc), oth[rr, :], "sW")
            k.dma("sync", ol(hc), otl[rr, :], "sW")
        whh = A_wq[:, 0:2048]
        wll = A_wq[:, 2048:4096]
        evb = A_x[:, 0:512]
        for g in range(2):
            for i_l in range(4):
                tile = g * 4 + i_l
                k.wait("gpsimd", "sV")
                k.wait("gpsimd", "sS")
                k.dma("gpsimd", A_f32[:, i_l * 2048:i_l * 2048 + 2048],
                      x1s[tile * 128:tile * 128 + 128, :], "sG")
            for ec in range(16):
                k.wait("sync", "sT")
                k.dma("sync", A_w[:], wo_[ec], "sW")
                k.wait("vector", "sW")
                k.wait("vector", "sT")
                k.op("vector", "tensor_copy", out=whh[:], in_=A_w[:])
                k.op("vector", "scalar_tensor_tensor", s="sV", out=wll[:], in0=A_w[:],
                     scalar=1.0, in1=whh[:], op0=ALU.mult, op1=ALU.subtract)
                k.wait("tensor", "sV")
                k.wait("tensor", "sW")
                for hc in range(16):
                    wh = whh[:, hc * 128:hc * 128 + 128]
                    wl = wll[:, hc * 128:hc * 128 + 128]
                    ohs = oh(hc)[:, g * 512:g * 512 + 512]
                    ols = ol(hc)[:, g * 512:g * 512 + 512]
                    k.mm(pG[:, 0:512], wh, ohs, hc == 0, False)
                    k.mm(pG[:, 0:512], wl, ohs, False, False)
                    k.mm(pG[:, 0:512], wh, ols, False, hc == 15,
                         s=("sT" if hc == 15 else None))
                k.wait("vector", "sT")
                k.op("vector", "tensor_copy", s="sV", out=evb[:], in_=pG[:, 0:512])
                k.wait("tensor", "sV")
                k.wait("tensor", "sG")
                for i_l in range(4):
                    k.tr(pA[:, i_l * 128:i_l * 128 + 128],
                         evb[:, i_l * 128:i_l * 128 + 128], idf[:],
                         s=("sT" if i_l == 3 else None))
                k.wait("vector", "sT")
                for i_l in range(4):
                    dst = A_f32[:, i_l * 2048 + ec * 128:i_l * 2048 + ec * 128 + 128]
                    k.op("vector", "tensor_tensor", s=("sV" if i_l == 3 else None),
                         out=dst, in0=pA[:, i_l * 128:i_l * 128 + 128], in1=dst,
                         op=ALU.add)
                k.wait("tensor", "sV")
            for i_l in range(4):
                tile = g * 4 + i_l
                r = slice(tile * 128, tile * 128 + 128)
                xsl = A_f32[:, i_l * 2048:i_l * 2048 + 2048]
                nsl = A_x[:, 2048:4096]
                sq = A_x[:, 0:2048]
                k.wait("gpsimd", "sV")
                k.dma("gpsimd", x2s[r, :], xsl, "sG")
                k.wait("vector", "sS")
                k.op("vector", "tensor_reduce", s="sV", out=stat[:, 0:1], in_=xsl,
                     axis=AX.X, op=ALU.add)
                k.wait("scalar", "sV")
                k.op("scalar", "activation", s="sS", out=stat[:, 1:2],
                     in_=stat[:, 0:1], func=AF.Identity, scale=R2048)
                k.wait("vector", "sS")
                k.wait("vector", "sG")
                k.op("vector", "tensor_scalar", out=xsl, in0=xsl,
                     scalar1=stat[:, 1:2], scalar2=None, op0=ALU.subtract)
                k.op("vector", "tensor_tensor", out=sq, in0=xsl, in1=xsl, op=ALU.mult)
                k.op("vector", "tensor_reduce", s="sV", out=stat[:, 2:3], in_=sq,
                     axis=AX.X, op=ALU.add)
                k.wait("scalar", "sV")
                k.op("scalar", "activation", s="sS", out=stat[:, 3:4], in_=stat[:, 2:3],
                     func=AF.Sqrt, scale=R2048, bias=EPS5)
                k.wait("vector", "sS")
                k.op("vector", "reciprocal", s="sV", out=stat[:, 4:5], in_=stat[:, 3:4])
                k.wait("vector", "sV")
                k.op("vector", "tensor_scalar", s="sV", out=nsl, in0=xsl,
                     scalar1=stat[:, 4:5], scalar2=None, op0=ALU.mult)
                k.wait("gpsimd", "sV")
                k.dma("gpsimd", xn2s[r, :], nsl, "sG")
                k.op("vector", "tensor_tensor", out=sq, in0=nsl, in1=nsl, op=ALU.mult)
                k.op("vector", "tensor_reduce", s="sV", out=stat[:, 5:6], in_=sq,
                     axis=AX.X, op=ALU.add)
                k.wait("scalar", "sV")
                k.op("scalar", "activation", s="sS", out=stat[:, 6:7], in_=stat[:, 5:6],
                     func=AF.Sqrt, scale=R2048, bias=EPS6)
                k.wait("vector", "sS")
                k.op("vector", "reciprocal", s="sV", out=stat[:, 7:8], in_=stat[:, 6:7])
                k.wait("vector", "sV")
                k.op("vector", "tensor_scalar", out=xsl, in0=nsl,
                     scalar1=stat[:, 7:8], scalar2=None, op0=ALU.mult)
                k.op("vector", "tensor_tensor", out=xsl, in0=xsl, in1=nw[:], op=ALU.mult)
                k.op("vector", "tensor_reduce", s="sV", out=stat[:, 11:12], in_=xsl,
                     axis=AX.X, op=ALU.max, apply_absolute_value=True)
                k.wait("vector", "sV")
                k.op("vector", "tensor_tensor", s="sV", out=gcol2[:, tile:tile + 1],
                     in0=stat[:, 11:12], in1=conb[:, 4:5], op=ALU.max)
                k.wait("vector", "sV")
                k.op("vector", "reciprocal", s="sV", out=stat[:, 9:10],
                     in_=gcol2[:, tile:tile + 1])
                k.wait("scalar", "sV")
                k.op("scalar", "activation", s="sS", out=stat[:, 10:11],
                     in_=stat[:, 9:10], func=AF.Identity, scale=127.0)
                k.wait("vector", "sS")
                k.op("vector", "tensor_scalar", out=xsl, in0=xsl,
                     scalar1=stat[:, 10:11], scalar2=None, op0=ALU.mult)
                stg2 = A_wq[:, 4096:6144]
                for hcol in range(4):
                    cs = slice(hcol * 512, hcol * 512 + 512)
                    k.op("vector", "tensor_copy", s="sV", out=xqi[:], in_=xsl[:, cs])
                    k.wait("vector", "sV")
                    k.op("vector", "tensor_copy", s="sV", out=stg2[:, cs], in_=xqi[:])
                k.wait("gpsimd", "sV")
                k.dma("gpsimd", xq2[r, :], stg2, "sG")
        k.barrier()

        emit_ffn(0, xq2, x2s, xn2s, gcol2, False, "y")

        k.wait("gpsimd", "sY")
        k.wait("gpsimd", "sG")
        k.wait("gpsimd", "sW")
        k.wait("gpsimd", "sM")

        block = ctx.enter_context(nc.Block())

        @block.sync
        def _(e):
            for f in k.q["sync"]:
                f(e)

        @block.tensor
        def _(e):
            for f in k.q["tensor"]:
                f(e)

        @block.vector
        def _(e):
            for f in k.q["vector"]:
                f(e)

        @block.scalar
        def _(e):
            for f in k.q["scalar"]:
                f(e)

        @block.gpsimd
        def _(e):
            for f in k.q["gpsimd"]:
                f(e)

    return nc


def _chunkT(w):
    return np.ascontiguousarray(
        w.reshape(16, 128, 16, 128).transpose(2, 1, 0, 3).reshape(16, 128, 2048))


def _masks_for(half):
    ms = np.zeros((NU, 128, 512), dtype=ml_dtypes.bfloat16)
    mi = 0
    for s2 in range(2):
        for sl in JSEQ[s2]:
            gi = half * 1024 + s2 * 512 + np.arange(512)[None, :]
            if sl < 8:
                gj = half * 1024 + sl * 128 + np.arange(128)[:, None]
            else:
                gj = (1 - half) * 1024 + (sl - 8) * 128 + np.arange(128)[:, None]
            ms[mi] = (gj < gi).astype(np.float32)
            mi += 1
    return ms


def kernel(L0_attn_hidden_states, L0_attn_Wq, L0_attn_Wk, L0_attn_Wv, L0_attn_Wo,
           L0_ffn_weight_up, L0_ffn_weight_down, L0_ffn_norm_weight, L0_ffn_gamma):
    x = np.ascontiguousarray(L0_attn_hidden_states, dtype=np.float32)
    wu = np.ascontiguousarray(
        np.asarray(L0_ffn_weight_up, dtype=np.float32)
        .reshape(64, 128, 16, 128).transpose(0, 3, 2, 1).reshape(64, 128, 2048))
    wdn = np.ascontiguousarray(
        np.asarray(L0_ffn_weight_down, dtype=np.float32).T.reshape(64, 128, 2048))
    wq_ = _chunkT(np.asarray(L0_attn_Wq, dtype=np.float32))
    wk_ = _chunkT(np.asarray(L0_attn_Wk, dtype=np.float32))
    wv_ = _chunkT(np.asarray(L0_attn_Wv, dtype=np.float32))
    wo_ = _chunkT(np.asarray(L0_attn_Wo, dtype=np.float32))
    nwr = np.ascontiguousarray(
        np.broadcast_to(np.asarray(L0_ffn_norm_weight, dtype=np.float32), (128, E)))
    gamv = np.asarray(L0_ffn_gamma, dtype=np.float32).reshape(1, E)
    con = np.zeros((128, 6), dtype=np.float32)
    con[:, 0] = 1.0
    con[:, 1] = 1e-5
    con[:, 2] = 1e-6
    con[:, 3] = 127.0
    con[:, 4] = 1e-8
    ones128 = np.ones((128, 128), dtype=np.float32)
    idn_f = np.eye(128, dtype=np.float32)
    idn_b = np.eye(128, dtype=np.float32).astype(ml_dtypes.bfloat16)
    jj = np.arange(128)
    tmat = (-(jj[:, None] >= jj[None, :]).astype(np.float32)).astype(ml_dtypes.bfloat16)

    nc = build()
    in_maps = []
    for c in range(8):
        b, half = c // 2, c % 2
        own = x[b, half * 1024:(half + 1) * 1024]
        oth2 = x[b, (1 - half) * 1024:(2 - half) * 1024]
        x_c = np.ascontiguousarray(np.concatenate([own, oth2], axis=0))
        in_maps.append({
            "x_in": x_c, "wu": wu, "wd": wdn, "wq_": wq_, "wk_": wk_,
            "wv_": wv_, "wo_": wo_, "nwr": nwr, "gamv": gamv, "con": con,
            "ones128": ones128, "idn_f": idn_f, "idn_b": idn_b, "tmat": tmat,
            "masks": _masks_for(half),
        })
    res = run_bass_kernel_spmd(nc, in_maps, core_ids=list(range(8)))
    out = np.empty((4, T, E), dtype=np.float32)
    for c in range(8):
        b, half = c // 2, c % 2
        out[b, half * 1024:(half + 1) * 1024] = res.results[c]["y"]
    return out


if __name__ == "__main__":
    import reference as _r
    ins = _r.setup_inputs()
    o = kernel(**{kk: np.asarray(v) for kk, v in ins.items()})
    e = np.asarray(_r.reference(**ins))
    err = np.linalg.norm(o - e) / (np.linalg.norm(e) + 1e-12)
    print("rel l2:", err)


# version 07
# speedup vs baseline: 1.2276x; 1.0174x over previous
import numpy as np
import ml_dtypes
import concourse.bass as bass
import concourse.mybir as mybir
from concourse.bass_utils import run_bass_kernel_spmd
from contextlib import ExitStack

F32 = mybir.dt.float32
BF16 = mybir.dt.bfloat16
I32 = mybir.dt.int32
AF = mybir.ActivationFunctionType
ALU = mybir.AluOpType
AX = mybir.AxisListType

T = 2048
E = 2048
NJC = 64
NT = 16
OWN = 8
NH = 32
JB = 4
R2048 = float(2 ** -11)
RMEAN = float(2 ** -24)

JSEQ = [
    [3, 2, 1, 0, 15, 14, 13, 12, 11, 10, 9, 8],
    [7, 6, 5, 4, 3, 2, 1, 0, 15, 14, 13, 12, 11, 10, 9, 8],
]
NU = len(JSEQ[0]) + len(JSEQ[1])


class K:
    def __init__(self):
        self.q = {"sync": [], "tensor": [], "vector": [], "scalar": [], "gpsimd": []}
        self.cnt = {}
        self.sems = {}

    def inc(self, s, k):
        self.cnt[s] = self.cnt.get(s, 0) + k
        return self.cnt[s]

    def c(self, s):
        return self.cnt.get(s, 0)

    def emit(self, eng, fn):
        self.q[eng].append(fn)

    def wait(self, eng, s, val=None):
        v = self.c(s) if val is None else val
        if v <= 0:
            return
        self.emit(eng, lambda e, h=s, v=v: e.wait_ge(self.sems[h], v))

    def dma(self, eng, out, in_, s):
        self.inc(s, 16)
        self.emit(eng, lambda e, o=out, i=in_, h=s: e.dma_start(out=o, in_=i).then_inc(self.sems[h], 16))

    def op(self, eng, meth, s=None, **kw):
        if s is not None:
            self.inc(s, 1)

        def f(e, m=meth, kw=kw, h=s):
            r = getattr(e, m)(**kw)
            if h is not None:
                r.then_inc(self.sems[h], 1)
        self.emit(eng, f)

    def mm(self, out, lhsT, rhs, start, stop, s=None):
        if s is not None:
            self.inc(s, 1)

        def f(e, o=out, l=lhsT, r=rhs, st=start, sp=stop, h=s):
            x = e.matmul(o, l, r, start=st, stop=sp)
            if h is not None:
                x.then_inc(self.sems[h], 1)
        self.emit("tensor", f)

    def tr(self, out, in_, idn, s=None):
        if s is not None:
            self.inc(s, 1)

        def f(e, o=out, i=in_, d=idn, h=s):
            x = e.transpose(o, i, d)
            if h is not None:
                x.then_inc(self.sems[h], 1)
        self.emit("tensor", f)

    def vnop(self):
        self.inc("sV", 1)
        self.emit("vector", lambda e: e.nop().then_inc(self.sems["sV"], 1))

    def barrier(self):
        sems = [s for s in self.cnt if self.cnt[s] > 0]
        for eng in self.q:
            for s in sems:
                self.wait(eng, s)


def build():
    nc = bass.Bass("TRN2", target_bir_lowering=False, debug=False)
    ap = lambda t: t.ap()
    x_in = ap(nc.dram_tensor("x_in", [T, E], F32, kind="ExternalInput"))
    wu = ap(nc.dram_tensor("wu", [NJC, 128, E], F32, kind="ExternalInput"))
    wd = ap(nc.dram_tensor("wd", [NJC, 128, E], F32, kind="ExternalInput"))
    wq_ = ap(nc.dram_tensor("wq_", [16, 128, E], F32, kind="ExternalInput"))
    wk_ = ap(nc.dram_tensor("wk_", [16, 128, E], F32, kind="ExternalInput"))
    wv_ = ap(nc.dram_tensor("wv_", [16, 128, E], F32, kind="ExternalInput"))
    wo_ = ap(nc.dram_tensor("wo_", [16, 128, E], F32, kind="ExternalInput"))
    nwr = ap(nc.dram_tensor("nwr", [128, E], F32, kind="ExternalInput"))
    gamv = ap(nc.dram_tensor("gamv", [1, E], F32, kind="ExternalInput"))
    con = ap(nc.dram_tensor("con", [128, 6], F32, kind="ExternalInput"))
    ones128 = ap(nc.dram_tensor("ones128", [128, 128], F32, kind="ExternalInput"))
    idn_f = ap(nc.dram_tensor("idn_f", [128, 128], F32, kind="ExternalInput"))
    idn_b = ap(nc.dram_tensor("idn_b", [128, 128], BF16, kind="ExternalInput"))
    tmat = ap(nc.dram_tensor("tmat", [128, 128], BF16, kind="ExternalInput"))
    masks = ap(nc.dram_tensor("masks", [NU, 128, 512], BF16, kind="ExternalInput"))
    y = ap(nc.dram_tensor("y", [OWN * 128, E], F32, kind="ExternalOutput"))
    sc = lambda n, sh, dt: ap(nc.dram_tensor(n, sh, dt, kind="Internal"))
    x1s = sc("x1s", [T, E], F32)
    xns = sc("xns", [T, E], F32)
    x2s = sc("x2s", [OWN * 128, E], F32)
    xn2s = sc("xn2s", [OWN * 128, E], F32)
    xq1 = sc("xq1", [T, E], BF16)
    xq2 = sc("xq2", [OWN * 128, E], BF16)
    wqsu = sc("wqsu", [NJC, 128, E], BF16)
    wqsd = sc("wqsd", [NJC, 128, E], BF16)
    xt1h = sc("xt1h", [16, 128, T], BF16)
    xt1l = sc("xt1l", [16, 128, T], BF16)
    qth = sc("qth", [E, OWN * 128], BF16)
    qtl = sc("qtl", [E, OWN * 128], BF16)
    kth = sc("kth", [E, T], BF16)
    ktl = sc("ktl", [E, T], BF16)
    vth = sc("vth", [E, T], BF16)
    vtl = sc("vtl", [E, T], BF16)
    oth = sc("oth", [E, OWN * 128], BF16)
    otl = sc("otl", [E, OWN * 128], BF16)

    k = K()
    with ExitStack() as ctx:
        sb = lambda n, sh, dt: ctx.enter_context(nc.sbuf_tensor(n, sh, dt))
        psb = lambda n, sh: ctx.enter_context(nc.psum_tensor(n, sh, F32))
        for s in ["sW", "sG", "sM", "sT", "sV", "sS", "sY"]:
            k.sems[s] = ctx.enter_context(nc.semaphore(s))

        A_bf = sb("A_bf", [128, 32768], BF16)
        A_f32 = sb("A_f32", [128, 16384], F32)
        A_wq = sb("A_wq", [128, 12288], BF16)
        A_x = sb("A_x", [128, 4096], F32)
        A_w = sb("A_w", [128, 2048], F32)
        nw = sb("nw", [128, E], F32)
        gprep = sb("gprep", [128, E], F32)
        xqi = sb("xqi", [128, 512], I32)
        conb = sb("conb", [128, 6], F32)
        o128 = sb("o128", [128, 128], F32)
        idf = sb("idf", [128, 128], F32)
        idb = sb("idb", [128, 128], BF16)
        tmb = sb("tmb", [128, 128], BF16)
        stat = sb("stat", [128, 24], F32)
        gcol = sb("gcol", [128, 16], F32)
        gcol2 = sb("gcol2", [128, 16], F32)
        scal = sb("scal", [1, 16], F32)
        srow = sb("srow", [1, 3072], F32)
        carry = srow[0:1, 0:512]
        mbuf = A_bf[:, 12288:13312]
        wsum = stat[:, 20:24]
        r128 = stat[:, 18:20]
        wsumc = lambda i: stat[:, 20 + i:21 + i]
        wsum2 = stat[:, 22:23]

        pA = psb("pA", [128, 512])
        pB = psb("pB", [128, 512])
        pC = psb("pC", [128, 512])
        pD = psb("pD", [128, 512])
        pE = psb("pE", [128, 512])
        pF = psb("pF", [128, 512])
        pG = psb("pG", [128, 512])
        pT = ctx.enter_context(nc.psum_tensor("pT", [128, 1024], BF16))

        ONES = conb[:, 0:1]
        EPS5 = conb[:, 1:2]
        EPS6 = conb[:, 2:3]

        k.dma("sync", conb[:], con, "sW")
        k.dma("sync", o128[:], ones128, "sW")
        k.dma("sync", idf[:], idn_f, "sW")
        k.dma("sync", idb[:], idn_b, "sW")
        k.dma("sync", tmb[:], tmat, "sW")
        k.dma("sync", nw[:], nwr, "sW")
        k.dma("sync", srow[0:1, 1024:1024 + E], gamv, "sW")
        k.emit("gpsimd", lambda e: e.memset(wsum, 0.0).then_inc(k.sems["sG"], 1))
        k.inc("sG", 1)
        wbuf = [A_w[:], A_f32[:, 0:2048], A_f32[:, 2048:4096], A_f32[:, 4096:6144]]
        red_cnt = []
        si = 0
        for wi, wsrc in ((0, wu), (1, wd)):
            for jc in range(NJC):
                if si >= 4:
                    k.wait("sync", "sV", red_cnt[si - 4])
                k.dma("sync", wbuf[si % 4], wsrc[jc], "sW")
                k.wait("vector", "sW")
                if si == 0:
                    k.wait("vector", "sG", 1)
                k.op("vector", "tensor_reduce", s="sV", out=wsum2, in_=wbuf[si % 4],
                     axis=AX.X, op=ALU.add, apply_absolute_value=True)
                k.wait("vector", "sV")
                k.op("vector", "tensor_tensor", s="sV", out=wsumc(wi),
                     in0=wsum2, in1=wsumc(wi), op=ALU.add)
                red_cnt.append(k.c("sV"))
                si += 1

        def emit_prep(src_x, dst_xn, dst_xq, gc, gc_i, stat0):
            xsl = A_x[:, 0:2048]
            nsl = A_x[:, 2048:4096]
            sq = A_f32[:, 2048:4096]
            k.wait("gpsimd", "sV")
            k.dma("gpsimd", xsl, src_x, "sG")
            k.wait("vector", "sG")
            k.wait("vector", "sS")
            k.op("vector", "tensor_reduce", s="sV", out=stat[:, stat0:stat0 + 1],
                 in_=xsl, axis=AX.X, op=ALU.add)
            k.wait("scalar", "sV")
            k.op("scalar", "activation", s="sS", out=stat[:, stat0 + 1:stat0 + 2],
                 in_=stat[:, stat0:stat0 + 1], func=AF.Identity, scale=R2048)
            k.wait("vector", "sS")
            k.op("vector", "tensor_scalar", out=xsl, in0=xsl,
                 scalar1=stat[:, stat0 + 1:stat0 + 2], scalar2=None, op0=ALU.subtract)
            k.op("vector", "tensor_tensor", out=sq, in0=xsl, in1=xsl, op=ALU.mult)
            k.op("vector", "tensor_reduce", s="sV", out=stat[:, stat0 + 2:stat0 + 3],
                 in_=sq, axis=AX.X, op=ALU.add)
            k.wait("scalar", "sV")
            k.op("scalar", "activation", s="sS", out=stat[:, stat0 + 3:stat0 + 4],
                 in_=stat[:, stat0 + 2:stat0 + 3], func=AF.Sqrt, scale=R2048, bias=EPS5)
            k.wait("vector", "sS")
            k.op("vector", "reciprocal", s="sV", out=stat[:, stat0 + 4:stat0 + 5],
                 in_=stat[:, stat0 + 3:stat0 + 4])
            k.wait("vector", "sV")
            k.op("vector", "tensor_scalar", s="sV", out=nsl, in0=xsl,
                 scalar1=stat[:, stat0 + 4:stat0 + 5], scalar2=None, op0=ALU.mult)
            k.wait("gpsimd", "sV")
            k.dma("gpsimd", dst_xn, nsl, "sG")
            k.op("vector", "tensor_tensor", out=sq, in0=nsl, in1=nsl, op=ALU.mult)
            k.op("vector", "tensor_reduce", s="sV", out=stat[:, stat0 + 5:stat0 + 6],
                 in_=sq, axis=AX.X, op=ALU.add)
            k.wait("scalar", "sV")
            k.op("scalar", "activation", s="sS", out=stat[:, stat0 + 6:stat0 + 7],
                 in_=stat[:, stat0 + 5:stat0 + 6], func=AF.Sqrt, scale=R2048, bias=EPS6)
            k.wait("vector", "sS")
            k.op("vector", "reciprocal", s="sV", out=stat[:, stat0 + 7:stat0 + 8],
                 in_=stat[:, stat0 + 6:stat0 + 7])
            k.wait("vector", "sV")
            k.op("vector", "tensor_scalar", out=xsl, in0=nsl,
                 scalar1=stat[:, stat0 + 7:stat0 + 8], scalar2=None, op0=ALU.mult)
            k.op("vector", "tensor_tensor", out=xsl, in0=xsl, in1=nw[:], op=ALU.mult)
            k.op("vector", "tensor_reduce", s="sV", out=stat[:, stat0 + 11:stat0 + 12],
                 in_=xsl, axis=AX.X, op=ALU.max, apply_absolute_value=True)
            k.wait("vector", "sV")
            k.op("vector", "tensor_tensor", s="sV", out=gc[:, gc_i:gc_i + 1],
                 in0=stat[:, stat0 + 11:stat0 + 12], in1=conb[:, 4:5], op=ALU.max)
            k.wait("vector", "sV")
            k.op("vector", "reciprocal", s="sV", out=stat[:, stat0 + 9:stat0 + 10],
                 in_=gc[:, gc_i:gc_i + 1])
            k.wait("scalar", "sV")
            k.op("scalar", "activation", s="sS", out=stat[:, stat0 + 10:stat0 + 11],
                 in_=stat[:, stat0 + 9:stat0 + 10], func=AF.Identity, scale=127.0)
            k.wait("vector", "sS")
            k.op("vector", "tensor_scalar", out=xsl, in0=xsl,
                 scalar1=stat[:, stat0 + 10:stat0 + 11], scalar2=None, op0=ALU.mult)
            stg = A_wq[:, 4096:6144]
            for hcol in range(4):
                cs = slice(hcol * 512, hcol * 512 + 512)
                k.op("vector", "tensor_copy", s="sV", out=xqi[:], in_=xsl[:, cs])
                k.wait("vector", "sV")
                k.op("vector", "tensor_copy", s="sV", out=stg[:, cs], in_=xqi[:])
            k.wait("gpsimd", "sV")
            k.dma("gpsimd", dst_xq, stg, "sG")

        k.wait("scalar", "sW", 7 * 16)
        for tile in range(NT):
            r = slice(tile * 128, tile * 128 + 128)
            emit_prep(x_in[r, :], xns[r, :], xq1[r, :], gcol, tile, 0)

        k.barrier()
        for wi in range(2):
            k.mm(pG[0:1, 0:1], wsumc(wi), ONES, True, True, s="sT")
            k.wait("scalar", "sT")
            k.op("scalar", "activation", s="sS", out=scal[0:1, wi:wi + 1],
                 in_=pG[0:1, 0:1], func=AF.Identity, scale=RMEAN)
            k.wait("vector", "sS")
            k.op("vector", "tensor_tensor", s="sV", out=scal[0:1, 8 + wi:9 + wi],
                 in0=scal[0:1, wi:wi + 1], in1=conb[0:1, 4:5], op=ALU.max)
            k.wait("vector", "sV")
            k.op("vector", "reciprocal", s="sV", out=scal[0:1, 2 + wi:3 + wi],
                 in_=scal[0:1, 8 + wi:9 + wi])
            k.wait("tensor", "sV")
            k.wait("vector", "sV")
            k.mm(pG[:, 0:1], o128[0:1, :], scal[0:1, 2 + wi:3 + wi], True, True, s="sT")
            k.wait("vector", "sT")
            k.op("vector", "tensor_copy", s="sV", out=stat[:, 18 + wi:19 + wi], in_=pG[:, 0:1])
            k.wait("tensor", "sV")
        k.op("vector", "reciprocal", s="sV", out=scal[0:1, 4:5], in_=conb[0:1, 3:4])
        k.wait("vector", "sV")
        k.op("vector", "tensor_tensor", s="sV", out=scal[0:1, 5:6],
             in0=scal[0:1, 8:9], in1=scal[0:1, 4:5], op=ALU.mult)
        k.wait("scalar", "sV")
        k.op("scalar", "activation", s="sS", out=scal[0:1, 6:7], in_=scal[0:1, 9:10],
             func=AF.Identity, scale=0.5)
        k.wait("vector", "sS")
        k.op("vector", "tensor_scalar", s="sV", out=srow[0:1, 1024:1024 + E],
             in0=srow[0:1, 1024:1024 + E], scalar1=scal[0:1, 6:7], scalar2=None, op0=ALU.mult)
        k.wait("tensor", "sV")
        for g4 in range(4):
            k.mm(pG[:, 0:512], o128[0:1, :], srow[0:1, 1024 + g4 * 512:1536 + g4 * 512],
                 True, True, s="sT")
            k.wait("vector", "sT")
            k.op("vector", "tensor_copy", s="sV", out=gprep[:, g4 * 512:g4 * 512 + 512],
                 in_=pG[:, 0:512])
            k.wait("tensor", "sV")
        k.barrier()

        def emit_ffn(ts0, xq_src, x_src, xn_src, gc, first_w, out_kind):
            n_tiles = 8
            ntok = 1024
            ngrp = 2
            xqT = lambda ec: A_bf[:, ec * 1024: ec * 1024 + ntok]
            hhi = lambda j4: A_bf[:, 16384 + j4 * 1024: 16384 + j4 * 1024 + ntok]
            hlo = lambda j4: A_bf[:, 20480 + j4 * 1024: 20480 + j4 * 1024 + ntok]
            acc = lambda sub: A_f32[:, sub * 2048: sub * 2048 + 2048]
            srep = A_x[:, 0:1024]
            hs = A_x[:, 1024:1536]
            hsl = A_x[:, 1536:2048]
            wqu = lambda par: A_wq[:, 8192 + par * 2048: 8192 + par * 2048 + 2048]
            wqd = lambda j4: A_wq[:, j4 * 2048: j4 * 2048 + 2048]
            stg = A_bf[:, 24576:24576 + 2048]

            for i in range(n_tiles):
                r = slice((ts0 + i) * 128, (ts0 + i) * 128 + 128)
                k.wait("gpsimd", "sT")
                k.dma("gpsimd", stg, xq_src[r, :], "sG")
                k.wait("tensor", "sG")
                for ecb in range(0, 16, 4):
                    for ec in range(ecb, ecb + 4):
                        k.tr(pT[:, (ec % 4) * 128:(ec % 4) * 128 + 128],
                             stg[:, ec * 128:ec * 128 + 128], idb[:],
                             s=("sT" if ec == ecb + 3 else None))
                    k.wait("vector", "sT")
                    for ec in range(ecb, ecb + 4):
                        k.op("vector", "tensor_copy", s=("sV" if ec == ecb + 3 else None),
                             out=xqT(ec)[:, i * 128:i * 128 + 128],
                             in_=pT[:, (ec % 4) * 128:(ec % 4) * 128 + 128])
                    k.wait("tensor", "sV")
            for i in range(n_tiles):
                k.mm(pG[0:1, (i % 4) * 128:(i % 4) * 128 + 128],
                     gc[:, ts0 + i:ts0 + i + 1], idf[:], True, True,
                     s=("sT" if i % 4 == 3 else None))
                if i % 4 == 3:
                    k.wait("vector", "sT")
                    base = (i // 4) * 512
                    k.op("vector", "tensor_scalar", s="sV", out=srow[0:1, base:base + 512],
                         in0=pG[0:1, 0:512], scalar1=scal[0:1, 5:6], scalar2=None,
                         op0=ALU.mult)
                    k.wait("tensor", "sV")
            for g in range(ngrp):
                k.mm(pG[:, 0:512], o128[0:1, :], srow[0:1, g * 512:g * 512 + 512],
                     True, True, s="sT")
                k.wait("vector", "sT")
                k.op("vector", "tensor_copy", s="sV", out=srep[:, g * 512:g * 512 + 512],
                     in_=pG[:, 0:512])
                k.wait("tensor", "sV")

            split_done = {-1: k.c("sV")}
            dn_evac = {0: 0, 1: 0}
            pu_free = {0: 0, 1: 0}
            wqv = {}
            upT = {}
            dnT_prev = 0
            for j4base in range(0, NJC, JB):
                if not first_w:
                    k.wait("sync", "sT", dnT_prev)
                    for jj2 in range(JB):
                        k.dma("sync", wqd(jj2)[:], wqsd[j4base + jj2], "sW")
                for jj in range(JB):
                    jc = j4base + jj
                    par = jc % 2
                    if first_w:
                        k.wait("sync", "sV", wqv.get(jc - 1, 0))
                        k.dma("sync", A_w[:], wu[jc], "sW")
                        k.wait("vector", "sW")
                        k.wait("vector", "sG")
                        k.op("vector", "tensor_scalar", out=A_w[:], in0=A_w[:],
                             scalar1=stat[:, 18:19], scalar2=None, op0=ALU.mult)
                        k.op("vector", "tensor_scalar", out=A_w[:], in0=A_w[:],
                             scalar1=1.0, scalar2=-1.0, op0=ALU.min, op1=ALU.max)
                        for hcol in range(4):
                            cs = slice(hcol * 512, hcol * 512 + 512)
                            k.op("vector", "tensor_copy", s="sV", out=xqi[:], in_=A_w[:, cs])
                            k.wait("vector", "sV")
                            k.op("vector", "tensor_copy", s="sV", out=wqu(par)[:, cs], in_=xqi[:])
                        k.wait("gpsimd", "sV")
                        k.dma("gpsimd", wqsu[jc], wqu(par), "sG")
                        k.wait("sync", "sV")
                        k.dma("sync", A_w[:], wd[jc], "sW")
                        k.wait("vector", "sW")
                        k.wait("vector", "sG")
                        k.op("vector", "tensor_scalar", out=A_w[:], in0=A_w[:],
                             scalar1=stat[:, 19:20], scalar2=None, op0=ALU.mult)
                        k.op("vector", "tensor_scalar", out=A_w[:], in0=A_w[:],
                             scalar1=1.0, scalar2=-1.0, op0=ALU.min, op1=ALU.max)
                        for hcol in range(4):
                            cs = slice(hcol * 512, hcol * 512 + 512)
                            k.op("vector", "tensor_copy", s="sV", out=xqi[:], in_=A_w[:, cs])
                            k.wait("vector", "sV")
                            k.op("vector", "tensor_copy", s="sV", out=wqd(jj)[:, cs], in_=xqi[:])
                        k.wait("gpsimd", "sV")
                        k.dma("gpsimd", wqsd[jc], wqd(jj), "sG")
                        wqv[jc] = k.c("sV")
                        wq_w = None
                    else:
                        k.wait("sync", "sT", upT.get(jc - 2, 0))
                        k.dma("sync", wqu(par)[:], wqsu[jc], "sW")
                        wqv[jc] = 0
                        wq_w = k.c("sW")
                    for g in range(ngrp):
                        gpar = (jc * ngrp + g) % 2
                        pU = pA if gpar == 0 else pB
                        if wq_w is not None:
                            k.wait("tensor", "sW", wq_w)
                        k.wait("tensor", "sV", max(wqv[jc], pu_free[gpar]))
                        for ec in range(16):
                            k.mm(pU[:, 0:512], wqu(par)[:, ec * 128:ec * 128 + 128],
                                 xqT(ec)[:, g * 512:g * 512 + 512],
                                 ec == 0, ec == 15, s=("sT" if ec == 15 else None))
                        if g == ngrp - 1:
                            upT[jc] = k.c("sT")
                        k.wait("vector", "sT")
                        k.op("vector", "tensor_tensor", s="sV", out=hs, in0=pU[:, 0:512],
                             in1=srep[:, g * 512:g * 512 + 512], op=ALU.mult)
                        pu_free[gpar] = k.c("sV")
                        k.wait("scalar", "sV")
                        k.op("scalar", "activation", s="sS", out=hsl, in_=hs, func=AF.Silu)
                        k.wait("vector", "sS")
                        k.op("vector", "tensor_copy",
                             out=hhi(jj)[:, g * 512:g * 512 + 512], in_=hsl)
                        k.op("vector", "scalar_tensor_tensor", s="sV",
                             out=hlo(jj)[:, g * 512:g * 512 + 512], in0=hsl, scalar=1.0,
                             in1=hhi(jj)[:, g * 512:g * 512 + 512],
                             op0=ALU.mult, op1=ALU.subtract)
                    split_done[jc] = k.c("sV")
                for sub in range(8):
                    for dg in range(4):
                        par = (sub * 4 + dg) % 2
                        pDn = pC if par == 0 else pD
                        k.wait("tensor", "sV", max(split_done[j4base + JB - 1], dn_evac[par]))
                        for jj in range(JB):
                            for hl in range(2):
                                hbuf = hhi if hl == 0 else hlo
                                k.mm(pDn[:, 0:512],
                                     hbuf(jj)[:, sub * 128:sub * 128 + 128],
                                     wqd(jj)[:, dg * 512:dg * 512 + 512],
                                     jj == 0 and hl == 0, jj == JB - 1 and hl == 1,
                                     s=("sT" if (jj == JB - 1 and hl == 1) else None))
                        k.wait("vector", "sT")
                        if j4base == 0:
                            k.op("vector", "tensor_copy", s="sV",
                                 out=acc(sub)[:, dg * 512:dg * 512 + 512], in_=pDn[:, 0:512])
                        else:
                            k.op("vector", "tensor_tensor", s="sV",
                                 out=acc(sub)[:, dg * 512:dg * 512 + 512],
                                 in0=pDn[:, 0:512],
                                 in1=acc(sub)[:, dg * 512:dg * 512 + 512], op=ALU.add)
                        dn_evac[par] = k.c("sV")
                dnT_prev = k.c("sT")

            k.barrier()
            for i in range(n_tiles):
                tile = ts0 + i
                r = slice(tile * 128, tile * 128 + 128)
                xsl = A_x[:, 0:2048]
                nsl = A_x[:, 2048:4096]
                if out_kind == "y":
                    k.wait("gpsimd", "sY")
                k.wait("gpsimd", "sV")
                k.wait("gpsimd", "sT")
                k.dma("gpsimd", xsl, x_src[r, :], "sG")
                k.dma("gpsimd", nsl, xn_src[r, :], "sG")
                k.wait("vector", "sG")
                k.op("vector", "tensor_tensor", out=acc(i), in0=acc(i), in1=gprep[:],
                     op=ALU.mult)
                k.op("vector", "scalar_tensor_tensor", out=nsl, in0=nsl, scalar=0.5,
                     in1=xsl, op0=ALU.mult, op1=ALU.add)
                k.op("vector", "tensor_tensor", s="sV", out=xsl, in0=acc(i), in1=nsl,
                     op=ALU.add)
                if out_kind == "y":
                    k.wait("sync", "sV")
                    k.dma("sync", y[i * 128:i * 128 + 128, :], xsl, "sY")
                    continue
                k.wait("gpsimd", "sV")
                k.dma("gpsimd", x1s[r, :], xsl, "sG")
                k.wait("vector", "sG")
                sq = acc(i)
                k.op("vector", "tensor_reduce", s="sV", out=stat[:, 12:13], in_=xsl,
                     axis=AX.X, op=ALU.add)
                k.wait("scalar", "sV")
                k.op("scalar", "activation", s="sS", out=stat[:, 13:14],
                     in_=stat[:, 12:13], func=AF.Identity, scale=R2048)
                k.wait("vector", "sS")
                k.op("vector", "tensor_scalar", out=xsl, in0=xsl,
                     scalar1=stat[:, 13:14], scalar2=None, op0=ALU.subtract)
                k.op("vector", "tensor_tensor", out=sq, in0=xsl, in1=xsl, op=ALU.mult)
                k.op("vector", "tensor_reduce", s="sV", out=stat[:, 14:15], in_=sq,
                     axis=AX.X, op=ALU.add)
                k.wait("scalar", "sV")
                k.op("scalar", "activation", s="sS", out=stat[:, 15:16],
                     in_=stat[:, 14:15], func=AF.Sqrt, scale=R2048, bias=EPS5)
                k.wait("vector", "sS")
                k.op("vector", "reciprocal", s="sV", out=stat[:, 16:17],
                     in_=stat[:, 15:16])
                k.wait("vector", "sV")
                k.op("vector", "tensor_scalar", s="sV", out=xsl, in0=xsl,
                     scalar1=stat[:, 16:17], scalar2=None, op0=ALU.mult)
                k.wait("tensor", "sV")
                sth = A_bf[:, 24576:24576 + 2048]
                stl = A_bf[:, 26624:26624 + 2048]
                for ecb in range(0, 16, 4):
                    for ec in range(ecb, ecb + 4):
                        k.tr(pA[:, (ec % 4) * 128:(ec % 4) * 128 + 128],
                             xsl[:, ec * 128:ec * 128 + 128], idf[:],
                             s=("sT" if ec == ecb + 3 else None))
                    k.wait("vector", "sT")
                    for ec in range(ecb, ecb + 4):
                        psl = pA[:, (ec % 4) * 128:(ec % 4) * 128 + 128]
                        k.op("vector", "tensor_copy",
                             out=sth[:, ec * 128:ec * 128 + 128], in_=psl)
                        k.op("vector", "scalar_tensor_tensor",
                             s=("sV" if ec == ecb + 3 else None),
                             out=stl[:, ec * 128:ec * 128 + 128], in0=psl, scalar=1.0,
                             in1=sth[:, ec * 128:ec * 128 + 128],
                             op0=ALU.mult, op1=ALU.subtract)
                    k.wait("tensor", "sV")
                k.wait("gpsimd", "sV")
                for ec in range(16):
                    k.dma("gpsimd", xt1h[ec, :, tile * 128:tile * 128 + 128],
                          sth[:, ec * 128:ec * 128 + 128], "sG")
                    k.dma("gpsimd", xt1l[ec, :, tile * 128:tile * 128 + 128],
                          stl[:, ec * 128:ec * 128 + 128], "sG")
            k.barrier()

        emit_ffn(0, xq1, x_in, xns, gcol, True, "x1")
        emit_ffn(8, xq1, x_in, xns, gcol, False, "x1")

        def emit_proj(mats, half):
            hof = half * 1024
            xh = lambda ec: A_bf[:, ec * 1024: ec * 1024 + 1024]
            xl = lambda ec: A_bf[:, 16384 + ec * 1024: 16384 + ec * 1024 + 1024]
            for ec in range(16):
                k.dma("sync", xh(ec), xt1h[ec, :, hof:hof + 1024], "sW")
                k.dma("sync", xl(ec), xt1l[ec, :, hof:hof + 1024], "sW")
            sth = A_wq[:, 8192:8192 + 512]
            stl = A_wq[:, 8704:8704 + 512]
            for (wsrc, dsth, dstl, ownrel) in mats:
                for dc in range(16):
                    par = dc % 2
                    whi = A_wq[:, par * 4096:par * 4096 + 2048]
                    wlo = A_wq[:, par * 4096 + 2048:par * 4096 + 4096]
                    k.wait("sync", "sT")
                    k.dma("sync", A_w[:], wsrc[dc], "sW")
                    k.wait("vector", "sW")
                    k.op("vector", "tensor_copy", out=whi[:], in_=A_w[:])
                    k.op("vector", "scalar_tensor_tensor", s="sV", out=wlo[:], in0=A_w[:],
                         scalar=1.0, in1=whi[:], op0=ALU.mult, op1=ALU.subtract)
                    for g in range(2):
                        pP = pA if g == 0 else pB
                        k.wait("tensor", "sW")
                        k.wait("tensor", "sV")
                        for ec in range(16):
                            xhs = xh(ec)[:, g * 512:g * 512 + 512]
                            xls = xl(ec)[:, g * 512:g * 512 + 512]
                            wh = whi[:, ec * 128:ec * 128 + 128]
                            wl = wlo[:, ec * 128:ec * 128 + 128]
                            k.mm(pP[:, 0:512], wh, xhs, ec == 0, False)
                            k.mm(pP[:, 0:512], wl, xhs, False, False)
                            k.mm(pP[:, 0:512], wh, xls, False, ec == 15,
                                 s=("sT" if ec == 15 else None))
                        k.wait("vector", "sT")
                        k.wait("vector", "sG")
                        qscale = 0.125 if ownrel else 1.0
                        k.op("vector", "tensor_scalar", out=sth[:], in0=pP[:, 0:512],
                             scalar1=qscale, scalar2=None, op0=ALU.mult)
                        k.op("vector", "scalar_tensor_tensor", s="sV", out=stl[:],
                             in0=pP[:, 0:512], scalar=qscale, in1=sth[:],
                             op0=ALU.mult, op1=ALU.subtract)
                        k.wait("gpsimd", "sV")
                        rr = slice(dc * 128, dc * 128 + 128)
                        if ownrel:
                            cc = slice(g * 512, g * 512 + 512)
                        else:
                            cc = slice(hof + g * 512, hof + g * 512 + 512)
                        k.dma("gpsimd", dsth[rr, cc], sth[:], "sG")
                        k.dma("gpsimd", dstl[rr, cc], stl[:], "sG")
            k.barrier()

        emit_proj([(wk_, kth, ktl, False), (wv_, vth, vtl, False),
                   (wq_, qth, qtl, True)], 0)
        emit_proj([(wk_, kth, ktl, False), (wv_, vth, vtl, False)], 1)

        kb_h = A_bf[0:64, 0:2048]
        kb_l = A_bf[0:64, 2048:4096]
        vtb = A_bf[:, 4096:6144]
        vb_h = A_bf[:, 6144:7168]
        vb_l = A_bf[:, 7168:8192]
        qb_h = A_bf[0:64, 8192:8704]
        qb_l = A_bf[0:64, 8704:9216]
        p_hi = lambda u: A_bf[:, 13312 + (u % 2) * 512:13824 + (u % 2) * 512]
        p_lo = lambda u: A_bf[:, 14336 + (u % 2) * 512:14848 + (u % 2) * 512]
        e_hi = lambda u: A_bf[:, 15360 + (u % 2) * 512:15872 + (u % 2) * 512]
        e_lo = lambda u: A_bf[:, 16384 + (u % 2) * 512:16896 + (u % 2) * 512]
        osth = A_bf[0:64, 11264:11776]
        ostl = A_bf[0:64, 11776:12288]
        t_f = lambda u: A_f32[:, (u % 2) * 2048:(u % 2) * 2048 + 512]
        p_f = lambda u: A_f32[:, (u % 2) * 2048 + 512:(u % 2) * 2048 + 1024]
        eS_f = lambda u: A_f32[:, (u % 2) * 2048 + 1024:(u % 2) * 2048 + 1536]
        e_f = lambda u: A_f32[:, (u % 2) * 2048 + 1536:(u % 2) * 2048 + 2048]

        for h in range(NH):
            hr = slice(h * 64, h * 64 + 64)
            k.wait("sync", "sT")
            k.wait("sync", "sV")
            k.dma("sync", kb_h, kth[hr, :], "sW")
            k.dma("sync", kb_l, ktl[hr, :], "sW")
            k.dma("sync", vtb[0:64, :], vth[hr, :], "sW")
            k.dma("sync", vtb[64:128, :], vtl[hr, :], "sW")
            k.wait("tensor", "sW")
            for jtb in range(0, 16, 4):
                for jt in range(jtb, jtb + 4):
                    k.tr(pT[:, (jt % 4) * 128:(jt % 4) * 128 + 128],
                         vtb[:, jt * 128:jt * 128 + 128], idb[:],
                         s=("sT" if jt == jtb + 3 else None))
                k.wait("vector", "sT")
                for jt in range(jtb, jtb + 4):
                    psl = pT[:, (jt % 4) * 128:(jt % 4) * 128 + 128]
                    k.op("vector", "tensor_copy", out=vb_h[:, jt * 64:jt * 64 + 64],
                         in_=psl[:, 0:64])
                    k.op("vector", "tensor_copy", s=("sV" if jt == jtb + 3 else None),
                         out=vb_l[:, jt * 64:jt * 64 + 64], in_=psl[:, 64:128])
                k.wait("tensor", "sV")
            for s2 in range(2):
                seq = JSEQ[s2]
                nu = len(seq)
                pO = pE if s2 == 0 else pF
                k.wait("sync", "sT")
                k.dma("sync", qb_h, qth[hr, s2 * 512:s2 * 512 + 512], "sW")
                k.dma("sync", qb_l, qtl[hr, s2 * 512:s2 * 512 + 512], "sW")
                k.inc("sS", 1)
                k.emit("scalar", lambda e: e.memzero(carry).then_inc(k.sems["sS"], 1))
                qld = k.c("sW")
                base_ts = k.c("sT")
                base_ss = k.c("sS")
                base_vs = k.c("sV")
                base_sm = k.c("sM")
                mbase = 0 if s2 == 0 else len(JSEQ[0])
                for pf in range(2):
                    k.dma("scalar", mbuf[:, (pf % 2) * 512:(pf % 2) * 512 + 512],
                          masks[mbase + pf], "sM")
                pZb = [pA, pC, pG]
                tz, tsx, tav = {}, {}, {}
                ss_exp = {}

                def em_z(v):
                    sl2 = seq[v]
                    ks2 = slice(sl2 * 128, sl2 * 128 + 128)
                    if v == 0:
                        k.wait("tensor", "sW", qld)
                    k.wait("tensor", "sS", ss_exp.get(v - 3, base_ss))
                    pZv = pZb[v % 3]
                    k.mm(pZv[:, 0:512], kb_h[:, ks2], qb_h[:], True, False)
                    k.mm(pZv[:, 0:512], kb_l[:, ks2], qb_h[:], False, False)
                    k.mm(pZv[:, 0:512], kb_h[:, ks2], qb_l[:], False, True, s="sT")
                    tz[v] = k.c("sT")

                def em_av(v):
                    sl2 = seq[v]
                    vsl = slice(sl2 * 64, sl2 * 64 + 64)
                    k.wait("tensor", "sV", base_vs + 3 * v + 3)
                    k.mm(pO[0:64, 0:512], vb_h[:, vsl], e_hi(v), v == 0, False)
                    k.mm(pO[0:64, 0:512], vb_h[:, vsl], e_lo(v), False, False)
                    k.mm(pO[0:64, 0:512], vb_l[:, vsl], e_hi(v), False, v == nu - 1,
                         s="sT")
                    tav[v] = k.c("sT")

                em_z(0)
                if nu > 1:
                    em_z(1)
                for ui, sl in enumerate(seq):
                    mpar = ui % 2
                    pS = pB if ui % 2 == 0 else pD
                    k.wait("tensor", "sV", base_vs + 3 * ui + 2)
                    k.wait("tensor", "sS", base_ss + 4 * ui - 1 if ui > 0 else base_ss)
                    k.mm(pS[:, 0:512], tmb[:], p_hi(ui), True, False)
                    k.mm(pS[:, 0:512], tmb[:], p_lo(ui), False, False)
                    k.mm(pS[:, 0:512], o128[0:1, :], carry, False, True, s="sT")
                    tsx[ui] = k.c("sT")
                    if ui >= 1:
                        em_av(ui - 1)
                    if ui + 2 < nu:
                        em_z(ui + 2)
                    k.wait("scalar", "sT", tz[ui])
                    if ui >= 2:
                        k.wait("scalar", "sV", base_vs + 3 * (ui - 2) + 3)
                    k.op("scalar", "activation", s="sS", out=t_f(ui),
                         in_=pZb[ui % 3][:, 0:512], func=AF.Exp)
                    ss_exp[ui] = k.c("sS")
                    k.wait("scalar", "sV", base_vs + 3 * ui + 1)
                    k.op("scalar", "activation", s="sS", out=p_f(ui), in_=t_f(ui),
                         func=AF.Ln, bias=ONES)
                    if ui + 2 < nu:
                        k.dma("scalar", mbuf[:, ((ui + 2) % 2) * 512:((ui + 2) % 2) * 512 + 512],
                              masks[mbase + ui + 2], "sM")
                    k.wait("scalar", "sT", tsx[ui])
                    k.op("scalar", "copy", s="sS", out=carry, in_=pS[0:1, 0:512])
                    k.op("scalar", "activation", s="sS", out=eS_f(ui), in_=pS[:, 0:512],
                         func=AF.Exp)
                    k.wait("vector", "sS", base_ss + 4 * ui + 1)
                    k.wait("vector", "sM", base_sm + 16 * (ui + 1))
                    k.op("vector", "tensor_tensor", s="sV", out=t_f(ui), in0=t_f(ui),
                         in1=mbuf[:, mpar * 512:mpar * 512 + 512], op=ALU.mult)
                    k.wait("vector", "sS", base_ss + 4 * ui + 2)
                    if ui >= 2:
                        k.wait("vector", "sT", tsx[ui - 2])
                    k.op("vector", "tensor_copy", out=p_hi(ui), in_=p_f(ui))
                    k.op("vector", "scalar_tensor_tensor", s="sV", out=p_lo(ui), in0=p_f(ui),
                         scalar=1.0, in1=p_hi(ui), op0=ALU.mult, op1=ALU.subtract)
                    k.wait("vector", "sS", base_ss + 4 * ui + 4)
                    if ui >= 2:
                        k.wait("vector", "sT", tav[ui - 2])
                    k.op("vector", "tensor_tensor", out=e_f(ui), in0=t_f(ui), in1=eS_f(ui),
                         op=ALU.mult)
                    k.op("vector", "tensor_copy", out=e_hi(ui), in_=e_f(ui))
                    k.op("vector", "scalar_tensor_tensor", s="sV", out=e_lo(ui), in0=e_f(ui),
                         scalar=1.0, in1=e_hi(ui), op0=ALU.mult, op1=ALU.subtract)
                em_av(nu - 1)
                k.wait("vector", "sT")
                k.wait("vector", "sG")
                k.op("vector", "tensor_copy", out=osth[:], in_=pO[0:64, 0:512])
                k.op("vector", "scalar_tensor_tensor", s="sV", out=ostl[:],
                     in0=pO[0:64, 0:512], scalar=1.0, in1=osth[:],
                     op0=ALU.mult, op1=ALU.subtract)
                k.wait("gpsimd", "sV")
                k.dma("gpsimd", oth[hr, s2 * 512:s2 * 512 + 512], osth[:], "sG")
                k.dma("gpsimd", otl[hr, s2 * 512:s2 * 512 + 512], ostl[:], "sG")
        k.barrier()

        oh = lambda hc: A_bf[:, hc * 1024: hc * 1024 + 1024]
        ol = lambda hc: A_bf[:, 16384 + hc * 1024: 16384 + hc * 1024 + 1024]
        for hc in range(16):
            rr = slice(hc * 128, hc * 128 + 128)
            k.dma("sync", oh(hc), oth[rr, :], "sW")
            k.dma("sync", ol(hc), otl[rr, :], "sW")
        wfb = [A_w[:], A_f32[:, 8192:10240]]
        wh2 = [A_wq[:, 0:2048], A_wq[:, 6144:8192]]
        wl2 = [A_wq[:, 2048:4096], A_wq[:, 8192:10240]]
        evb = A_x[:, 0:512]
        o_wld = {}
        o_split = {-1: 0, -2: 0}
        o_mm = {-1: 0, -2: 0}
        for g in range(2):
            for i_l in range(4):
                tile = g * 4 + i_l
                k.wait("gpsimd", "sV")
                k.wait("gpsimd", "sS")
                k.dma("gpsimd", A_f32[:, i_l * 2048:i_l * 2048 + 2048],
                      x1s[tile * 128:tile * 128 + 128, :], "sG")
            for ec in range(16):
                gi = g * 16 + ec
                par = gi % 2
                if gi == 0:
                    k.dma("sync", wfb[0], wo_[0], "sW")
                    o_wld[0] = k.c("sW")
                if ec + 1 < 16:
                    k.wait("sync", "sV", o_split[gi - 1])
                    k.dma("sync", wfb[(gi + 1) % 2], wo_[ec + 1], "sW")
                    o_wld[gi + 1] = k.c("sW")
                elif g == 0:
                    k.wait("sync", "sV", o_split[gi - 1])
                    k.dma("sync", wfb[(gi + 1) % 2], wo_[0], "sW")
                    o_wld[gi + 1] = k.c("sW")
                k.wait("vector", "sW", o_wld[gi])
                k.wait("vector", "sT", o_mm[gi - 2])
                k.op("vector", "tensor_copy", out=wh2[par][:], in_=wfb[par])
                k.op("vector", "scalar_tensor_tensor", s="sV", out=wl2[par][:],
                     in0=wfb[par], scalar=1.0, in1=wh2[par][:],
                     op0=ALU.mult, op1=ALU.subtract)
                o_split[gi] = k.c("sV")
                k.wait("tensor", "sV", o_split[gi])
                for hc in range(16):
                    wh = wh2[par][:, hc * 128:hc * 128 + 128]
                    wl = wl2[par][:, hc * 128:hc * 128 + 128]
                    ohs = oh(hc)[:, g * 512:g * 512 + 512]
                    ols = ol(hc)[:, g * 512:g * 512 + 512]
                    k.mm(pG[:, 0:512], wh, ohs, hc == 0, False)
                    k.mm(pG[:, 0:512], wl, ohs, False, False)
                    k.mm(pG[:, 0:512], wh, ols, False, hc == 15,
                         s=("sT" if hc == 15 else None))
                o_mm[gi] = k.c("sT")
                k.wait("vector", "sT")
                k.op("vector", "tensor_copy", s="sV", out=evb[:], in_=pG[:, 0:512])
                k.wait("tensor", "sV")
                k.wait("tensor", "sG")
                for i_l in range(4):
                    k.tr(pA[:, i_l * 128:i_l * 128 + 128],
                         evb[:, i_l * 128:i_l * 128 + 128], idf[:],
                         s=("sT" if i_l == 3 else None))
                k.wait("vector", "sT")
                for i_l in range(4):
                    dst = A_f32[:, i_l * 2048 + ec * 128:i_l * 2048 + ec * 128 + 128]
                    k.op("vector", "tensor_tensor", s=("sV" if i_l == 3 else None),
                         out=dst, in0=pA[:, i_l * 128:i_l * 128 + 128], in1=dst,
                         op=ALU.add)
                k.wait("tensor", "sV")
            for i_l in range(4):
                tile = g * 4 + i_l
                r = slice(tile * 128, tile * 128 + 128)
                xsl = A_f32[:, i_l * 2048:i_l * 2048 + 2048]
                nsl = A_x[:, 2048:4096]
                sq = A_x[:, 0:2048]
                k.wait("gpsimd", "sV")
                k.dma("gpsimd", x2s[r, :], xsl, "sG")
                k.wait("vector", "sS")
                k.op("vector", "tensor_reduce", s="sV", out=stat[:, 0:1], in_=xsl,
                     axis=AX.X, op=ALU.add)
                k.wait("scalar", "sV")
                k.op("scalar", "activation", s="sS", out=stat[:, 1:2],
                     in_=stat[:, 0:1], func=AF.Identity, scale=R2048)
                k.wait("vector", "sS")
                k.wait("vector", "sG")
                k.op("vector", "tensor_scalar", out=xsl, in0=xsl,
                     scalar1=stat[:, 1:2], scalar2=None, op0=ALU.subtract)
                k.op("vector", "tensor_tensor", out=sq, in0=xsl, in1=xsl, op=ALU.mult)
                k.op("vector", "tensor_reduce", s="sV", out=stat[:, 2:3], in_=sq,
                     axis=AX.X, op=ALU.add)
                k.wait("scalar", "sV")
                k.op("scalar", "activation", s="sS", out=stat[:, 3:4], in_=stat[:, 2:3],
                     func=AF.Sqrt, scale=R2048, bias=EPS5)
                k.wait("vector", "sS")
                k.op("vector", "reciprocal", s="sV", out=stat[:, 4:5], in_=stat[:, 3:4])
                k.wait("vector", "sV")
                k.op("vector", "tensor_scalar", s="sV", out=nsl, in0=xsl,
                     scalar1=stat[:, 4:5], scalar2=None, op0=ALU.mult)
                k.wait("gpsimd", "sV")
                k.dma("gpsimd", xn2s[r, :], nsl, "sG")
                k.op("vector", "tensor_tensor", out=sq, in0=nsl, in1=nsl, op=ALU.mult)
                k.op("vector", "tensor_reduce", s="sV", out=stat[:, 5:6], in_=sq,
                     axis=AX.X, op=ALU.add)
                k.wait("scalar", "sV")
                k.op("scalar", "activation", s="sS", out=stat[:, 6:7], in_=stat[:, 5:6],
                     func=AF.Sqrt, scale=R2048, bias=EPS6)
                k.wait("vector", "sS")
                k.op("vector", "reciprocal", s="sV", out=stat[:, 7:8], in_=stat[:, 6:7])
                k.wait("vector", "sV")
                k.op("vector", "tensor_scalar", out=xsl, in0=nsl,
                     scalar1=stat[:, 7:8], scalar2=None, op0=ALU.mult)
                k.op("vector", "tensor_tensor", out=xsl, in0=xsl, in1=nw[:], op=ALU.mult)
                k.op("vector", "tensor_reduce", s="sV", out=stat[:, 11:12], in_=xsl,
                     axis=AX.X, op=ALU.max, apply_absolute_value=True)
                k.wait("vector", "sV")
                k.op("vector", "tensor_tensor", s="sV", out=gcol2[:, tile:tile + 1],
                     in0=stat[:, 11:12], in1=conb[:, 4:5], op=ALU.max)
                k.wait("vector", "sV")
                k.op("vector", "reciprocal", s="sV", out=stat[:, 9:10],
                     in_=gcol2[:, tile:tile + 1])
                k.wait("scalar", "sV")
                k.op("scalar", "activation", s="sS", out=stat[:, 10:11],
                     in_=stat[:, 9:10], func=AF.Identity, scale=127.0)
                k.wait("vector", "sS")
                k.op("vector", "tensor_scalar", out=xsl, in0=xsl,
                     scalar1=stat[:, 10:11], scalar2=None, op0=ALU.mult)
                stg2 = A_wq[:, 4096:6144]
                for hcol in range(4):
                    cs = slice(hcol * 512, hcol * 512 + 512)
                    k.op("vector", "tensor_copy", s="sV", out=xqi[:], in_=xsl[:, cs])
                    k.wait("vector", "sV")
                    k.op("vector", "tensor_copy", s="sV", out=stg2[:, cs], in_=xqi[:])
                k.wait("gpsimd", "sV")
                k.dma("gpsimd", xq2[r, :], stg2, "sG")
        k.barrier()

        emit_ffn(0, xq2, x2s, xn2s, gcol2, False, "y")

        k.wait("gpsimd", "sY")
        k.wait("gpsimd", "sG")
        k.wait("gpsimd", "sW")
        k.wait("gpsimd", "sM")

        block = ctx.enter_context(nc.Block())

        @block.sync
        def _(e):
            for f in k.q["sync"]:
                f(e)

        @block.tensor
        def _(e):
            for f in k.q["tensor"]:
                f(e)

        @block.vector
        def _(e):
            for f in k.q["vector"]:
                f(e)

        @block.scalar
        def _(e):
            for f in k.q["scalar"]:
                f(e)

        @block.gpsimd
        def _(e):
            for f in k.q["gpsimd"]:
                f(e)

    return nc


def _chunkT(w):
    return np.ascontiguousarray(
        w.reshape(16, 128, 16, 128).transpose(2, 1, 0, 3).reshape(16, 128, 2048))


def _masks_for(half):
    ms = np.zeros((NU, 128, 512), dtype=ml_dtypes.bfloat16)
    mi = 0
    for s2 in range(2):
        for sl in JSEQ[s2]:
            gi = half * 1024 + s2 * 512 + np.arange(512)[None, :]
            if sl < 8:
                gj = half * 1024 + sl * 128 + np.arange(128)[:, None]
            else:
                gj = (1 - half) * 1024 + (sl - 8) * 128 + np.arange(128)[:, None]
            ms[mi] = (gj < gi).astype(np.float32)
            mi += 1
    return ms


def kernel(L0_attn_hidden_states, L0_attn_Wq, L0_attn_Wk, L0_attn_Wv, L0_attn_Wo,
           L0_ffn_weight_up, L0_ffn_weight_down, L0_ffn_norm_weight, L0_ffn_gamma):
    x = np.ascontiguousarray(L0_attn_hidden_states, dtype=np.float32)
    wu = np.ascontiguousarray(
        np.asarray(L0_ffn_weight_up, dtype=np.float32)
        .reshape(64, 128, 16, 128).transpose(0, 3, 2, 1).reshape(64, 128, 2048))
    wdn = np.ascontiguousarray(
        np.asarray(L0_ffn_weight_down, dtype=np.float32).T.reshape(64, 128, 2048))
    wq_ = _chunkT(np.asarray(L0_attn_Wq, dtype=np.float32))
    wk_ = _chunkT(np.asarray(L0_attn_Wk, dtype=np.float32))
    wv_ = _chunkT(np.asarray(L0_attn_Wv, dtype=np.float32))
    wo_ = _chunkT(np.asarray(L0_attn_Wo, dtype=np.float32))
    nwr = np.ascontiguousarray(
        np.broadcast_to(np.asarray(L0_ffn_norm_weight, dtype=np.float32), (128, E)))
    gamv = np.asarray(L0_ffn_gamma, dtype=np.float32).reshape(1, E)
    con = np.zeros((128, 6), dtype=np.float32)
    con[:, 0] = 1.0
    con[:, 1] = 1e-5
    con[:, 2] = 1e-6
    con[:, 3] = 127.0
    con[:, 4] = 1e-8
    ones128 = np.ones((128, 128), dtype=np.float32)
    idn_f = np.eye(128, dtype=np.float32)
    idn_b = np.eye(128, dtype=np.float32).astype(ml_dtypes.bfloat16)
    jj = np.arange(128)
    tmat = (-(jj[:, None] >= jj[None, :]).astype(np.float32)).astype(ml_dtypes.bfloat16)

    nc = build()
    in_maps = []
    for c in range(8):
        b, half = c // 2, c % 2
        own = x[b, half * 1024:(half + 1) * 1024]
        oth2 = x[b, (1 - half) * 1024:(2 - half) * 1024]
        x_c = np.ascontiguousarray(np.concatenate([own, oth2], axis=0))
        in_maps.append({
            "x_in": x_c, "wu": wu, "wd": wdn, "wq_": wq_, "wk_": wk_,
            "wv_": wv_, "wo_": wo_, "nwr": nwr, "gamv": gamv, "con": con,
            "ones128": ones128, "idn_f": idn_f, "idn_b": idn_b, "tmat": tmat,
            "masks": _masks_for(half),
        })
    res = run_bass_kernel_spmd(nc, in_maps, core_ids=list(range(8)))
    out = np.empty((4, T, E), dtype=np.float32)
    for c in range(8):
        b, half = c // 2, c % 2
        out[b, half * 1024:(half + 1) * 1024] = res.results[c]["y"]
    return out


if __name__ == "__main__":
    import reference as _r
    ins = _r.setup_inputs()
    o = kernel(**{kk: np.asarray(v) for kk, v in ins.items()})
    e = np.asarray(_r.reference(**ins))
    err = np.linalg.norm(o - e) / (np.linalg.norm(e) + 1e-12)
    print("rel l2:", err)


# version 08
# speedup vs baseline: 1.2518x; 1.0198x over previous
import numpy as np
import ml_dtypes
import concourse.bass as bass
import concourse.mybir as mybir
from concourse.bass_utils import run_bass_kernel_spmd
from contextlib import ExitStack

F32 = mybir.dt.float32
BF16 = mybir.dt.bfloat16
I32 = mybir.dt.int32
AF = mybir.ActivationFunctionType
ALU = mybir.AluOpType
AX = mybir.AxisListType

T = 2048
E = 2048
NJC = 64
NT = 16
OWN = 8
NH = 32
JB = 4
R2048 = float(2 ** -11)
RMEAN = float(2 ** -24)

JSEQ = [
    [3, 2, 1, 0, 15, 14, 13, 12, 11, 10, 9, 8],
    [7, 6, 5, 4, 3, 2, 1, 0, 15, 14, 13, 12, 11, 10, 9, 8],
]
NU = len(JSEQ[0]) + len(JSEQ[1])


class K:
    def __init__(self):
        self.q = {"sync": [], "tensor": [], "vector": [], "scalar": [], "gpsimd": []}
        self.cnt = {}
        self.sems = {}

    def inc(self, s, k):
        self.cnt[s] = self.cnt.get(s, 0) + k
        return self.cnt[s]

    def c(self, s):
        return self.cnt.get(s, 0)

    def emit(self, eng, fn):
        self.q[eng].append(fn)

    def wait(self, eng, s, val=None):
        v = self.c(s) if val is None else val
        if v <= 0:
            return
        self.emit(eng, lambda e, h=s, v=v: e.wait_ge(self.sems[h], v))

    def dma(self, eng, out, in_, s):
        self.inc(s, 16)
        self.emit(eng, lambda e, o=out, i=in_, h=s: e.dma_start(out=o, in_=i).then_inc(self.sems[h], 16))

    def op(self, eng, meth, s=None, **kw):
        if s is not None:
            self.inc(s, 1)

        def f(e, m=meth, kw=kw, h=s):
            r = getattr(e, m)(**kw)
            if h is not None:
                r.then_inc(self.sems[h], 1)
        self.emit(eng, f)

    def mm(self, out, lhsT, rhs, start, stop, s=None):
        if s is not None:
            self.inc(s, 1)

        def f(e, o=out, l=lhsT, r=rhs, st=start, sp=stop, h=s):
            x = e.matmul(o, l, r, start=st, stop=sp)
            if h is not None:
                x.then_inc(self.sems[h], 1)
        self.emit("tensor", f)

    def tr(self, out, in_, idn, s=None):
        if s is not None:
            self.inc(s, 1)

        def f(e, o=out, i=in_, d=idn, h=s):
            x = e.transpose(o, i, d)
            if h is not None:
                x.then_inc(self.sems[h], 1)
        self.emit("tensor", f)

    def vnop(self):
        self.inc("sV", 1)
        self.emit("vector", lambda e: e.nop().then_inc(self.sems["sV"], 1))

    def barrier(self):
        sems = [s for s in self.cnt if self.cnt[s] > 0]
        for eng in self.q:
            for s in sems:
                self.wait(eng, s)


def build():
    nc = bass.Bass("TRN2", target_bir_lowering=False, debug=False)
    ap = lambda t: t.ap()
    x_in = ap(nc.dram_tensor("x_in", [T, E], F32, kind="ExternalInput"))
    wu = ap(nc.dram_tensor("wu", [NJC, 128, E], F32, kind="ExternalInput"))
    wd = ap(nc.dram_tensor("wd", [NJC, 128, E], F32, kind="ExternalInput"))
    wq_ = ap(nc.dram_tensor("wq_", [16, 128, E], F32, kind="ExternalInput"))
    wk_ = ap(nc.dram_tensor("wk_", [16, 128, E], F32, kind="ExternalInput"))
    wv_ = ap(nc.dram_tensor("wv_", [16, 128, E], F32, kind="ExternalInput"))
    wo_ = ap(nc.dram_tensor("wo_", [16, 128, E], F32, kind="ExternalInput"))
    nwr = ap(nc.dram_tensor("nwr", [128, E], F32, kind="ExternalInput"))
    gamv = ap(nc.dram_tensor("gamv", [1, E], F32, kind="ExternalInput"))
    con = ap(nc.dram_tensor("con", [128, 6], F32, kind="ExternalInput"))
    ones128 = ap(nc.dram_tensor("ones128", [128, 128], F32, kind="ExternalInput"))
    idn_f = ap(nc.dram_tensor("idn_f", [128, 128], F32, kind="ExternalInput"))
    idn_b = ap(nc.dram_tensor("idn_b", [128, 128], BF16, kind="ExternalInput"))
    tmat = ap(nc.dram_tensor("tmat", [128, 128], BF16, kind="ExternalInput"))
    masks = ap(nc.dram_tensor("masks", [NU, 128, 512], BF16, kind="ExternalInput"))
    y = ap(nc.dram_tensor("y", [OWN * 128, E], F32, kind="ExternalOutput"))
    sc = lambda n, sh, dt: ap(nc.dram_tensor(n, sh, dt, kind="Internal"))
    x1s = sc("x1s", [T, E], F32)
    xns = sc("xns", [T, E], F32)
    x2s = sc("x2s", [OWN * 128, E], F32)
    xn2s = sc("xn2s", [OWN * 128, E], F32)
    xq1 = sc("xq1", [T, E], BF16)
    xq2 = sc("xq2", [OWN * 128, E], BF16)
    wqsu = sc("wqsu", [NJC, 128, E], BF16)
    wqsd = sc("wqsd", [NJC, 128, E], BF16)
    xt1h = sc("xt1h", [16, 128, T], BF16)
    xt1l = sc("xt1l", [16, 128, T], BF16)
    qth = sc("qth", [E, OWN * 128], BF16)
    qtl = sc("qtl", [E, OWN * 128], BF16)
    kth = sc("kth", [E, T], BF16)
    ktl = sc("ktl", [E, T], BF16)
    vth = sc("vth", [E, T], BF16)
    vtl = sc("vtl", [E, T], BF16)
    oth = sc("oth", [E, OWN * 128], BF16)
    otl = sc("otl", [E, OWN * 128], BF16)

    k = K()
    with ExitStack() as ctx:
        sb = lambda n, sh, dt: ctx.enter_context(nc.sbuf_tensor(n, sh, dt))
        psb = lambda n, sh: ctx.enter_context(nc.psum_tensor(n, sh, F32))
        for s in ["sW", "sG", "sM", "sT", "sV", "sS", "sY"]:
            k.sems[s] = ctx.enter_context(nc.semaphore(s))

        A_bf = sb("A_bf", [128, 32768], BF16)
        A_f32 = sb("A_f32", [128, 16384], F32)
        A_wq = sb("A_wq", [128, 12288], BF16)
        A_x = sb("A_x", [128, 4096], F32)
        A_w = sb("A_w", [128, 2048], F32)
        nw = sb("nw", [128, E], F32)
        gprep = sb("gprep", [128, E], F32)
        xqi = sb("xqi", [128, 512], I32)
        conb = sb("conb", [128, 6], F32)
        o128 = sb("o128", [128, 128], F32)
        idf = sb("idf", [128, 128], F32)
        idb = sb("idb", [128, 128], BF16)
        tmb = sb("tmb", [128, 128], BF16)
        stat = sb("stat", [128, 24], F32)
        gcol = sb("gcol", [128, 16], F32)
        gcol2 = sb("gcol2", [128, 16], F32)
        scal = sb("scal", [1, 16], F32)
        srow = sb("srow", [1, 3072], F32)
        carry = srow[0:1, 0:512]
        mbuf = A_bf[:, 12288:13312]
        wsum = stat[:, 20:24]
        r128 = stat[:, 18:20]
        wsumc = lambda i: stat[:, 20 + i:21 + i]
        wsum2 = stat[:, 22:23]

        pA = psb("pA", [128, 512])
        pB = psb("pB", [128, 512])
        pC = psb("pC", [128, 512])
        pD = psb("pD", [128, 512])
        pE = psb("pE", [128, 512])
        pF = psb("pF", [128, 512])
        pG = psb("pG", [128, 512])
        pT = ctx.enter_context(nc.psum_tensor("pT", [128, 1024], BF16))

        ONES = conb[:, 0:1]
        EPS5 = conb[:, 1:2]
        EPS6 = conb[:, 2:3]

        k.dma("sync", conb[:], con, "sW")
        k.dma("sync", o128[:], ones128, "sW")
        k.dma("sync", idf[:], idn_f, "sW")
        k.dma("sync", idb[:], idn_b, "sW")
        k.dma("sync", tmb[:], tmat, "sW")
        k.dma("sync", nw[:], nwr, "sW")
        k.dma("sync", srow[0:1, 1024:1024 + E], gamv, "sW")
        k.emit("gpsimd", lambda e: e.memset(wsum, 0.0).then_inc(k.sems["sG"], 1))
        k.inc("sG", 1)
        wbuf = [A_w[:], A_f32[:, 0:2048], A_f32[:, 2048:4096], A_f32[:, 4096:6144]]
        red_cnt = []
        si = 0
        for wi, wsrc in ((0, wu), (1, wd)):
            for jc in range(NJC):
                if si >= 4:
                    k.wait("sync", "sV", red_cnt[si - 4])
                k.dma("sync", wbuf[si % 4], wsrc[jc], "sW")
                k.wait("vector", "sW")
                if si == 0:
                    k.wait("vector", "sG", 1)
                k.op("vector", "tensor_reduce", s="sV", out=wsum2, in_=wbuf[si % 4],
                     axis=AX.X, op=ALU.add, apply_absolute_value=True)
                k.wait("vector", "sV")
                k.op("vector", "tensor_tensor", s="sV", out=wsumc(wi),
                     in0=wsum2, in1=wsumc(wi), op=ALU.add)
                red_cnt.append(k.c("sV"))
                si += 1

        def emit_prep(src_x, dst_xn, dst_xq, gc, gc_i, stat0):
            xsl = A_x[:, 0:2048]
            nsl = A_x[:, 2048:4096]
            sq = A_f32[:, 2048:4096]
            k.wait("gpsimd", "sV")
            k.dma("gpsimd", xsl, src_x, "sG")
            k.wait("vector", "sG")
            k.wait("vector", "sS")
            k.op("vector", "tensor_reduce", s="sV", out=stat[:, stat0:stat0 + 1],
                 in_=xsl, axis=AX.X, op=ALU.add)
            k.wait("scalar", "sV")
            k.op("scalar", "activation", s="sS", out=stat[:, stat0 + 1:stat0 + 2],
                 in_=stat[:, stat0:stat0 + 1], func=AF.Identity, scale=R2048)
            k.wait("vector", "sS")
            k.op("vector", "tensor_scalar", out=xsl, in0=xsl,
                 scalar1=stat[:, stat0 + 1:stat0 + 2], scalar2=None, op0=ALU.subtract)
            k.op("vector", "tensor_tensor", out=sq, in0=xsl, in1=xsl, op=ALU.mult)
            k.op("vector", "tensor_reduce", s="sV", out=stat[:, stat0 + 2:stat0 + 3],
                 in_=sq, axis=AX.X, op=ALU.add)
            k.wait("scalar", "sV")
            k.op("scalar", "activation", s="sS", out=stat[:, stat0 + 3:stat0 + 4],
                 in_=stat[:, stat0 + 2:stat0 + 3], func=AF.Sqrt, scale=R2048, bias=EPS5)
            k.wait("vector", "sS")
            k.op("vector", "reciprocal", s="sV", out=stat[:, stat0 + 4:stat0 + 5],
                 in_=stat[:, stat0 + 3:stat0 + 4])
            k.wait("vector", "sV")
            k.op("vector", "tensor_scalar", s="sV", out=nsl, in0=xsl,
                 scalar1=stat[:, stat0 + 4:stat0 + 5], scalar2=None, op0=ALU.mult)
            k.wait("gpsimd", "sV")
            k.dma("gpsimd", dst_xn, nsl, "sG")
            k.op("vector", "tensor_tensor", out=sq, in0=nsl, in1=nsl, op=ALU.mult)
            k.op("vector", "tensor_reduce", s="sV", out=stat[:, stat0 + 5:stat0 + 6],
                 in_=sq, axis=AX.X, op=ALU.add)
            k.wait("scalar", "sV")
            k.op("scalar", "activation", s="sS", out=stat[:, stat0 + 6:stat0 + 7],
                 in_=stat[:, stat0 + 5:stat0 + 6], func=AF.Sqrt, scale=R2048, bias=EPS6)
            k.wait("vector", "sS")
            k.op("vector", "reciprocal", s="sV", out=stat[:, stat0 + 7:stat0 + 8],
                 in_=stat[:, stat0 + 6:stat0 + 7])
            k.wait("vector", "sV")
            k.op("vector", "tensor_scalar", out=xsl, in0=nsl,
                 scalar1=stat[:, stat0 + 7:stat0 + 8], scalar2=None, op0=ALU.mult)
            k.op("vector", "tensor_tensor", out=xsl, in0=xsl, in1=nw[:], op=ALU.mult)
            k.op("vector", "tensor_reduce", s="sV", out=stat[:, stat0 + 11:stat0 + 12],
                 in_=xsl, axis=AX.X, op=ALU.max, apply_absolute_value=True)
            k.wait("vector", "sV")
            k.op("vector", "tensor_tensor", s="sV", out=gc[:, gc_i:gc_i + 1],
                 in0=stat[:, stat0 + 11:stat0 + 12], in1=conb[:, 4:5], op=ALU.max)
            k.wait("vector", "sV")
            k.op("vector", "reciprocal", s="sV", out=stat[:, stat0 + 9:stat0 + 10],
                 in_=gc[:, gc_i:gc_i + 1])
            k.wait("scalar", "sV")
            k.op("scalar", "activation", s="sS", out=stat[:, stat0 + 10:stat0 + 11],
                 in_=stat[:, stat0 + 9:stat0 + 10], func=AF.Identity, scale=127.0)
            k.wait("vector", "sS")
            k.op("vector", "tensor_scalar", out=xsl, in0=xsl,
                 scalar1=stat[:, stat0 + 10:stat0 + 11], scalar2=None, op0=ALU.mult)
            stg = A_wq[:, 4096:6144]
            for hcol in range(4):
                cs = slice(hcol * 512, hcol * 512 + 512)
                k.op("vector", "tensor_copy", s="sV", out=xqi[:], in_=xsl[:, cs])
                k.wait("vector", "sV")
                k.op("vector", "tensor_copy", s="sV", out=stg[:, cs], in_=xqi[:])
            k.wait("gpsimd", "sV")
            k.dma("gpsimd", dst_xq, stg, "sG")

        k.wait("scalar", "sW", 7 * 16)
        for tile in range(NT):
            r = slice(tile * 128, tile * 128 + 128)
            emit_prep(x_in[r, :], xns[r, :], xq1[r, :], gcol, tile, 0)

        k.barrier()
        for wi in range(2):
            k.mm(pG[0:1, 0:1], wsumc(wi), ONES, True, True, s="sT")
            k.wait("scalar", "sT")
            k.op("scalar", "activation", s="sS", out=scal[0:1, wi:wi + 1],
                 in_=pG[0:1, 0:1], func=AF.Identity, scale=RMEAN)
            k.wait("vector", "sS")
            k.op("vector", "tensor_tensor", s="sV", out=scal[0:1, 8 + wi:9 + wi],
                 in0=scal[0:1, wi:wi + 1], in1=conb[0:1, 4:5], op=ALU.max)
            k.wait("vector", "sV")
            k.op("vector", "reciprocal", s="sV", out=scal[0:1, 2 + wi:3 + wi],
                 in_=scal[0:1, 8 + wi:9 + wi])
            k.wait("tensor", "sV")
            k.wait("vector", "sV")
            k.mm(pG[:, 0:1], o128[0:1, :], scal[0:1, 2 + wi:3 + wi], True, True, s="sT")
            k.wait("vector", "sT")
            k.op("vector", "tensor_copy", s="sV", out=stat[:, 18 + wi:19 + wi], in_=pG[:, 0:1])
            k.wait("tensor", "sV")
        k.op("vector", "reciprocal", s="sV", out=scal[0:1, 4:5], in_=conb[0:1, 3:4])
        k.wait("vector", "sV")
        k.op("vector", "tensor_tensor", s="sV", out=scal[0:1, 5:6],
             in0=scal[0:1, 8:9], in1=scal[0:1, 4:5], op=ALU.mult)
        k.wait("scalar", "sV")
        k.op("scalar", "activation", s="sS", out=scal[0:1, 6:7], in_=scal[0:1, 9:10],
             func=AF.Identity, scale=0.5)
        k.wait("vector", "sS")
        k.op("vector", "tensor_scalar", s="sV", out=srow[0:1, 1024:1024 + E],
             in0=srow[0:1, 1024:1024 + E], scalar1=scal[0:1, 6:7], scalar2=None, op0=ALU.mult)
        k.wait("tensor", "sV")
        for g4 in range(4):
            k.mm(pG[:, 0:512], o128[0:1, :], srow[0:1, 1024 + g4 * 512:1536 + g4 * 512],
                 True, True, s="sT")
            k.wait("vector", "sT")
            k.op("vector", "tensor_copy", s="sV", out=gprep[:, g4 * 512:g4 * 512 + 512],
                 in_=pG[:, 0:512])
            k.wait("tensor", "sV")
        k.barrier()

        def emit_ffn(ts0, xq_src, x_src, xn_src, gc, first_w, out_kind):
            n_tiles = 8
            ntok = 1024
            ngrp = 2
            xqT = lambda ec: A_bf[:, ec * 1024: ec * 1024 + ntok]
            hhi = lambda j4: A_bf[:, 16384 + j4 * 1024: 16384 + j4 * 1024 + ntok]
            hlo = lambda j4: A_bf[:, 20480 + j4 * 1024: 20480 + j4 * 1024 + ntok]
            acc = lambda sub: A_f32[:, sub * 2048: sub * 2048 + 2048]
            srep = A_x[:, 0:1024]
            hs = A_x[:, 1024:1536]
            hsl = A_x[:, 1536:2048]
            wqu = lambda par: A_wq[:, 8192 + par * 2048: 8192 + par * 2048 + 2048]
            wqd = lambda j4: A_wq[:, j4 * 2048: j4 * 2048 + 2048]
            stg = A_bf[:, 24576:24576 + 2048]

            for i in range(n_tiles):
                r = slice((ts0 + i) * 128, (ts0 + i) * 128 + 128)
                k.wait("gpsimd", "sT")
                k.dma("gpsimd", stg, xq_src[r, :], "sG")
                k.wait("tensor", "sG")
                for ecb in range(0, 16, 4):
                    for ec in range(ecb, ecb + 4):
                        k.tr(pT[:, (ec % 4) * 128:(ec % 4) * 128 + 128],
                             stg[:, ec * 128:ec * 128 + 128], idb[:],
                             s=("sT" if ec == ecb + 3 else None))
                    k.wait("vector", "sT")
                    for ec in range(ecb, ecb + 4):
                        k.op("vector", "tensor_copy", s=("sV" if ec == ecb + 3 else None),
                             out=xqT(ec)[:, i * 128:i * 128 + 128],
                             in_=pT[:, (ec % 4) * 128:(ec % 4) * 128 + 128])
                    k.wait("tensor", "sV")
            for i in range(n_tiles):
                k.mm(pG[0:1, (i % 4) * 128:(i % 4) * 128 + 128],
                     gc[:, ts0 + i:ts0 + i + 1], idf[:], True, True,
                     s=("sT" if i % 4 == 3 else None))
                if i % 4 == 3:
                    k.wait("vector", "sT")
                    base = (i // 4) * 512
                    k.op("vector", "tensor_scalar", s="sV", out=srow[0:1, base:base + 512],
                         in0=pG[0:1, 0:512], scalar1=scal[0:1, 5:6], scalar2=None,
                         op0=ALU.mult)
                    k.wait("tensor", "sV")
            for g in range(ngrp):
                k.mm(pG[:, 0:512], o128[0:1, :], srow[0:1, g * 512:g * 512 + 512],
                     True, True, s="sT")
                k.wait("vector", "sT")
                k.op("vector", "tensor_copy", s="sV", out=srep[:, g * 512:g * 512 + 512],
                     in_=pG[:, 0:512])
                k.wait("tensor", "sV")

            split_done = {-1: k.c("sV")}
            dn_evac = {0: 0, 1: 0}
            pu_free = {0: 0, 1: 0}
            wqv = {}
            upT = {}
            dnT_prev = 0
            for j4base in range(0, NJC, JB):
                if not first_w:
                    k.wait("sync", "sT", dnT_prev)
                    for jj2 in range(JB):
                        k.dma("sync", wqd(jj2)[:], wqsd[j4base + jj2], "sW")
                for jj in range(JB):
                    jc = j4base + jj
                    par = jc % 2
                    if first_w:
                        k.wait("sync", "sV", wqv.get(jc - 1, 0))
                        k.dma("sync", A_w[:], wu[jc], "sW")
                        k.wait("vector", "sW")
                        k.wait("vector", "sG")
                        k.op("vector", "tensor_scalar", out=A_w[:], in0=A_w[:],
                             scalar1=stat[:, 18:19], scalar2=None, op0=ALU.mult)
                        k.op("vector", "tensor_scalar", out=A_w[:], in0=A_w[:],
                             scalar1=1.0, scalar2=-1.0, op0=ALU.min, op1=ALU.max)
                        for hcol in range(4):
                            cs = slice(hcol * 512, hcol * 512 + 512)
                            k.op("vector", "tensor_copy", s="sV", out=xqi[:], in_=A_w[:, cs])
                            k.wait("vector", "sV")
                            k.op("vector", "tensor_copy", s="sV", out=wqu(par)[:, cs], in_=xqi[:])
                        k.wait("gpsimd", "sV")
                        k.dma("gpsimd", wqsu[jc], wqu(par), "sG")
                        k.wait("sync", "sV")
                        k.dma("sync", A_w[:], wd[jc], "sW")
                        k.wait("vector", "sW")
                        k.wait("vector", "sG")
                        k.op("vector", "tensor_scalar", out=A_w[:], in0=A_w[:],
                             scalar1=stat[:, 19:20], scalar2=None, op0=ALU.mult)
                        k.op("vector", "tensor_scalar", out=A_w[:], in0=A_w[:],
                             scalar1=1.0, scalar2=-1.0, op0=ALU.min, op1=ALU.max)
                        for hcol in range(4):
                            cs = slice(hcol * 512, hcol * 512 + 512)
                            k.op("vector", "tensor_copy", s="sV", out=xqi[:], in_=A_w[:, cs])
                            k.wait("vector", "sV")
                            k.op("vector", "tensor_copy", s="sV", out=wqd(jj)[:, cs], in_=xqi[:])
                        k.wait("gpsimd", "sV")
                        k.dma("gpsimd", wqsd[jc], wqd(jj), "sG")
                        wqv[jc] = k.c("sV")
                        wq_w = None
                    else:
                        k.wait("sync", "sT", upT.get(jc - 2, 0))
                        k.dma("sync", wqu(par)[:], wqsu[jc], "sW")
                        wqv[jc] = 0
                        wq_w = k.c("sW")
                    for g in range(ngrp):
                        gpar = (jc * ngrp + g) % 2
                        pU = pA if gpar == 0 else pB
                        if wq_w is not None:
                            k.wait("tensor", "sW", wq_w)
                        k.wait("tensor", "sV", max(wqv[jc], pu_free[gpar]))
                        for ec in range(16):
                            k.mm(pU[:, 0:512], wqu(par)[:, ec * 128:ec * 128 + 128],
                                 xqT(ec)[:, g * 512:g * 512 + 512],
                                 ec == 0, ec == 15, s=("sT" if ec == 15 else None))
                        if g == ngrp - 1:
                            upT[jc] = k.c("sT")
                        k.wait("vector", "sT")
                        k.op("vector", "tensor_tensor", s="sV", out=hs, in0=pU[:, 0:512],
                             in1=srep[:, g * 512:g * 512 + 512], op=ALU.mult)
                        pu_free[gpar] = k.c("sV")
                        k.wait("scalar", "sV")
                        k.op("scalar", "activation", s="sS", out=hsl, in_=hs, func=AF.Silu)
                        k.wait("vector", "sS")
                        k.op("vector", "tensor_copy",
                             out=hhi(jj)[:, g * 512:g * 512 + 512], in_=hsl)
                        k.op("vector", "scalar_tensor_tensor", s="sV",
                             out=hlo(jj)[:, g * 512:g * 512 + 512], in0=hsl, scalar=1.0,
                             in1=hhi(jj)[:, g * 512:g * 512 + 512],
                             op0=ALU.mult, op1=ALU.subtract)
                    split_done[jc] = k.c("sV")
                for sub in range(8):
                    for dg in range(4):
                        par = (sub * 4 + dg) % 2
                        pDn = pC if par == 0 else pD
                        k.wait("tensor", "sV", max(split_done[j4base + JB - 1], dn_evac[par]))
                        for jj in range(JB):
                            for hl in range(2):
                                hbuf = hhi if hl == 0 else hlo
                                k.mm(pDn[:, 0:512],
                                     hbuf(jj)[:, sub * 128:sub * 128 + 128],
                                     wqd(jj)[:, dg * 512:dg * 512 + 512],
                                     jj == 0 and hl == 0, jj == JB - 1 and hl == 1,
                                     s=("sT" if (jj == JB - 1 and hl == 1) else None))
                        k.wait("vector", "sT")
                        if j4base == 0:
                            k.op("vector", "tensor_copy", s="sV",
                                 out=acc(sub)[:, dg * 512:dg * 512 + 512], in_=pDn[:, 0:512])
                        else:
                            k.op("vector", "tensor_tensor", s="sV",
                                 out=acc(sub)[:, dg * 512:dg * 512 + 512],
                                 in0=pDn[:, 0:512],
                                 in1=acc(sub)[:, dg * 512:dg * 512 + 512], op=ALU.add)
                        dn_evac[par] = k.c("sV")
                dnT_prev = k.c("sT")

            k.barrier()
            for i in range(n_tiles):
                tile = ts0 + i
                r = slice(tile * 128, tile * 128 + 128)
                xsl = A_x[:, 0:2048]
                nsl = A_x[:, 2048:4096]
                if out_kind == "y":
                    k.wait("gpsimd", "sY")
                k.wait("gpsimd", "sV")
                k.wait("gpsimd", "sT")
                k.dma("gpsimd", xsl, x_src[r, :], "sG")
                k.dma("gpsimd", nsl, xn_src[r, :], "sG")
                k.wait("vector", "sG")
                k.op("vector", "tensor_tensor", out=acc(i), in0=acc(i), in1=gprep[:],
                     op=ALU.mult)
                k.op("vector", "scalar_tensor_tensor", out=nsl, in0=nsl, scalar=0.5,
                     in1=xsl, op0=ALU.mult, op1=ALU.add)
                k.op("vector", "tensor_tensor", s="sV", out=xsl, in0=acc(i), in1=nsl,
                     op=ALU.add)
                if out_kind == "y":
                    k.wait("sync", "sV")
                    k.dma("sync", y[i * 128:i * 128 + 128, :], xsl, "sY")
                    continue
                k.wait("gpsimd", "sV")
                k.dma("gpsimd", x1s[r, :], xsl, "sG")
                k.wait("vector", "sG")
                sq = acc(i)
                k.op("vector", "tensor_reduce", s="sV", out=stat[:, 12:13], in_=xsl,
                     axis=AX.X, op=ALU.add)
                k.wait("scalar", "sV")
                k.op("scalar", "activation", s="sS", out=stat[:, 13:14],
                     in_=stat[:, 12:13], func=AF.Identity, scale=R2048)
                k.wait("vector", "sS")
                k.op("vector", "tensor_scalar", out=xsl, in0=xsl,
                     scalar1=stat[:, 13:14], scalar2=None, op0=ALU.subtract)
                k.op("vector", "tensor_tensor", out=sq, in0=xsl, in1=xsl, op=ALU.mult)
                k.op("vector", "tensor_reduce", s="sV", out=stat[:, 14:15], in_=sq,
                     axis=AX.X, op=ALU.add)
                k.wait("scalar", "sV")
                k.op("scalar", "activation", s="sS", out=stat[:, 15:16],
                     in_=stat[:, 14:15], func=AF.Sqrt, scale=R2048, bias=EPS5)
                k.wait("vector", "sS")
                k.op("vector", "reciprocal", s="sV", out=stat[:, 16:17],
                     in_=stat[:, 15:16])
                k.wait("vector", "sV")
                k.op("vector", "tensor_scalar", s="sV", out=xsl, in0=xsl,
                     scalar1=stat[:, 16:17], scalar2=None, op0=ALU.mult)
                k.wait("tensor", "sV")
                sth = A_bf[:, 24576:24576 + 2048]
                stl = A_bf[:, 26624:26624 + 2048]
                for ecb in range(0, 16, 4):
                    for ec in range(ecb, ecb + 4):
                        k.tr(pA[:, (ec % 4) * 128:(ec % 4) * 128 + 128],
                             xsl[:, ec * 128:ec * 128 + 128], idf[:],
                             s=("sT" if ec == ecb + 3 else None))
                    k.wait("vector", "sT")
                    for ec in range(ecb, ecb + 4):
                        psl = pA[:, (ec % 4) * 128:(ec % 4) * 128 + 128]
                        k.op("vector", "tensor_copy",
                             out=sth[:, ec * 128:ec * 128 + 128], in_=psl)
                        k.op("vector", "scalar_tensor_tensor",
                             s=("sV" if ec == ecb + 3 else None),
                             out=stl[:, ec * 128:ec * 128 + 128], in0=psl, scalar=1.0,
                             in1=sth[:, ec * 128:ec * 128 + 128],
                             op0=ALU.mult, op1=ALU.subtract)
                    k.wait("tensor", "sV")
                k.wait("gpsimd", "sV")
                for ec in range(16):
                    k.dma("gpsimd", xt1h[ec, :, tile * 128:tile * 128 + 128],
                          sth[:, ec * 128:ec * 128 + 128], "sG")
                    k.dma("gpsimd", xt1l[ec, :, tile * 128:tile * 128 + 128],
                          stl[:, ec * 128:ec * 128 + 128], "sG")
            k.barrier()

        emit_ffn(0, xq1, x_in, xns, gcol, True, "x1")
        emit_ffn(8, xq1, x_in, xns, gcol, False, "x1")

        def emit_proj(mats, half):
            hof = half * 1024
            xh = lambda ec: A_bf[:, ec * 1024: ec * 1024 + 1024]
            xl = lambda ec: A_bf[:, 16384 + ec * 1024: 16384 + ec * 1024 + 1024]
            for ec in range(16):
                k.dma("sync", xh(ec), xt1h[ec, :, hof:hof + 1024], "sW")
                k.dma("sync", xl(ec), xt1l[ec, :, hof:hof + 1024], "sW")
            sth = A_wq[:, 8192:8192 + 512]
            stl = A_wq[:, 8704:8704 + 512]
            slabs = [(wsrc, dc, dsth, dstl, ownrel)
                     for (wsrc, dsth, dstl, ownrel) in mats for dc in range(16)]
            pfb = [A_w[:], A_f32[:, 0:2048]]
            p_wld = {}
            p_split = {-1: 0}
            p_mm = {-1: 0, -2: 0}
            p_ev = {}
            for i, (wsrc, dc, dsth, dstl, ownrel) in enumerate(slabs):
                par = i % 2
                whi = A_wq[:, par * 4096:par * 4096 + 2048]
                wlo = A_wq[:, par * 4096 + 2048:par * 4096 + 4096]
                if i == 0:
                    k.dma("sync", pfb[0], wsrc[dc], "sW")
                    p_wld[0] = k.c("sW")
                if i + 1 < len(slabs):
                    ws2, dc2 = slabs[i + 1][0], slabs[i + 1][1]
                    k.wait("sync", "sV", p_split[i - 1])
                    k.dma("sync", pfb[(i + 1) % 2], ws2[dc2], "sW")
                    p_wld[i + 1] = k.c("sW")
                k.wait("vector", "sW", p_wld[i])
                k.wait("vector", "sT", p_mm[i - 2])
                k.op("vector", "tensor_copy", out=whi[:], in_=pfb[par])
                k.op("vector", "scalar_tensor_tensor", s="sV", out=wlo[:], in0=pfb[par],
                     scalar=1.0, in1=whi[:], op0=ALU.mult, op1=ALU.subtract)
                p_split[i] = k.c("sV")
                for g in range(2):
                    pP = pA if g == 0 else pB
                    k.wait("tensor", "sV",
                           max(p_split[i], p_ev.get((g, i - 1), 0)))
                    for ec in range(16):
                        xhs = xh(ec)[:, g * 512:g * 512 + 512]
                        xls = xl(ec)[:, g * 512:g * 512 + 512]
                        wh = whi[:, ec * 128:ec * 128 + 128]
                        wl = wlo[:, ec * 128:ec * 128 + 128]
                        k.mm(pP[:, 0:512], wh, xhs, ec == 0, False)
                        k.mm(pP[:, 0:512], wl, xhs, False, False)
                        k.mm(pP[:, 0:512], wh, xls, False, ec == 15,
                             s=("sT" if ec == 15 else None))
                    k.wait("vector", "sT")
                    k.wait("vector", "sG")
                    qscale = 0.125 if ownrel else 1.0
                    k.op("vector", "tensor_scalar", out=sth[:], in0=pP[:, 0:512],
                         scalar1=qscale, scalar2=None, op0=ALU.mult)
                    k.op("vector", "scalar_tensor_tensor", s="sV", out=stl[:],
                         in0=pP[:, 0:512], scalar=qscale, in1=sth[:],
                         op0=ALU.mult, op1=ALU.subtract)
                    p_ev[(g, i)] = k.c("sV")
                    k.wait("gpsimd", "sV")
                    rr = slice(dc * 128, dc * 128 + 128)
                    if ownrel:
                        cc = slice(g * 512, g * 512 + 512)
                    else:
                        cc = slice(hof + g * 512, hof + g * 512 + 512)
                    k.dma("gpsimd", dsth[rr, cc], sth[:], "sG")
                    k.dma("gpsimd", dstl[rr, cc], stl[:], "sG")
                p_mm[i] = k.c("sT")
            k.barrier()

        emit_proj([(wk_, kth, ktl, False), (wv_, vth, vtl, False),
                   (wq_, qth, qtl, True)], 0)
        emit_proj([(wk_, kth, ktl, False), (wv_, vth, vtl, False)], 1)

        kb_h = A_bf[0:64, 0:2048]
        kb_l = A_bf[0:64, 2048:4096]
        vtb = A_bf[:, 4096:6144]
        vb_h = A_bf[:, 6144:7168]
        vb_l = A_bf[:, 7168:8192]
        qb_h = A_bf[0:64, 8192:8704]
        qb_l = A_bf[0:64, 8704:9216]
        p_hi = lambda u: A_bf[:, 13312 + (u % 2) * 512:13824 + (u % 2) * 512]
        p_lo = lambda u: A_bf[:, 14336 + (u % 2) * 512:14848 + (u % 2) * 512]
        e_hi = lambda u: A_bf[:, 15360 + (u % 2) * 512:15872 + (u % 2) * 512]
        e_lo = lambda u: A_bf[:, 16384 + (u % 2) * 512:16896 + (u % 2) * 512]
        osth = A_bf[0:64, 11264:11776]
        ostl = A_bf[0:64, 11776:12288]
        t_f = lambda u: A_f32[:, (u % 2) * 2048:(u % 2) * 2048 + 512]
        p_f = lambda u: A_f32[:, (u % 2) * 2048 + 512:(u % 2) * 2048 + 1024]
        eS_f = lambda u: A_f32[:, (u % 2) * 2048 + 1024:(u % 2) * 2048 + 1536]
        e_f = lambda u: A_f32[:, (u % 2) * 2048 + 1536:(u % 2) * 2048 + 2048]

        for h in range(NH):
            hr = slice(h * 64, h * 64 + 64)
            k.wait("sync", "sT")
            k.wait("sync", "sV")
            k.dma("sync", kb_h, kth[hr, :], "sW")
            k.dma("sync", kb_l, ktl[hr, :], "sW")
            k.dma("sync", vtb[0:64, :], vth[hr, :], "sW")
            k.dma("sync", vtb[64:128, :], vtl[hr, :], "sW")
            k.wait("tensor", "sW")
            for jtb in range(0, 16, 4):
                for jt in range(jtb, jtb + 4):
                    k.tr(pT[:, (jt % 4) * 128:(jt % 4) * 128 + 128],
                         vtb[:, jt * 128:jt * 128 + 128], idb[:],
                         s=("sT" if jt == jtb + 3 else None))
                k.wait("vector", "sT")
                for jt in range(jtb, jtb + 4):
                    psl = pT[:, (jt % 4) * 128:(jt % 4) * 128 + 128]
                    k.op("vector", "tensor_copy", out=vb_h[:, jt * 64:jt * 64 + 64],
                         in_=psl[:, 0:64])
                    k.op("vector", "tensor_copy", s=("sV" if jt == jtb + 3 else None),
                         out=vb_l[:, jt * 64:jt * 64 + 64], in_=psl[:, 64:128])
                k.wait("tensor", "sV")
            for s2 in range(2):
                seq = JSEQ[s2]
                nu = len(seq)
                pO = pE if s2 == 0 else pF
                k.wait("sync", "sT")
                k.dma("sync", qb_h, qth[hr, s2 * 512:s2 * 512 + 512], "sW")
                k.dma("sync", qb_l, qtl[hr, s2 * 512:s2 * 512 + 512], "sW")
                k.inc("sS", 1)
                k.emit("scalar", lambda e: e.memzero(carry).then_inc(k.sems["sS"], 1))
                qld = k.c("sW")
                base_ts = k.c("sT")
                base_ss = k.c("sS")
                base_vs = k.c("sV")
                base_sm = k.c("sM")
                mbase = 0 if s2 == 0 else len(JSEQ[0])
                for pf in range(2):
                    k.dma("scalar", mbuf[:, (pf % 2) * 512:(pf % 2) * 512 + 512],
                          masks[mbase + pf], "sM")
                pZb = [pA, pC, pG]
                tz, tsx, tav = {}, {}, {}
                ss_exp = {}

                def em_z(v):
                    sl2 = seq[v]
                    ks2 = slice(sl2 * 128, sl2 * 128 + 128)
                    if v == 0:
                        k.wait("tensor", "sW", qld)
                    k.wait("tensor", "sS", ss_exp.get(v - 3, base_ss))
                    pZv = pZb[v % 3]
                    k.mm(pZv[:, 0:512], kb_h[:, ks2], qb_h[:], True, False)
                    k.mm(pZv[:, 0:512], kb_l[:, ks2], qb_h[:], False, False)
                    k.mm(pZv[:, 0:512], kb_h[:, ks2], qb_l[:], False, True, s="sT")
                    tz[v] = k.c("sT")

                def em_av(v):
                    sl2 = seq[v]
                    vsl = slice(sl2 * 64, sl2 * 64 + 64)
                    k.wait("tensor", "sV", base_vs + 3 * v + 3)
                    k.mm(pO[0:64, 0:512], vb_h[:, vsl], e_hi(v), v == 0, False)
                    k.mm(pO[0:64, 0:512], vb_h[:, vsl], e_lo(v), False, False)
                    k.mm(pO[0:64, 0:512], vb_l[:, vsl], e_hi(v), False, v == nu - 1,
                         s="sT")
                    tav[v] = k.c("sT")

                em_z(0)
                if nu > 1:
                    em_z(1)
                for ui, sl in enumerate(seq):
                    mpar = ui % 2
                    pS = pB if ui % 2 == 0 else pD
                    k.wait("tensor", "sV", base_vs + 3 * ui + 2)
                    k.wait("tensor", "sS", base_ss + 4 * ui - 1 if ui > 0 else base_ss)
                    k.mm(pS[:, 0:512], tmb[:], p_hi(ui), True, False)
                    k.mm(pS[:, 0:512], tmb[:], p_lo(ui), False, False)
                    k.mm(pS[:, 0:512], o128[0:1, :], carry, False, True, s="sT")
                    tsx[ui] = k.c("sT")
                    if ui >= 1:
                        em_av(ui - 1)
                    if ui + 2 < nu:
                        em_z(ui + 2)
                    k.wait("scalar", "sT", tz[ui])
                    if ui >= 2:
                        k.wait("scalar", "sV", base_vs + 3 * (ui - 2) + 3)
                    k.op("scalar", "activation", s="sS", out=t_f(ui),
                         in_=pZb[ui % 3][:, 0:512], func=AF.Exp)
                    ss_exp[ui] = k.c("sS")
                    k.wait("scalar", "sV", base_vs + 3 * ui + 1)
                    k.op("scalar", "activation", s="sS", out=p_f(ui), in_=t_f(ui),
                         func=AF.Ln, bias=ONES)
                    if ui + 2 < nu:
                        k.dma("scalar", mbuf[:, ((ui + 2) % 2) * 512:((ui + 2) % 2) * 512 + 512],
                              masks[mbase + ui + 2], "sM")
                    k.wait("scalar", "sT", tsx[ui])
                    k.op("scalar", "copy", s="sS", out=carry, in_=pS[0:1, 0:512])
                    k.op("scalar", "activation", s="sS", out=eS_f(ui), in_=pS[:, 0:512],
                         func=AF.Exp)
                    k.wait("vector", "sS", base_ss + 4 * ui + 1)
                    k.wait("vector", "sM", base_sm + 16 * (ui + 1))
                    k.op("vector", "tensor_tensor", s="sV", out=t_f(ui), in0=t_f(ui),
                         in1=mbuf[:, mpar * 512:mpar * 512 + 512], op=ALU.mult)
                    k.wait("vector", "sS", base_ss + 4 * ui + 2)
                    if ui >= 2:
                        k.wait("vector", "sT", tsx[ui - 2])
                    k.op("vector", "tensor_copy", out=p_hi(ui), in_=p_f(ui))
                    k.op("vector", "scalar_tensor_tensor", s="sV", out=p_lo(ui), in0=p_f(ui),
                         scalar=1.0, in1=p_hi(ui), op0=ALU.mult, op1=ALU.subtract)
                    k.wait("vector", "sS", base_ss + 4 * ui + 4)
                    if ui >= 2:
                        k.wait("vector", "sT", tav[ui - 2])
                    k.op("vector", "tensor_tensor", out=e_f(ui), in0=t_f(ui), in1=eS_f(ui),
                         op=ALU.mult)
                    k.op("vector", "tensor_copy", out=e_hi(ui), in_=e_f(ui))
                    k.op("vector", "scalar_tensor_tensor", s="sV", out=e_lo(ui), in0=e_f(ui),
                         scalar=1.0, in1=e_hi(ui), op0=ALU.mult, op1=ALU.subtract)
                em_av(nu - 1)
                k.wait("vector", "sT")
                k.wait("vector", "sG")
                k.op("vector", "tensor_copy", out=osth[:], in_=pO[0:64, 0:512])
                k.op("vector", "scalar_tensor_tensor", s="sV", out=ostl[:],
                     in0=pO[0:64, 0:512], scalar=1.0, in1=osth[:],
                     op0=ALU.mult, op1=ALU.subtract)
                k.wait("gpsimd", "sV")
                k.dma("gpsimd", oth[hr, s2 * 512:s2 * 512 + 512], osth[:], "sG")
                k.dma("gpsimd", otl[hr, s2 * 512:s2 * 512 + 512], ostl[:], "sG")
        k.barrier()

        oh = lambda hc: A_bf[:, hc * 1024: hc * 1024 + 1024]
        ol = lambda hc: A_bf[:, 16384 + hc * 1024: 16384 + hc * 1024 + 1024]
        for hc in range(16):
            rr = slice(hc * 128, hc * 128 + 128)
            k.dma("sync", oh(hc), oth[rr, :], "sW")
            k.dma("sync", ol(hc), otl[rr, :], "sW")
        wfb = [A_w[:], A_f32[:, 8192:10240]]
        wh2 = [A_wq[:, 0:2048], A_wq[:, 6144:8192]]
        wl2 = [A_wq[:, 2048:4096], A_wq[:, 8192:10240]]
        evb = A_x[:, 0:512]
        o_wld = {}
        o_split = {-1: 0, -2: 0}
        o_mm = {-1: 0, -2: 0}
        for g in range(2):
            for i_l in range(4):
                tile = g * 4 + i_l
                k.wait("gpsimd", "sV")
                k.wait("gpsimd", "sS")
                k.dma("gpsimd", A_f32[:, i_l * 2048:i_l * 2048 + 2048],
                      x1s[tile * 128:tile * 128 + 128, :], "sG")
            for ec in range(16):
                gi = g * 16 + ec
                par = gi % 2
                if gi == 0:
                    k.dma("sync", wfb[0], wo_[0], "sW")
                    o_wld[0] = k.c("sW")
                if ec + 1 < 16:
                    k.wait("sync", "sV", o_split[gi - 1])
                    k.dma("sync", wfb[(gi + 1) % 2], wo_[ec + 1], "sW")
                    o_wld[gi + 1] = k.c("sW")
                elif g == 0:
                    k.wait("sync", "sV", o_split[gi - 1])
                    k.dma("sync", wfb[(gi + 1) % 2], wo_[0], "sW")
                    o_wld[gi + 1] = k.c("sW")
                k.wait("vector", "sW", o_wld[gi])
                k.wait("vector", "sT", o_mm[gi - 2])
                k.op("vector", "tensor_copy", out=wh2[par][:], in_=wfb[par])
                k.op("vector", "scalar_tensor_tensor", s="sV", out=wl2[par][:],
                     in0=wfb[par], scalar=1.0, in1=wh2[par][:],
                     op0=ALU.mult, op1=ALU.subtract)
                o_split[gi] = k.c("sV")
                k.wait("tensor", "sV", o_split[gi])
                for hc in range(16):
                    wh = wh2[par][:, hc * 128:hc * 128 + 128]
                    wl = wl2[par][:, hc * 128:hc * 128 + 128]
                    ohs = oh(hc)[:, g * 512:g * 512 + 512]
                    ols = ol(hc)[:, g * 512:g * 512 + 512]
                    k.mm(pG[:, 0:512], wh, ohs, hc == 0, False)
                    k.mm(pG[:, 0:512], wl, ohs, False, False)
                    k.mm(pG[:, 0:512], wh, ols, False, hc == 15,
                         s=("sT" if hc == 15 else None))
                o_mm[gi] = k.c("sT")
                k.wait("vector", "sT")
                k.op("vector", "tensor_copy", s="sV", out=evb[:], in_=pG[:, 0:512])
                k.wait("tensor", "sV")
                k.wait("tensor", "sG")
                for i_l in range(4):
                    k.tr(pA[:, i_l * 128:i_l * 128 + 128],
                         evb[:, i_l * 128:i_l * 128 + 128], idf[:],
                         s=("sT" if i_l == 3 else None))
                k.wait("vector", "sT")
                for i_l in range(4):
                    dst = A_f32[:, i_l * 2048 + ec * 128:i_l * 2048 + ec * 128 + 128]
                    k.op("vector", "tensor_tensor", s=("sV" if i_l == 3 else None),
                         out=dst, in0=pA[:, i_l * 128:i_l * 128 + 128], in1=dst,
                         op=ALU.add)
                k.wait("tensor", "sV")
            for i_l in range(4):
                tile = g * 4 + i_l
                r = slice(tile * 128, tile * 128 + 128)
                xsl = A_f32[:, i_l * 2048:i_l * 2048 + 2048]
                nsl = A_x[:, 2048:4096]
                sq = A_x[:, 0:2048]
                k.wait("gpsimd", "sV")
                k.dma("gpsimd", x2s[r, :], xsl, "sG")
                k.wait("vector", "sS")
                k.op("vector", "tensor_reduce", s="sV", out=stat[:, 0:1], in_=xsl,
                     axis=AX.X, op=ALU.add)
                k.wait("scalar", "sV")
                k.op("scalar", "activation", s="sS", out=stat[:, 1:2],
                     in_=stat[:, 0:1], func=AF.Identity, scale=R2048)
                k.wait("vector", "sS")
                k.wait("vector", "sG")
                k.op("vector", "tensor_scalar", out=xsl, in0=xsl,
                     scalar1=stat[:, 1:2], scalar2=None, op0=ALU.subtract)
                k.op("vector", "tensor_tensor", out=sq, in0=xsl, in1=xsl, op=ALU.mult)
                k.op("vector", "tensor_reduce", s="sV", out=stat[:, 2:3], in_=sq,
                     axis=AX.X, op=ALU.add)
                k.wait("scalar", "sV")
                k.op("scalar", "activation", s="sS", out=stat[:, 3:4], in_=stat[:, 2:3],
                     func=AF.Sqrt, scale=R2048, bias=EPS5)
                k.wait("vector", "sS")
                k.op("vector", "reciprocal", s="sV", out=stat[:, 4:5], in_=stat[:, 3:4])
                k.wait("vector", "sV")
                k.op("vector", "tensor_scalar", s="sV", out=nsl, in0=xsl,
                     scalar1=stat[:, 4:5], scalar2=None, op0=ALU.mult)
                k.wait("gpsimd", "sV")
                k.dma("gpsimd", xn2s[r, :], nsl, "sG")
                k.op("vector", "tensor_tensor", out=sq, in0=nsl, in1=nsl, op=ALU.mult)
                k.op("vector", "tensor_reduce", s="sV", out=stat[:, 5:6], in_=sq,
                     axis=AX.X, op=ALU.add)
                k.wait("scalar", "sV")
                k.op("scalar", "activation", s="sS", out=stat[:, 6:7], in_=stat[:, 5:6],
                     func=AF.Sqrt, scale=R2048, bias=EPS6)
                k.wait("vector", "sS")
                k.op("vector", "reciprocal", s="sV", out=stat[:, 7:8], in_=stat[:, 6:7])
                k.wait("vector", "sV")
                k.op("vector", "tensor_scalar", out=xsl, in0=nsl,
                     scalar1=stat[:, 7:8], scalar2=None, op0=ALU.mult)
                k.op("vector", "tensor_tensor", out=xsl, in0=xsl, in1=nw[:], op=ALU.mult)
                k.op("vector", "tensor_reduce", s="sV", out=stat[:, 11:12], in_=xsl,
                     axis=AX.X, op=ALU.max, apply_absolute_value=True)
                k.wait("vector", "sV")
                k.op("vector", "tensor_tensor", s="sV", out=gcol2[:, tile:tile + 1],
                     in0=stat[:, 11:12], in1=conb[:, 4:5], op=ALU.max)
                k.wait("vector", "sV")
                k.op("vector", "reciprocal", s="sV", out=stat[:, 9:10],
                     in_=gcol2[:, tile:tile + 1])
                k.wait("scalar", "sV")
                k.op("scalar", "activation", s="sS", out=stat[:, 10:11],
                     in_=stat[:, 9:10], func=AF.Identity, scale=127.0)
                k.wait("vector", "sS")
                k.op("vector", "tensor_scalar", out=xsl, in0=xsl,
                     scalar1=stat[:, 10:11], scalar2=None, op0=ALU.mult)
                stg2 = A_wq[:, 4096:6144]
                for hcol in range(4):
                    cs = slice(hcol * 512, hcol * 512 + 512)
                    k.op("vector", "tensor_copy", s="sV", out=xqi[:], in_=xsl[:, cs])
                    k.wait("vector", "sV")
                    k.op("vector", "tensor_copy", s="sV", out=stg2[:, cs], in_=xqi[:])
                k.wait("gpsimd", "sV")
                k.dma("gpsimd", xq2[r, :], stg2, "sG")
        k.barrier()

        emit_ffn(0, xq2, x2s, xn2s, gcol2, False, "y")

        k.wait("gpsimd", "sY")
        k.wait("gpsimd", "sG")
        k.wait("gpsimd", "sW")
        k.wait("gpsimd", "sM")

        block = ctx.enter_context(nc.Block())

        @block.sync
        def _(e):
            for f in k.q["sync"]:
                f(e)

        @block.tensor
        def _(e):
            for f in k.q["tensor"]:
                f(e)

        @block.vector
        def _(e):
            for f in k.q["vector"]:
                f(e)

        @block.scalar
        def _(e):
            for f in k.q["scalar"]:
                f(e)

        @block.gpsimd
        def _(e):
            for f in k.q["gpsimd"]:
                f(e)

    return nc


def _chunkT(w):
    return np.ascontiguousarray(
        w.reshape(16, 128, 16, 128).transpose(2, 1, 0, 3).reshape(16, 128, 2048))


def _masks_for(half):
    ms = np.zeros((NU, 128, 512), dtype=ml_dtypes.bfloat16)
    mi = 0
    for s2 in range(2):
        for sl in JSEQ[s2]:
            gi = half * 1024 + s2 * 512 + np.arange(512)[None, :]
            if sl < 8:
                gj = half * 1024 + sl * 128 + np.arange(128)[:, None]
            else:
                gj = (1 - half) * 1024 + (sl - 8) * 128 + np.arange(128)[:, None]
            ms[mi] = (gj < gi).astype(np.float32)
            mi += 1
    return ms


def kernel(L0_attn_hidden_states, L0_attn_Wq, L0_attn_Wk, L0_attn_Wv, L0_attn_Wo,
           L0_ffn_weight_up, L0_ffn_weight_down, L0_ffn_norm_weight, L0_ffn_gamma):
    x = np.ascontiguousarray(L0_attn_hidden_states, dtype=np.float32)
    wu = np.ascontiguousarray(
        np.asarray(L0_ffn_weight_up, dtype=np.float32)
        .reshape(64, 128, 16, 128).transpose(0, 3, 2, 1).reshape(64, 128, 2048))
    wdn = np.ascontiguousarray(
        np.asarray(L0_ffn_weight_down, dtype=np.float32).T.reshape(64, 128, 2048))
    wq_ = _chunkT(np.asarray(L0_attn_Wq, dtype=np.float32))
    wk_ = _chunkT(np.asarray(L0_attn_Wk, dtype=np.float32))
    wv_ = _chunkT(np.asarray(L0_attn_Wv, dtype=np.float32))
    wo_ = _chunkT(np.asarray(L0_attn_Wo, dtype=np.float32))
    nwr = np.ascontiguousarray(
        np.broadcast_to(np.asarray(L0_ffn_norm_weight, dtype=np.float32), (128, E)))
    gamv = np.asarray(L0_ffn_gamma, dtype=np.float32).reshape(1, E)
    con = np.zeros((128, 6), dtype=np.float32)
    con[:, 0] = 1.0
    con[:, 1] = 1e-5
    con[:, 2] = 1e-6
    con[:, 3] = 127.0
    con[:, 4] = 1e-8
    ones128 = np.ones((128, 128), dtype=np.float32)
    idn_f = np.eye(128, dtype=np.float32)
    idn_b = np.eye(128, dtype=np.float32).astype(ml_dtypes.bfloat16)
    jj = np.arange(128)
    tmat = (-(jj[:, None] >= jj[None, :]).astype(np.float32)).astype(ml_dtypes.bfloat16)

    nc = build()
    in_maps = []
    for c in range(8):
        b, half = c // 2, c % 2
        own = x[b, half * 1024:(half + 1) * 1024]
        oth2 = x[b, (1 - half) * 1024:(2 - half) * 1024]
        x_c = np.ascontiguousarray(np.concatenate([own, oth2], axis=0))
        in_maps.append({
            "x_in": x_c, "wu": wu, "wd": wdn, "wq_": wq_, "wk_": wk_,
            "wv_": wv_, "wo_": wo_, "nwr": nwr, "gamv": gamv, "con": con,
            "ones128": ones128, "idn_f": idn_f, "idn_b": idn_b, "tmat": tmat,
            "masks": _masks_for(half),
        })
    res = run_bass_kernel_spmd(nc, in_maps, core_ids=list(range(8)))
    out = np.empty((4, T, E), dtype=np.float32)
    for c in range(8):
        b, half = c // 2, c % 2
        out[b, half * 1024:(half + 1) * 1024] = res.results[c]["y"]
    return out


if __name__ == "__main__":
    import reference as _r
    ins = _r.setup_inputs()
    o = kernel(**{kk: np.asarray(v) for kk, v in ins.items()})
    e = np.asarray(_r.reference(**ins))
    err = np.linalg.norm(o - e) / (np.linalg.norm(e) + 1e-12)
    print("rel l2:", err)
